# version 01
# baseline (speedup 1.0000x reference)
import math
from contextlib import ExitStack

import numpy as np
import ml_dtypes
import concourse.bass as bass
import concourse.mybir as mybir
from concourse.bass_utils import run_bass_kernel_spmd

F32 = mybir.dt.float32
BF16 = mybir.dt.bfloat16
AF = mybir.ActivationFunctionType
ALU = mybir.AluOpType
AX = mybir.AxisListType

NCORE = 8
T = 2048
NT = 16
D = 1024
ND = 8
BF = ml_dtypes.bfloat16
RG = [[0, 1, 2, 3], [4, 5, 6, 7]]

ENGS = ("tensor", "vector", "scalar", "sync", "gpsimd")
SEMS = {"T": 1, "V": 1, "A": 1, "G": 1, "Dg": 16, "Ds": 16, "Da": 16, "C": 1}
ENG_SEM = {"tensor": "T", "vector": "V", "scalar": "A", "gpsimd": "G"}
ENG_DMA_SEM = {"gpsimd": "Dg", "sync": "Ds", "scalar": "Da"}


STRICT = True
STRICT_ENGS = ("vector", "scalar")
STRICT_SEMS = tuple(SEMS)


class Prog:
    def __init__(self):
        self.q = {e: [] for e in ENGS}
        self.cnt = {s: 0 for s in SEMS}

    def emit(self, eng, fn, inc=None):
        if STRICT and eng in STRICT_ENGS:
            for s2 in STRICT_SEMS:
                self.q[eng].append(("w", s2, self.cnt[s2]))
            if inc is None:
                inc = ENG_SEM.get(eng)
                if eng in ("sync",):
                    inc = None
        tok = None
        if inc is not None:
            self.cnt[inc] += SEMS[inc]
            tok = (inc, self.cnt[inc])
        self.q[eng].append(("i", fn, inc))
        return tok

    def step(self, eng, fn):
        return self.emit(eng, fn, inc=ENG_SEM[eng])

    def dma(self, eng, out, in_, **kw):
        return self.emit(eng, lambda e: e.dma_start(out=out, in_=in_, **kw),
                         inc=ENG_DMA_SEM[eng])

    def wait(self, eng, tok):
        if tok is None:
            return
        sem, val = tok
        assert val <= self.cnt[sem], f"wait on future {sem}={val} > {self.cnt[sem]}"
        self.q[eng].append(("w", sem, val))

    def wait_all(self, eng, sem):
        self.q[eng].append(("w", sem, self.cnt[sem]))

    def barrier(self, eng, sems=("T", "V", "A")):
        for s in sems:
            self.wait_all(eng, s)

    def tok(self, sem):
        return (sem, self.cnt[sem])

    def realize(self, nc, stk):
        semh = {s: stk.enter_context(nc.semaphore(f"sem_{s}")) for s in SEMS}
        blk = stk.enter_context(nc.Block())
        for eng in ENGS:
            items = self.q[eng]

            def make(eng_items):
                def body(e):
                    for it in eng_items:
                        if it[0] == "w":
                            e.wait_ge(semh[it[1]], it[2])
                        else:
                            inst = it[1](e)
                            if it[2] is not None:
                                inst.then_inc(semh[it[2]], SEMS[it[2]])
                return body

            getattr(blk, eng)(make(items))


class PsumPool:
    def __init__(self, p, banks):
        self.p = p
        self.banks = banks
        self.freetok = [None] * len(banks)
        self.i = 0

    def acquire(self, eng="tensor"):
        i = self.i
        self.i = (self.i + 1) % len(self.banks)
        self.p.wait(eng, self.freetok[i])
        return i, self.banks[i]

    def release(self, i, tok):
        self.freetok[i] = tok


def host_consts():
    c = {}
    c["c_ident"] = np.eye(128, dtype=BF)
    m = np.zeros((128, 2, 256), dtype=BF)
    for r in range(2):
        for pp in range(128):
            m[pp, r, :] = (np.arange(256) >= (r * 128 + pp))
    c["c_l1mask"] = np.ascontiguousarray(m.reshape(128, 512))
    s = np.zeros((128, 128), dtype=BF)
    for k in range(128):
        s[k, k + 1:] = 1
    c["c_l3strict"] = s
    tr = np.zeros((128, 128), dtype=BF)
    for k in range(128):
        tr[k, k:] = 1
    c["c_tril"] = tr
    c["c_ones_row"] = np.ones((1, 128), dtype=BF)
    c["c_ones_rowf"] = np.ones((1, 128), dtype=np.float32)
    c["c_ones_col"] = np.ones((128, 1), dtype=BF)
    c["c_iota8"] = np.tile(np.arange(8, dtype=np.float32)[None, :], (128, 1))
    c["c_identf"] = np.eye(128, dtype=np.float32)
    c["c_ones_colf"] = np.ones((128, 1), dtype=np.float32)
    c["c_trilf"] = np.tril(np.ones((128, 128), np.float32)).T.copy()
    return c


CONST_SPECS = {"c_ident": ([128, 128], BF16), "c_l1mask": ([128, 512], BF16),
               "c_l3strict": ([128, 128], BF16), "c_tril": ([128, 128], BF16),
               "c_ones_row": ([1, 128], BF16), "c_ones_rowf": ([1, 128], F32),
               "c_ones_col": ([128, 1], BF16), "c_iota8": ([128, 8], F32),
               "c_identf": ([128, 128], F32), "c_ones_colf": ([128, 1], F32),
               "c_trilf": ([128, 128], F32)}


def bcast_free(ap, n):
    return bass.AP(ap.tensor, ap.offset, list(ap.ap) + [[0, n]])


class MK:
    def __init__(self, nc, p, dbg=False):
        self.nc = nc
        self.p = p
        self.dbg = dbg
        self.dbg_outs = []
        self.input_names = []
        self._tp_i = 0
        self.fvals = []
        self._fmap = {}
        self._din_c = {}
        self._tpf_i = 0

    def fconst(self, v, dt=None):
        for d in ([dt] if dt is not None else [F32, BF16]):
            key = (d, float(v))
            if key not in self._fmap:
                if float(v) not in self.fvals:
                    self.fvals.append(float(v))
                i = self.fvals.index(float(v))
                ap = self.cvals[:, i:i + 1] if d == F32 else self.cvalsb[:, i:i + 1]
                self.nc.const_aps.register(d, float(v), ap)
                self._fmap[key] = True
        return float(v)

    def din(self, name, shape, dtype=F32):
        if name in self._din_c:
            return self._din_c[name]
        t = self.nc.dram_tensor(name, list(shape), dtype, kind="ExternalInput")
        self.input_names.append(name)
        self._din_c[name] = t.ap()
        return self._din_c[name]

    def dout(self, name, shape, dtype=F32):
        return self.nc.dram_tensor(name, list(shape), dtype, kind="ExternalOutput").ap()

    def dint(self, name, shape, dtype=F32):
        return self.nc.dram_tensor(name, list(shape), dtype, kind="Internal").ap()

    def dump(self, name, sb_ap, dtype=None):
        if not self.dbg:
            return
        p = self.p
        shape = [d[1] for d in sb_ap.ap]
        dt = dtype or sb_ap.dtype
        o = self.dout("dbg_" + name, shape, dt)
        self.dbg_outs.append("dbg_" + name)
        eng = "sync" if dt == sb_ap.dtype else "gpsimd"
        p.barrier(eng, ("T", "V", "A", "G", "Dg", "Ds", "Da", "C"))
        tok = p.dma(eng, o, sb_ap)
        for e2 in ("tensor", "vector", "scalar", "sync", "gpsimd"):
            p.wait(e2, tok)

    def alloc(self, stk):
        nc = self.nc
        S = lambda name, shape, dt: stk.enter_context(nc.sbuf_tensor(name, shape, dt))
        self.xres = S("xres", [128, NT * 1024], F32)
        self.xn_fm = S("xn_fm", [128, ND * 2048], BF16)
        self.scra = S("scra", [128, 16384], BF16)
        self.scrb = S("scrb", [128, 16384], BF16)
        self.stats = S("stats", [128, 768], F32)
        self.rowb = S("rowb", [1, 5120], BF16)
        self.rowf = S("rowf", [1, 2048], F32)
        self.wbuf = S("wbuf", [128, 4096], BF16)
        self.sm = [S("sm1", [128, 1024], BF16), S("sm2", [128, 1024], BF16)]
        self.sst_f = S("sst_f", [128, 1152], F32)
        self.sst_b = S("sst_b", [128, 1024], BF16)
        self.och = S("och", [128, 2048], BF16)
        for name, (shape, dt) in CONST_SPECS.items():
            setattr(self, name, S(name + "_sb", shape, dt))
        self.cvals = S("cvals", [128, 64], F32)
        self.cvalsb = S("cvalsb", [128, 64], BF16)
        P = lambda name, shape, dt: stk.enter_context(nc.psum_tensor(name, shape, dt))
        self.pb = [P(f"pb{i}", [128, 512], F32) for i in range(6)]
        self.pt = [P(f"pt{i}", [128, 1024], BF16) for i in range(2)]
        self.pt_tok = [None, None]
        self.pool = PsumPool(self.p, self.pb[:5])
        self.ptf = self.pb[5]
        self.ptf_tok = [None, None]

    def load_consts(self):
        for name in CONST_SPECS:
            shape, dt = CONST_SPECS[name]
            ap = self.din(name, shape, dt)
            self.p.dma("sync", getattr(self, name)[:, :], ap[:, :])
        fv = self.din("c_fvals", [128, 64], F32)
        self.p.dma("sync", self.cvals[:, :], fv[:, :])
        self.p.dma("gpsimd", self.cvalsb[:, :], fv[:, :])
        return self.p.tok("Ds")

    def transpose(self, src, dst=None, evac="scalar", evac_fn=None, out_base=0):
        p = self.p
        pp = src.ap[0][1]
        ff = src.ap[-1][1]
        b = self._tp_i
        self._tp_i = 1 - b
        p.wait("tensor", self.pt_tok[b])
        p.step("tensor", lambda e: e.transpose(self.pt[b][out_base:out_base + ff, 0:pp], src,
                                               self.c_ident[0:pp, 0:pp]))
        ttok = p.tok("T")
        pt_ap = self.pt[b][out_base:out_base + ff, 0:pp]
        p.wait(evac, ttok)
        if evac_fn is not None:
            etok = evac_fn(pt_ap)
        elif evac == "scalar":
            etok = p.step("scalar", lambda e: e.copy(dst, pt_ap))
        else:
            etok = p.step("vector", lambda e: e.tensor_copy(dst, pt_ap))
        self.pt_tok[b] = etok
        return etok

    def vflush(self):
        return self.p.step("vector", lambda e: e.tensor_copy(
            self.stats[0:1, 767:768], self.cvals[0:1, 0:1]))

    def transpose_f32(self, src, dst, evac="vector"):
        p = self.p
        pp = src.ap[0][1]
        ff = src.ap[-1][1]
        b = self._tpf_i
        self._tpf_i = 1 - b
        p.wait("tensor", self.ptf_tok[b])
        bank_ap = self.ptf[0:ff, b * 128:b * 128 + pp]
        p.step("tensor", lambda e: e.transpose(bank_ap, src, self.c_identf[0:pp, 0:pp]))
        ttok = p.tok("T")
        p.wait(evac, ttok)
        if evac == "scalar":
            etok = p.step("scalar", lambda e: e.copy(dst, bank_ap))
        else:
            etok = p.step("vector", lambda e: e.tensor_copy(dst, bank_ap))
        self.ptf_tok[b] = etok
        return etok

    def gate_stage1_fp32(self, tt, rhs_fn, nout, cs_row, svb_row):
        p = self.p
        st = self.stats
        p.step("vector", lambda e, tt=tt: e.tensor_scalar(
            st[:, 720 + 0:721 + 0], st[:, 32 + tt:33 + tt], -1.0, None, ALU.mult))
        p.wait("tensor", p.tok("V"))
        t1 = self.transpose_f32(st[:, 720:721], self.rowf[0:1, 0:128])
        t2 = self.transpose_f32(st[:, 96 + tt:97 + tt], self.rowf[0:1, 128:256])
        p.wait("tensor", t1)
        p.wait("tensor", t2)
        bi, bank = self.pool.acquire()
        for db in range(ND):
            fl = self.transpose_f32(self.xres[:, tt * 1024 + db * 128: tt * 1024 + (db + 1) * 128],
                                    self.sst_f[:, 0:128])
            p.wait("tensor", fl)
            p.emit("tensor", lambda e, db=db, bank=bank, rhs=rhs_fn(db), nout=nout: e.matmul(
                bank[:, 0:nout], self.sst_f[:, 0:128], rhs, start=(db == 0), stop=False))
        p.emit("tensor", lambda e, bank=bank, nout=nout, cs_row=cs_row: e.matmul(
            bank[:, 0:nout], self.rowf[0:1, 0:128], cs_row,
            start=False, stop=(svb_row is None)), inc="T" if svb_row is None else None)
        if svb_row is not None:
            p.emit("tensor", lambda e, bank=bank, nout=nout, svb_row=svb_row: e.matmul(
                bank[:, 0:nout], self.rowf[0:1, 128:256], svb_row,
                start=False, stop=True), inc="T")
        return bi, bank, p.tok("T")

    def allgather(self, name, rows, cols, dtype=F32):
        i = self.dint(name + "_in", [rows, cols], dtype)
        o = self.dint(name + "_out", [rows * 4, cols], dtype)
        return i, o

    def ag_go(self, in_ap, out_ap, fill_tok, kind="AllGather", groups=None):
        p = self.p
        p.wait("gpsimd", fill_tok)
        op = ALU.bypass if kind == "AllGather" else ALU.add
        rg = groups if groups is not None else RG
        return p.emit("gpsimd", lambda g: g.collective_compute(
            kind, op, replica_groups=rg,
            ins=[in_ap[:, :]], outs=[out_ap[:, :]]), inc="C")

    def emit_load_x(self, x_in):
        return self.p.dma("sync", self.xres.rearrange("p (t d) -> p t d", d=1024),
                          x_in.rearrange("(t p) d -> p t d", p=128))

    def emit_store_x(self, y_out):
        p = self.p
        p.barrier("sync", ("V", "T", "A"))
        tok = p.dma("sync", y_out.rearrange("(t p) d -> p t d", p=128),
                    self.xres.rearrange("p (t d) -> p t d", d=1024))
        p.wait("sync", tok)

    def emit_ln(self, xres_tok=None, write_dram=None, ag16=None):
        p = self.p
        st = self.stats
        if xres_tok is not None:
            p.wait("vector", xres_tok)
            p.wait("scalar", xres_tok)
        self.dump("xres_in", self.xres[:, :])
        dumps = [self.sm[0][:, :], self.sm[1][:, :],
                 self.wbuf[:, 0:1024], self.wbuf[:, 1024:2048]]
        for t in range(NT):
            xs = self.xres[:, t * 1024:(t + 1) * 1024]
            p.step("vector", lambda e, xs=xs, t=t: e.tensor_reduce(
                st[:, t:t + 1], xs, AX.X, ALU.add))
            p.step("scalar", lambda e, xs=xs, t=t: e.activation(
                dumps[t % 4], xs, AF.Square, accum_out=st[:, 16 + t:17 + t]))
        p.wait("vector", p.tok("A"))
        p.step("vector", lambda e: e.tensor_scalar(st[:, 32:48], st[:, 0:16], 1.0 / 1024, None, ALU.mult))
        p.step("vector", lambda e: e.tensor_scalar(st[:, 48:64], st[:, 16:32], 1.0 / 1024, None, ALU.mult))
        p.step("vector", lambda e: e.tensor_tensor(st[:, 64:80], st[:, 32:48], st[:, 32:48], ALU.mult))
        p.step("vector", lambda e: e.tensor_tensor(st[:, 80:96], st[:, 48:64], st[:, 64:80], ALU.subtract))
        p.wait("scalar", p.tok("V"))
        p.step("scalar", lambda e: e.activation(st[:, 96:112], st[:, 80:96], AF.Sqrt, bias=self.fconst(1e-5)))
        p.wait("vector", p.tok("A"))
        p.step("vector", lambda e: e.reciprocal(st[:, 112:128], st[:, 96:112]))
        sm_war = [p.tok("A"), p.tok("A")]
        ag_tok = None
        for t in range(NT):
            sm = self.sm[t % 2]
            p.wait("vector", sm_war[t % 2])
            p.step("vector", lambda e, t=t, sm=sm: e.tensor_scalar(
                sm[:, :], self.xres[:, t * 1024:(t + 1) * 1024],
                st[:, 32 + t:33 + t], st[:, 112 + t:113 + t], ALU.subtract, ALU.mult))
            vt = self.vflush()
            if t == 0 and ag16 is not None:
                p.wait("gpsimd", vt)
                ag_tok = p.dma("gpsimd", ag16[:, :], sm[1:17, :])
            if write_dram is not None:
                p.wait("sync", vt)
                p.dma("sync", write_dram[t * 128:(t + 1) * 128, :], sm[:, :])
            p.wait("tensor", vt)
            if t == 0 and ag16 is not None:
                p.wait("tensor", ag_tok)
            last = None
            for db in range(ND):
                last = self.transpose(
                    sm[:, db * 128:(db + 1) * 128],
                    self.xn_fm[:, db * 2048 + t * 128: db * 2048 + (t + 1) * 128])
            sm_war[t % 2] = last
            if write_dram is not None:
                sm_war[t % 2] = None
        if write_dram is not None:
            p.wait_all("vector", "Ds")
        self.dump("stats_ln", self.stats[:, :])
        self.dump("xnfm", self.xn_fm[:, :])
        return ag_tok

    def emit_l0_attn_pre(self):
        p = self.p
        wq = self.din("L0_attn_qkv_w", [1024, 3072])
        bq = self.din("L0_attn_qkv_b", [3072])
        wp = self.din("L0_attn_proj_w", [1024, 1024])
        bp = self.din("L0_attn_proj_b", [1024])
        agi = self.dint("ag_l0kv_in", [64, 1024])
        ago = self.dint("ag_l0kv_out", [64, 1024])
        p.barrier("gpsimd", ("A", "V"))
        bq_tok = p.dma("gpsimd", self.rowb[0:1, 0:3072], bq.rearrange("f -> () f"))
        xn_tok = p.tok("A")

        k_tm = self.scra
        v_tm = self.scrb
        wslab = self.wbuf.rearrange("p (db f) -> p db f", f=512)

        def project_tm(fcs, dest):
            for fc in fcs:
                p.wait_all("gpsimd", "A")
                wtok = p.dma("gpsimd", wslab,
                             wq[:, fc * 512:(fc + 1) * 512].rearrange("(db p) f -> p db f", p=128))
                p.wait("tensor", wtok)
                p.wait("tensor", bq_tok)
                for tt in range(NT):
                    bi, bank = self.pool.acquire()
                    for db in range(ND):
                        p.emit("tensor", lambda e, db=db, tt=tt, bank=bank: e.matmul(
                            bank[:, 0:512],
                            self.xn_fm[:, db * 2048 + tt * 128:db * 2048 + (tt + 1) * 128],
                            wslab[:, db, :], start=(db == 0), stop=False))
                    p.step("tensor", lambda e, fc=fc, bank=bank: e.matmul(
                        bank[:, 0:512], self.c_ones_row[:, :],
                        self.rowb[0:1, fc * 512:(fc + 1) * 512], start=False, stop=True))
                    mtok = p.tok("T")
                    p.wait("scalar", mtok)
                    etok = p.step("scalar", lambda e, d=dest(fc, tt), bank=bank: e.copy(d, bank[:, 0:512]))
                    self.pool.release(bi, etok)

        def l2norm_tiles(buf):
            st = self.stats
            p.wait_all("scalar", "A")
            for t in range(NT):
                ks = buf[:, t * 1024:(t + 1) * 1024]
                sm = self.sm[t % 2]
                c0 = 128 + 64 * (t % 2)
                p.wait_all("scalar", "V")
                for h in range(16):
                    p.step("scalar", lambda e, ks=ks, sm=sm, h=h, c0=c0: e.activation(
                        sm[:, h * 64:(h + 1) * 64],
                        bass.AP(ks.tensor, ks.offset + h * 64, [list(ks.ap[0]), [1, 64]]),
                        AF.Square, accum_out=st[:, c0 + h:c0 + h + 1]))
                p.wait("scalar", p.tok("A"))
                p.step("scalar", lambda e, c0=c0: e.activation(
                    st[:, c0 + 16:c0 + 32], st[:, c0:c0 + 16], AF.Sqrt))
                p.wait("vector", p.tok("A"))
                p.step("vector", lambda e, c0=c0: e.tensor_scalar(
                    st[:, c0 + 16:c0 + 32], st[:, c0 + 16:c0 + 32], 1e-12, None, ALU.max))
                p.step("vector", lambda e, c0=c0: e.reciprocal(
                    st[:, c0 + 32:c0 + 48], st[:, c0 + 16:c0 + 32]))
                p.step("vector", lambda e, ks=ks, sm=sm, c0=c0: e.tensor_tensor(
                    sm.rearrange("p (h d) -> p h d", d=64),
                    ks.rearrange("p (h d) -> p h d", d=64),
                    bcast_free(st[:, c0 + 32:c0 + 48], 64), ALU.mult))
                p.step("vector", lambda e, ks=ks, sm=sm: e.tensor_copy(ks, sm[:, :]))
            self.vflush()

        p.wait("tensor", xn_tok)
        project_tm([2, 3], lambda fc, tt: k_tm[:, tt * 1024 + (fc - 2) * 512: tt * 1024 + (fc - 1) * 512])
        project_tm([4, 5], lambda fc, tt: v_tm[:, tt * 1024 + (fc - 4) * 512: tt * 1024 + (fc - 3) * 512])
        l2norm_tiles(k_tm)
        self.dump("stats_l2", self.stats[:, :])
        self.dump("ktm", k_tm[:, :])
        self.dump("vtm", v_tm[:, :])

        p.wait_all("tensor", "V")
        for h in range(16):
            bi, bank = self.pool.acquire()
            for tt in range(NT):
                p.emit("tensor", lambda e, h=h, tt=tt, bank=bank: e.matmul(
                    bank[0:64, 0:64],
                    k_tm[:, tt * 1024 + h * 64: tt * 1024 + (h + 1) * 64],
                    v_tm[:, tt * 1024 + h * 64: tt * 1024 + (h + 1) * 64],
                    start=(tt == 0), stop=(tt == 15)),
                    inc="T" if tt == 15 else None)
            mtok = p.tok("T")
            p.wait("vector", mtok)
            etok = p.step("vector", lambda e, h=h, bank=bank: e.tensor_copy(
                self.sst_f[0:64, h * 64:(h + 1) * 64], bank[0:64, 0:64]))
            self.pool.release(bi, etok)
        fill = None
        p.wait("gpsimd", self.vflush())
        fill = p.dma("gpsimd", agi[:, :], self.sst_f[0:64, 0:1024])
        ag_tok = self.ag_go(agi, ago, fill, kind="AllReduce")

        project_tm([0, 1], lambda fc, tt: k_tm[:, tt * 1024 + fc * 512: tt * 1024 + (fc + 1) * 512])
        l2norm_tiles(k_tm)
        p.wait("tensor", p.tok("V"))
        for t in range(NT):
            for pbk in range(8):
                self.transpose(k_tm[:, t * 1024 + pbk * 128: t * 1024 + (pbk + 1) * 128],
                               self.scrb[:, pbk * 2048 + t * 128: pbk * 2048 + (t + 1) * 128])
        self.dump("qfm", self.scrb[:, :])
        p.barrier("gpsimd", ("T",))
        bp_tok = p.dma("gpsimd", self.rowb[0:1, 4096:5120], bp.rearrange("f -> () f"))
        return dict(ag=ag_tok, ago=ago, bp=bp_tok, wp=wp)

    def emit_l0_attn_post(self, ctx):
        p = self.p
        ago = ctx["ago"]
        p.wait("gpsimd", ctx["ag"])
        p.wait_all("gpsimd", "V")
        tok = p.dma("gpsimd", self.sst_f[0:64, 0:1024], ago[0:64, :])
        p.wait("sync", tok)
        self.dump("kvsum", self.sst_f[0:64, 0:1024])
        p.wait("vector", tok)
        p.step("vector", lambda e: e.memset(self.sst_b[:, 0:1024], 0.0))
        vw = lambda buf, p0, p1, sub: buf[p0:p1, 0:1024].rearrange(
            "p (h two c) -> p h two c", two=2, c=64)[:, :, sub, :]
        p.step("vector", lambda e: e.tensor_copy(
            vw(self.sst_b, 0, 64, 0), vw(self.sst_f, 0, 64, 0)))
        vtok = self.vflush()
        p.wait("gpsimd", vtok)
        kv2_tok = p.dma("gpsimd", vw(self.sst_b, 64, 128, 1), vw(self.sst_f, 0, 64, 1))
        p.wait("tensor", vtok)
        p.wait("tensor", kv2_tok)
        ao = self.scra
        for pr in range(8):
            for tc in range(4):
                bi, bank = self.pool.acquire()
                p.step("tensor", lambda e, pr=pr, tc=tc, bank=bank: e.matmul(
                    bank[:, 0:512], self.sst_b[:, pr * 128:(pr + 1) * 128],
                    self.scrb[:, pr * 2048 + tc * 512: pr * 2048 + (tc + 1) * 512],
                    start=True, stop=True))
                mtok = p.tok("T")
                p.wait("scalar", mtok)
                etok = p.step("scalar", lambda e, pr=pr, tc=tc, bank=bank: e.copy(
                    ao[:, pr * 2048 + tc * 512: pr * 2048 + (tc + 1) * 512], bank[:, 0:512]))
                self.pool.release(bi, etok)
        ao_tok = p.tok("A")
        wslab = self.wbuf.rearrange("p (db f) -> p db f", f=512)
        wp = ctx["wp"]
        p.wait("tensor", ao_tok)
        for oc in range(2):
            p.wait_all("gpsimd", "T")
            wtok = p.dma("gpsimd", wslab,
                         wp[:, oc * 512:(oc + 1) * 512].rearrange("(db p) f -> p db f", p=128))
            p.wait("tensor", wtok)
            p.wait("tensor", ctx["bp"])
            for tt in range(NT):
                bi, bank = self.pool.acquire()
                for cb in range(8):
                    p.emit("tensor", lambda e, cb=cb, tt=tt, bank=bank: e.matmul(
                        bank[:, 0:512], ao[:, cb * 2048 + tt * 128: cb * 2048 + (tt + 1) * 128],
                        wslab[:, cb, :], start=(cb == 0), stop=False))
                p.step("tensor", lambda e, oc=oc, bank=bank: e.matmul(
                    bank[:, 0:512], self.c_ones_row[:, :],
                    self.rowb[0:1, 4096 + oc * 512: 4096 + (oc + 1) * 512], start=False, stop=True))
                mtok = p.tok("T")
                p.wait("vector", mtok)
                dst = self.xres[:, tt * 1024 + oc * 512: tt * 1024 + (oc + 1) * 512]
                etok = p.step("vector", lambda e, dst=dst, bank=bank: e.tensor_tensor(
                    dst, dst, bank[:, 0:512], ALU.add))
                self.pool.release(bi, etok)

    def emit_l0_ffn(self, ago16, ag16_tok):
        p = self.p
        w1 = self.din("L0_ffn_W1", [4096, 1024])
        b1 = self.din("L0_ffn_b1", [4096])
        w2 = self.din("L0_ffn_W2", [1024, 4096])
        b2 = self.din("L0_ffn_b2", [1024])
        scale = self.din("L0_ffn_scale", [1])
        cdram = self.dint("l0c", [4, 4, 2048], BF16)
        hbuf = self.scra
        scrb = self.scrb
        wb = self.wbuf

        def scol(ap2d, start, step, count):
            return bass.AP(ap2d.tensor, ap2d.offset + start, [list(ap2d.ap[0]), [step, count]])

        sct = p.dma("sync", self.stats[0:1, 513:514], scale.rearrange("a -> () a"))
        p.wait("tensor", sct)
        bi, bank = self.pool.acquire()
        p.step("tensor", lambda e, bank=bank: e.matmul(
            bank[:, 0:1], self.c_ones_rowf[:, :], self.stats[0:1, 513:514], start=True, stop=True))
        p.wait("vector", p.tok("T"))
        et = p.step("vector", lambda e, bank=bank: e.tensor_copy(self.stats[:, 512:513], bank[:, 0:1]))
        self.pool.release(bi, et)

        p.barrier("gpsimd", ("T", "V", "A"))
        bt = p.dma("gpsimd", self.sm[0][0:32, 0:128], b1.rearrange("(a b) -> a b", b=128))
        p.wait("tensor", bt)
        self.transpose(self.sm[0][0:32, 0:128], evac="vector",
                       evac_fn=lambda pt: p.step("vector", lambda e, pt=pt: e.tensor_copy(
                           self.stats[:, 544:576], pt)))

        p.wait("gpsimd", ag16_tok)
        p.wait("sync", ag16_tok)
        ev1_tok = p.dma("gpsimd", self.rowb[0:1, 0:4096].rearrange("o (r f) -> o r f", r=4),
                        ago16.rearrange("(r two) f -> r two f", two=2)[0:4, 0, :])
        b2_tok = p.dma("gpsimd", self.rowb[0:1, 4096:5120], b2.rearrange("f -> () f"))
        p.barrier("sync", ("V", "T", "A"))
        x16_tok = p.dma("sync", wb[0:16, 2048:3072], ago16[0:16, :])
        p.wait("tensor", x16_tok)
        for db in range(ND):
            self.transpose(wb[0:16, 2048 + db * 128: 2048 + (db + 1) * 128],
                           wb[:, 3072 + db * 16: 3072 + (db + 1) * 16])
        xf_tok = p.tok("A")
        p.wait("sync", xf_tok)
        for i in range(4):
            st_tok = p.dma("sync", self.sm[1][0:128, 0:8].rearrange("p (h j) -> p h j", h=2),
                           ago16[8 + 2 * i, :].rearrange("(h p j) -> p h j", h=2, p=128))
            p.wait("tensor", st_tok)
            for h in range(2):
                self.transpose(self.sm[1][0:128, h * 4:(h + 1) * 4],
                               wb[0:4, 2048 + i * 256 + h * 128: 2048 + i * 256 + (h + 1) * 128])
            p.wait("sync", p.tok("T"))
        e2_tok = p.tok("A")

        p.wait("tensor", xf_tok)
        for tc in range(4):
            bi, bank = self.pool.acquire()
            for db in range(ND):
                lhs = scol(wb[:, 3072 + db * 16: 3072 + (db + 1) * 16], 1, 2, 4)
                p.emit("tensor", lambda e, lhs=lhs, db=db, tc=tc, bank=bank: e.matmul(
                    bank[0:4, 0:512], lhs,
                    self.xn_fm[:, db * 2048 + tc * 512: db * 2048 + (tc + 1) * 512],
                    start=(db == 0), stop=(db == 7)), inc="T" if db == 7 else None)
            mtok = p.tok("T")
            p.wait("vector", mtok)
            etok = p.step("vector", lambda e, tc=tc, bank=bank: e.tensor_scalar(
                wb[0:4, tc * 512:(tc + 1) * 512], bank[0:4, 0:512],
                self.stats[0:4, 512:513], None, ALU.mult))
            self.pool.release(bi, etok)
        p.wait("tensor", ev1_tok)
        p.wait("tensor", self.vflush())

        for j in range(4):
            if j >= 1:
                p.wait("gpsimd", p.tok("T"))
                at = p.dma("gpsimd", wb[0:1, 0:2048], wb[j:j + 1, 0:2048])
                p.wait("tensor", at)
            for hb in range(8):
                pbg = j * 8 + hb
                sb = hb % 2
                stage = scrb[:, 2048 + sb * 1024: 2048 + (sb + 1) * 1024]
                w1t = scrb[:, sb * 1024: (sb + 1) * 1024]
                p.barrier("gpsimd", ("T",))
                stok = p.dma("gpsimd", stage, w1[pbg * 128:(pbg + 1) * 128, :])
                p.wait("tensor", stok)
                for db in range(ND):
                    self.transpose(stage[:, db * 128:(db + 1) * 128],
                                   w1t[:, db * 128:(db + 1) * 128])
                p.wait("tensor", p.tok("A"))
                for tc in range(4):
                    bi, bank = self.pool.acquire()
                    for db in range(ND):
                        p.emit("tensor", lambda e, db=db, tc=tc, bank=bank, w1t=w1t: e.matmul(
                            bank[:, 0:512], w1t[:, db * 128:(db + 1) * 128],
                            self.xn_fm[:, db * 2048 + tc * 512: db * 2048 + (tc + 1) * 512],
                            start=(db == 0), stop=False))
                    lora_lhs = self.rowb[0:1, j * 1024 + hb * 128: j * 1024 + (hb + 1) * 128]
                    p.step("tensor", lambda e, lora_lhs=lora_lhs, tc=tc, bank=bank: e.matmul(
                        bank[:, 0:512], lora_lhs, wb[0:1, tc * 512:(tc + 1) * 512],
                        start=False, stop=True))
                    mtok = p.tok("T")
                    p.wait("scalar", mtok)
                    etok = p.step("scalar", lambda e, hb=hb, tc=tc, bank=bank, pbg=pbg: e.activation(
                        hbuf[:, hb * 2048 + tc * 512: hb * 2048 + (tc + 1) * 512],
                        bank[:, 0:512], AF.Gelu, bias=self.stats[:, 544 + pbg:545 + pbg]))
                    self.pool.release(bi, etok)
            h_tok = p.tok("A")
            if j == 0:
                self.dump("h0", hbuf[:, :])
                self.dump("aT", wb[0:4, 0:2048])
                self.dump("x16fm", wb[:, 3072:3200])
                self.dump("evens", self.rowb[0:1, 0:5120])

            p.wait("tensor", h_tok)
            p.wait("tensor", e2_tok)
            for tc in range(4):
                bi, bank = self.pool.acquire()
                for hb in range(ND):
                    lhs = scol(wb[:, 3072 + hb * 16: 3072 + (hb + 1) * 16], 9, 2, 4)
                    p.emit("tensor", lambda e, lhs=lhs, hb=hb, tc=tc, bank=bank: e.matmul(
                        bank[0:4, 0:512], lhs,
                        hbuf[:, hb * 2048 + tc * 512: hb * 2048 + (tc + 1) * 512],
                        start=(hb == 0), stop=(hb == 7)), inc="T" if hb == 7 else None)
                mtok = p.tok("T")
                p.wait("vector", mtok)
                etok = p.step("vector", lambda e, tc=tc, bank=bank: e.tensor_scalar(
                    scrb[0:4, 6144 + tc * 512: 6144 + (tc + 1) * 512], bank[0:4, 0:512],
                    self.stats[0:4, 512:513], None, ALU.mult))
                self.pool.release(bi, etok)
            p.wait("sync", self.vflush())
            c_tok = p.dma("sync", cdram[j, :, :], scrb[0:4, 6144:8192])

            for ob in range(8):
                sb = ob % 2
                stage = scrb[:, 2048 + sb * 1024: 2048 + (sb + 1) * 1024]
                p.barrier("gpsimd", ("T",))
                stok = p.dma("gpsimd", stage, w2[ob * 128:(ob + 1) * 128, j * 1024:(j + 1) * 1024])
                p.wait("tensor", stok)
                for hb in range(8):
                    self.transpose(stage[:, hb * 128:(hb + 1) * 128],
                                   scrb[:, 8192 + hb * 1024 + ob * 128: 8192 + hb * 1024 + (ob + 1) * 128])
            w2_tok = p.tok("A")

            cslot = [scrb[0:4, 4096 + 0:4096 + 2048], scrb[0:4, 0 + 0:0 + 2048]]
            if j == 3:
                p.wait("sync", c_tok)
                p.barrier("sync", ("T",))
                cs_tok = {}
                for i in range(2):
                    cs_tok[i] = p.dma("sync", cslot[i][:, :], cdram[:, i, :])

            p.wait("tensor", w2_tok)
            p.wait("tensor", b2_tok)
            for oc in range(2):
                if j == 3 and oc == 1:
                    for i in range(2):
                        p.wait("sync", p.tok("T"))
                        cs_tok[i] = p.dma("sync", cslot[i][:, :], cdram[:, 2 + i, :])
                for tt in range(NT):
                    bi, bank = self.pool.acquire()
                    mms = []
                    for hb in range(8):
                        mms.append((bank[:, 0:512],
                                    hbuf[:, hb * 2048 + tt * 128: hb * 2048 + (tt + 1) * 128],
                                    scrb[:, 8192 + hb * 1024 + oc * 512: 8192 + hb * 1024 + (oc + 1) * 512]))
                    if j == 0:
                        mms.append((bank[:, 0:512], self.c_ones_row[:, :],
                                    self.rowb[0:1, 4096 + oc * 512: 4096 + (oc + 1) * 512]))
                    if j == 3:
                        for isub in range(2):
                            i = 2 * oc + isub
                            p.wait("tensor", cs_tok[isub])
                            mms.append((bank[:, isub * 256:(isub + 1) * 256],
                                        cslot[isub][:, tt * 128:(tt + 1) * 128],
                                        wb[0:4, 2048 + i * 256: 2048 + (i + 1) * 256]))
                    for mi, (o_, l_, r_) in enumerate(mms):
                        p.emit("tensor", lambda e, o_=o_, l_=l_, r_=r_, mi=mi, n=len(mms): e.matmul(
                            o_, l_, r_, start=(mi == 0), stop=(mi == n - 1)),
                            inc="T" if mi == len(mms) - 1 else None)
                    mtok = p.tok("T")
                    p.wait("vector", mtok)
                    dst = self.xres[:, tt * 1024 + oc * 512: tt * 1024 + (oc + 1) * 512]
                    etok = p.step("vector", lambda e, dst=dst, bank=bank: e.tensor_tensor(
                        dst, dst, bank[:, 0:512], ALU.add))
                    self.pool.release(bi, etok)

    def emit_l1(self):
        p = self.p
        wqkv = self.din("L1_attn_W_qkv", [3072, 1024])
        wout = self.din("L1_attn_W_out", [1024, 1024])
        q_dram = self.dint("l1q", [1024, 2048], BF16)
        agi = self.dint("ag_l1s_in", [64, 1040])
        ago = self.dint("ag_l1s_out", [256, 1040])
        rmask = self.din("c_rankmask", [128, 4])
        rm_tok = p.dma("sync", self.stats[:, 700:704], rmask[:, :])

        k_tm = self.scra
        v_tm = self.scrb
        wb = self.wbuf
        st = self.stats
        LN_S = self.fconst(math.log(0.125))

        def build_wT(wsrc, f0, nfb, dst, stage):
            for fb in range(nfb):
                p.barrier("gpsimd", ("T", "A", "V"))
                stok = p.dma("gpsimd", stage, wsrc[f0 + fb * 128:f0 + (fb + 1) * 128, :])
                p.wait("tensor", stok)
                for db in range(ND):
                    self.transpose(stage[:, db * 128:(db + 1) * 128],
                                   dst[:, db * (nfb * 128) + fb * 128: db * (nfb * 128) + (fb + 1) * 128])

        def elu_epi(bank, dest, scale_ln, do_elu=True):
            mtok = p.tok("T")
            p.wait("vector", mtok)
            if not do_elu:
                return p.step("vector", lambda e: e.tensor_copy(dest, bank[:, 0:512]))
            p.step("vector", lambda e: e.tensor_scalar(self.sm[1][:, 0:512], bank[:, 0:512], 0.0, None, ALU.min))
            p.step("scalar", lambda e: e.activation(self.och[:, 0:512], self.sm[1][:, 0:512], AF.Exp,
                                                    bias=scale_ln))
            p.step("vector", lambda e: e.tensor_scalar(self.sm[1][:, 512:1024], bank[:, 0:512],
                                                       math.exp(scale_ln), 0.0, ALU.mult, ALU.max))
            return p.step("vector", lambda e: e.tensor_tensor(dest, self.och[:, 0:512],
                                                              self.sm[1][:, 512:1024], ALU.add))

        xn_tok = p.tok("A")
        p.wait("tensor", xn_tok)
        for fb in range(8):
            build_wT(wqkv, fb * 128, 1, wb[:, 0:1024], wb[:, 1024:2048])
            p.wait("tensor", p.tok("A"))
            for tc in range(4):
                bi, bank = self.pool.acquire()
                for db in range(ND):
                    p.emit("tensor", lambda e, db=db, tc=tc, bank=bank: e.matmul(
                        bank[:, 0:512], wb[:, db * 128:(db + 1) * 128],
                        self.xn_fm[:, db * 2048 + tc * 512: db * 2048 + (tc + 1) * 512],
                        start=(db == 0), stop=(db == 7)), inc="T" if db == 7 else None)
                etok = elu_epi(bank, v_tm[:, fb * 2048 + tc * 512: fb * 2048 + (tc + 1) * 512], LN_S)
                self.pool.release(bi, etok)
        p.wait("sync", p.tok("V"))
        qd_tok = p.dma("sync", q_dram.rearrange("(pb p) t -> p pb t", p=128),
                       v_tm.rearrange("p (pb t) -> p pb t", t=2048))

        for part, dst, do_elu in ((1, k_tm, True), (2, v_tm, True)):
            for fc in range(2):
                f0 = part * 1024 + fc * 512
                if part == 2 and fc == 0:
                    p.wait("gpsimd", qd_tok)
                build_wT(wqkv, f0, 4, wb[:, 0:4096], self.och[:, 0:1024])
                p.wait("tensor", p.tok("A"))
                for tt in range(NT):
                    bi, bank = self.pool.acquire()
                    for db in range(ND):
                        p.emit("tensor", lambda e, db=db, tt=tt, bank=bank: e.matmul(
                            bank[:, 0:512],
                            self.xn_fm[:, db * 2048 + tt * 128:db * 2048 + (tt + 1) * 128],
                            wb[:, db * 512:(db + 1) * 512], start=(db == 0), stop=(db == 7)),
                            inc="T" if db == 7 else None)
                    etok = elu_epi(bank, dst[:, tt * 1024 + fc * 512: tt * 1024 + (fc + 1) * 512],
                                   0.0, do_elu=(part == 1))
                    self.pool.release(bi, etok)

        self.vflush()
        p.wait_all("tensor", "V")
        for h in range(16):
            bi, bank = self.pool.acquire()
            for tt in range(NT):
                p.emit("tensor", lambda e, h=h, tt=tt, bank=bank: e.matmul(
                    bank[0:64, 0:64],
                    k_tm[:, tt * 1024 + h * 64: tt * 1024 + (h + 1) * 64],
                    v_tm[:, tt * 1024 + h * 64: tt * 1024 + (h + 1) * 64],
                    start=(tt == 0), stop=False))
                p.emit("tensor", lambda e, h=h, tt=tt, bank=bank: e.matmul(
                    bank[0:64, 64:65],
                    k_tm[:, tt * 1024 + h * 64: tt * 1024 + (h + 1) * 64],
                    self.c_ones_col[:, :], start=False, stop=(tt == 15)),
                    inc="T" if tt == 15 else None)
            mtok = p.tok("T")
            p.wait("vector", mtok)
            etok = p.step("vector", lambda e, h=h, bank=bank: e.tensor_copy(
                self.sst_f[(h % 2) * 64:(h % 2) * 64 + 64, 520 + (h // 2) * 65: 520 + (h // 2) * 65 + 65],
                bank[0:64, 0:65]))
            self.pool.release(bi, etok)
        fl = self.vflush()
        p.wait("gpsimd", fl)
        agi3 = agi.rearrange("d (hb two c) -> d hb two c", two=2, c=65)
        f2a = p.dma("gpsimd", agi3[:, :, 0, :], self.sst_f[0:64, 520:1040].rearrange("d (hb c) -> d hb c", c=65))
        p.wait("gpsimd", f2a)
        f2 = p.dma("gpsimd", agi3[:, :, 1, :], self.sst_f[64:128, 520:1040].rearrange("d (hb c) -> d hb c", c=65))
        ag_tok = self.ag_go(agi, ago, f2)

        p.wait("vector", rm_tok)
        p.step("vector", lambda e: e.memset(self.sst_f[:, 0:520], 0.0))
        for r in range(4):
            p.wait("gpsimd", ag_tok)
            p.wait_all("gpsimd", "V")
            g1 = p.dma("gpsimd", self.sst_f[0:64, 520:1040].rearrange("d (hb c) -> d hb c", c=65),
                       ago.rearrange("(r d) (hb two c) -> r d hb two c", r=4, two=2, c=65)[r, :, :, 0, :])
            g2 = p.dma("gpsimd", self.sst_f[64:128, 520:1040].rearrange("d (hb c) -> d hb c", c=65),
                       ago.rearrange("(r d) (hb two c) -> r d hb two c", r=4, two=2, c=65)[r, :, :, 1, :])
            p.wait("vector", g2)
            p.step("vector", lambda e, r=r: e.scalar_tensor_tensor(
                self.sst_f[:, 0:520], self.sst_f[:, 520:1040], st[:, 700 + r:701 + r],
                self.sst_f[:, 0:520], ALU.mult, ALU.add))
        p.step("vector", lambda e: e.tensor_copy(self.sst_b[:, 0:520], self.sst_f[:, 0:520]))
        self.vflush()

        och_dram = self.dint("l1o", [2048, 1024], BF16)
        for c in range(8):
            p.barrier("sync", ("T",))
            qtok = p.dma("sync", wb[:, 0:2048].rearrange("p (pb t) -> p pb t", t=256),
                         q_dram.rearrange("(pb p) t -> p pb t", p=128)[:, :, c * 256:(c + 1) * 256])
            p.wait("tensor", qtok)
            for h in range(16):
                b0 = (h % 2) * 64
                hb = h // 2
                for mt in range(2):
                    tt = 2 * c + mt
                    self.transpose(k_tm[:, tt * 1024 + h * 64: tt * 1024 + (h + 1) * 64],
                                   self.sm[0][b0:b0 + 64, mt * 128:(mt + 1) * 128], out_base=b0)
                p.wait("tensor", p.tok("A"))
                bi_s, bank_s = self.pool.acquire()
                for mt in range(2):
                    p.emit("tensor", lambda e, mt=mt, b0=b0, hb=hb, bank_s=bank_s: e.matmul(
                        bank_s[0:128, mt * 256:(mt + 1) * 256],
                        self.sm[0][b0:b0 + 64, mt * 128:(mt + 1) * 128],
                        wb[b0:b0 + 64, hb * 256:(hb + 1) * 256],
                        start=(mt == 0), stop=True), inc="T" if mt == 1 else None)
                mtok = p.tok("T")
                p.wait("vector", mtok)
                p.step("vector", lambda e, bank_s=bank_s: e.tensor_tensor(
                    self.sm[1][:, 0:512], bank_s[:, 0:512], self.c_l1mask[:, :], ALU.mult))
                sv = self.vflush()
                self.pool.release(bi_s, sv)
                p.wait("tensor", sv)
                bi_n, bank_n = self.pool.acquire()
                nm = []
                for lt2 in range(2):
                    for mt in range(2):
                        tt = 2 * c + mt
                        nm.append((bank_n[:, lt2 * 65:lt2 * 65 + 64],
                                   self.sm[1][0:128, mt * 256 + lt2 * 128: mt * 256 + (lt2 + 1) * 128],
                                   v_tm[:, tt * 1024 + h * 64: tt * 1024 + (h + 1) * 64]))
                        nm.append((bank_n[:, lt2 * 65 + 64:lt2 * 65 + 65],
                                   self.sm[1][0:128, mt * 256 + lt2 * 128: mt * 256 + (lt2 + 1) * 128],
                                   self.c_ones_col[:, :]))
                for mi, (o_, l_, r_) in enumerate(nm):
                    p.emit("tensor", lambda e, o_=o_, l_=l_, r_=r_, mi=mi: e.matmul(
                        o_, l_, r_, start=(mi == 0), stop=False))
                for lt2 in range(2):
                    p.emit("tensor", lambda e, lt2=lt2, b0=b0, hb=hb, bank_n=bank_n: e.matmul(
                        bank_n[:, lt2 * 65:(lt2 + 1) * 65],
                        wb[b0:b0 + 64, hb * 256 + lt2 * 128: hb * 256 + (lt2 + 1) * 128],
                        self.sst_b[b0:b0 + 64, hb * 65:(hb + 1) * 65],
                        start=False, stop=(lt2 == 1)), inc="T" if lt2 == 1 else None)
                ntok = p.tok("T")
                p.wait("vector", ntok)
                for lt2 in range(2):
                    p.step("vector", lambda e, lt2=lt2, bank_n=bank_n: e.tensor_scalar(
                        st[:, 704 + lt2:705 + lt2], bank_n[:, lt2 * 65 + 64:lt2 * 65 + 65],
                        1e-6, None, ALU.max))
                    p.step("vector", lambda e, lt2=lt2: e.reciprocal(
                        st[:, 706 + lt2:707 + lt2], st[:, 704 + lt2:705 + lt2]))
                    dtok = p.step("vector", lambda e, lt2=lt2, h=h, bank_n=bank_n: e.tensor_scalar(
                        self.och[:, lt2 * 1024 + h * 64: lt2 * 1024 + (h + 1) * 64],
                        bank_n[:, lt2 * 65:lt2 * 65 + 64],
                        st[:, 706 + lt2:707 + lt2], None, ALU.mult))
                self.pool.release(bi_n, p.tok("V"))
                bi_u, bank_u = self.pool.acquire()
                for mt in range(2):
                    tt = 2 * c + mt
                    p.emit("tensor", lambda e, mt=mt, tt=tt, h=h, bank_u=bank_u: e.matmul(
                        bank_u[0:64, 0:64],
                        k_tm[:, tt * 1024 + h * 64: tt * 1024 + (h + 1) * 64],
                        v_tm[:, tt * 1024 + h * 64: tt * 1024 + (h + 1) * 64],
                        start=(mt == 0), stop=False))
                    p.emit("tensor", lambda e, mt=mt, tt=tt, h=h, bank_u=bank_u: e.matmul(
                        bank_u[0:64, 64:65],
                        k_tm[:, tt * 1024 + h * 64: tt * 1024 + (h + 1) * 64],
                        self.c_ones_col[:, :], start=False, stop=(mt == 1)),
                        inc="T" if mt == 1 else None)
                utok = p.tok("T")
                p.wait("vector", utok)
                p.step("vector", lambda e, b0=b0, hb=hb, bank_u=bank_u: e.tensor_tensor(
                    self.sst_f[b0:b0 + 64, hb * 65:(hb + 1) * 65],
                    self.sst_f[b0:b0 + 64, hb * 65:(hb + 1) * 65],
                    bank_u[0:64, 0:65], ALU.add))
                ctok = p.step("vector", lambda e, b0=b0, hb=hb: e.tensor_copy(
                    self.sst_b[b0:b0 + 64, hb * 65:(hb + 1) * 65],
                    self.sst_f[b0:b0 + 64, hb * 65:(hb + 1) * 65]))
                self.pool.release(bi_u, ctok)
            fl2 = self.vflush()
            p.wait("sync", fl2)
            p.dma("sync", och_dram[c * 256:(c + 1) * 256, :].rearrange("(lt p) o -> p lt o", p=128),
                  self.och[:, 0:2048].rearrange("p (lt o) -> p lt o", o=1024))
        for ob in range(8):
            build_wT(wout, ob * 128, 1, k_tm[:, ob * 1024: ob * 1024 + 1024], self.och[:, 0:1024])
        woutT = k_tm.rearrange("p (ob eb j) -> p ob eb j", eb=8, j=128)
        for tt in range(NT):
            p.barrier("sync", ("T",))
            ltok = p.dma("sync", self.och[:, 0:1024],
                         och_dram[tt * 128:(tt + 1) * 128, :])
            p.wait("tensor", ltok)
            for eb in range(8):
                self.transpose(self.och[:, eb * 128:(eb + 1) * 128],
                               self.och[:, 1024 + eb * 128: 1024 + (eb + 1) * 128])
            p.wait("tensor", p.tok("A"))
            for oc in range(2):
                bi, bank = self.pool.acquire()
                for eb in range(8):
                    p.emit("tensor", lambda e, eb=eb, oc=oc, tt=tt, bank=bank: e.matmul(
                        bank[:, 0:512], self.och[:, 1024 + eb * 128: 1024 + (eb + 1) * 128],
                        woutT[:, oc * 4:(oc + 1) * 4, eb, :],
                        start=(eb == 0), stop=(eb == 7)), inc="T" if eb == 7 else None)
                mtok = p.tok("T")
                p.wait("vector", mtok)
                dst = self.xres[:, tt * 1024 + oc * 512: tt * 1024 + (oc + 1) * 512]
                etok = p.step("vector", lambda e, dst=dst, bank=bank: e.tensor_tensor(
                    dst, dst, bank[:, 0:512], ALU.add))
                self.pool.release(bi, etok)


    def emit_l1_ffn(self):
        p = self.p
        rtr_w = self.din("L1_ffn_router_w", [8, 1024])
        w1 = self.din("L1_ffn_w1", [8, 1024, 4096])
        w2 = self.din("L1_ffn_w2", [8, 4096, 1024])
        agci = self.dint("ag_l1c_in", [1, 16])
        agco = self.dint("ag_l1c_out", [8, 16])
        rmask8 = self.din("c_rankmask8", [128, 8])
        rm8_tok = p.dma("sync", self.stats[:, 708:716], rmask8[:, :])
        st = self.stats
        wb = self.wbuf
        C_CAP = 5120.0

        p.barrier("gpsimd", ("T", "A", "V"))
        stok = p.dma("gpsimd", self.sm[0][0:8, 0:1024], rtr_w[:, :])
        p.wait("tensor", stok)
        for db in range(ND):
            self.transpose(self.sm[0][0:8, db * 128:(db + 1) * 128], wb[:, db * 8:(db + 1) * 8])
        p.step("vector", lambda e: e.tensor_scalar(st[:, 640:648], self.c_iota8[:, :], 1e-6, None, ALU.mult))
        p.step("vector", lambda e: e.memset(st[0:1, 648:664], 0.0))
        p.wait("tensor", p.tok("A"))
        p.wait("tensor", p.tok("V"))

        for tt in range(NT):
            bi, bank = self.pool.acquire()
            for db in range(ND):
                p.emit("tensor", lambda e, db=db, tt=tt, bank=bank: e.matmul(
                    bank[:, 0:8], self.xn_fm[:, db * 2048 + tt * 128:db * 2048 + (tt + 1) * 128],
                    wb[:, db * 8:(db + 1) * 8], start=(db == 0), stop=(db == 7)),
                    inc="T" if db == 7 else None)
            mtok = p.tok("T")
            p.wait("vector", mtok)
            p.step("vector", lambda e, bank=bank: e.tensor_tensor(st[:, 664:672], bank[:, 0:8], st[:, 640:648], ALU.subtract))
            p.step("vector", lambda e: e.reduce_max(st[:, 672:673], st[:, 664:672], AX.X))
            p.step("vector", lambda e, tt=tt: e.tensor_scalar(
                self.och[:, tt * 8:(tt + 1) * 8], st[:, 664:672], st[:, 672:673], None, ALU.is_equal))
            p.step("vector", lambda e, tt=tt: e.scalar_tensor_tensor(
                st[:, 680:688], self.och[:, tt * 8:(tt + 1) * 8], -1e30, st[:, 664:672], ALU.mult, ALU.add))
            p.step("vector", lambda e: e.reduce_max(st[:, 673:674], st[:, 680:688], AX.X))
            p.step("vector", lambda e, tt=tt: e.tensor_scalar(
                self.och[:, 128 + tt * 8:128 + (tt + 1) * 8], st[:, 680:688], st[:, 673:674], None, ALU.is_equal))
            p.step("vector", lambda e: e.tensor_tensor(st[:, 674:675], st[:, 672:673], st[:, 673:674], ALU.subtract))
            p.wait("scalar", p.tok("V"))
            p.step("scalar", lambda e, tt=tt: e.activation(st[:, 448 + tt:449 + tt], st[:, 674:675], AF.Sigmoid))
            p.wait("vector", p.tok("A"))
            p.step("vector", lambda e, tt=tt: e.tensor_scalar(
                st[:, 464 + tt:465 + tt], st[:, 448 + tt:449 + tt], -1.0, 1.0, ALU.mult, ALU.add))
            for k in range(2):
                ohap = self.och[:, k * 128 + tt * 8: k * 128 + (tt + 1) * 8]
                p.wait("vector", mtok)
                bi2, bank2 = self.pool.acquire()
                p.wait("tensor", p.tok("V"))
                p.emit("tensor", lambda e, ohap=ohap, bank2=bank2: e.matmul(
                    bank2[:, 0:8], self.c_tril[:, :], ohap, start=True, stop=False))
                p.emit("tensor", lambda e, k=k, bank2=bank2: e.matmul(
                    bank2[:, 0:8], self.c_ones_rowf[:, :], st[0:1, 648 + k * 8:656 + k * 8],
                    start=False, stop=False))
                p.emit("tensor", lambda e, ohap=ohap, bank2=bank2: e.matmul(
                    bank2[0:1, 8:16], self.c_ones_col[:, :], ohap, start=False, stop=True),
                    inc="T")
                m2tok = p.tok("T")
                p.wait("vector", m2tok)
                p.step("vector", lambda e, ohap=ohap, bank2=bank2: e.tensor_tensor(
                    st[:, 680:688], bank2[:, 0:8], ohap, ALU.mult))
                p.step("vector", lambda e: e.reduce_sum(st[:, 688:689], st[:, 680:688], AX.X))
                p.step("vector", lambda e, k=k, tt=tt: e.tensor_scalar(
                    st[:, 480 + k * 16 + tt:481 + k * 16 + tt], st[:, 688:689], -1.0, None, ALU.add))
                etok = p.step("vector", lambda e, k=k, bank2=bank2: e.tensor_tensor(
                    st[0:1, 648 + k * 8:656 + k * 8], st[0:1, 648 + k * 8:656 + k * 8],
                    bank2[0:1, 8:16], ALU.add))
                self.pool.release(bi2, etok)
            self.pool.release(bi, p.tok("V"))
        fl = self.vflush()
        p.wait("gpsimd", fl)
        cf = p.dma("gpsimd", agci[:, :], st[0:1, 648:664])
        agc_tok = self.ag_go(agci, agco, cf, groups=[[0, 1, 2, 3, 4, 5, 6, 7]])
        p.wait("gpsimd", agc_tok)
        p.wait("vector", rm8_tok)
        p.step("vector", lambda e: e.memset(st[0:1, 624:640], 0.0))
        for r in range(8):
            p.wait_all("gpsimd", "V")
            g = p.dma("gpsimd", st[0:1, 608:624], agco[r:r + 1, :])
            p.wait("vector", g)
            p.step("vector", lambda e, r=r: e.scalar_tensor_tensor(
                st[0:1, 624:640], st[0:1, 608:624], st[0:1, 708 + r:709 + r],
                st[0:1, 624:640], ALU.mult, ALU.add))
        p.step("vector", lambda e: e.tensor_scalar(st[0:1, 624:640], st[0:1, 624:640], -1.0, C_CAP, ALU.mult, ALU.add))
        fl2 = self.vflush()
        p.wait("tensor", fl2)
        bi, bank = self.pool.acquire()
        for k in range(2):
            p.emit("tensor", lambda e, k=k, bank=bank: e.matmul(
                bank[:, k * 8:(k + 1) * 8], self.c_ones_rowf[:, :], st[0:1, 624 + k * 8:632 + k * 8],
                start=(k == 0), stop=(k == 1)), inc="T" if k == 1 else None)
        thr_tok = p.tok("T")
        p.wait("vector", thr_tok)
        p.step("vector", lambda e, bank=bank: e.tensor_copy(st[:, 576:592], bank[:, 0:16]))
        self.pool.release(bi, p.tok("V"))
        for tt in range(NT):
            for k in range(2):
                ohap = self.och[:, k * 128 + tt * 8: k * 128 + (tt + 1) * 8]
                p.step("vector", lambda e, ohap=ohap, k=k: e.tensor_tensor(
                    st[:, 680:688], ohap, st[:, 576 + k * 8:584 + k * 8], ALU.mult))
                p.step("vector", lambda e: e.reduce_sum(st[:, 688:689], st[:, 680:688], AX.X))
                p.step("vector", lambda e, k=k, tt=tt: e.tensor_tensor(
                    st[:, 689:690], st[:, 480 + k * 16 + tt:481 + k * 16 + tt], st[:, 688:689], ALU.is_lt))
                p.step("vector", lambda e, k=k, tt=tt: e.tensor_tensor(
                    st[:, 690:691], st[:, (448 if k == 0 else 464) + tt:(449 if k == 0 else 465) + tt],
                    st[:, 689:690], ALU.mult))
                if k == 0:
                    p.step("vector", lambda e, ohap=ohap, tt=tt: e.tensor_scalar(
                        st[:, 256 + tt * 8:264 + tt * 8], ohap, st[:, 690:691], None, ALU.mult))
                else:
                    p.step("vector", lambda e, ohap=ohap, tt=tt: e.scalar_tensor_tensor(
                        st[:, 256 + tt * 8:264 + tt * 8], ohap, st[:, 690:691],
                        st[:, 256 + tt * 8:264 + tt * 8], ALU.mult, ALU.add))
        self.vflush()

        hq = self.scrb
        for e_ in range(8):
            for q in range(4):
                p.barrier("gpsimd", ("T",))
                w1tok = p.dma("gpsimd", self.scra[:, 0:8192].rearrange("p (db f) -> p db f", f=1024),
                              w1[e_, :, q * 1024:(q + 1) * 1024].rearrange("(db p) f -> p db f", p=128))
                p.wait("tensor", w1tok)
                for hb in range(8):
                    for tc in range(4):
                        bi, bank = self.pool.acquire()
                        for db in range(ND):
                            p.emit("tensor", lambda e, db=db, hb=hb, tc=tc, bank=bank: e.matmul(
                                bank[:, 0:512],
                                self.scra[:, db * 1024 + hb * 128: db * 1024 + (hb + 1) * 128],
                                self.xn_fm[:, db * 2048 + tc * 512: db * 2048 + (tc + 1) * 512],
                                start=(db == 0), stop=(db == 7)), inc="T" if db == 7 else None)
                        mtok = p.tok("T")
                        p.wait("scalar", mtok)
                        etok = p.step("scalar", lambda e, hb=hb, tc=tc, bank=bank: e.activation(
                            hq[:, hb * 2048 + tc * 512: hb * 2048 + (tc + 1) * 512],
                            bank[:, 0:512], AF.Gelu))
                        self.pool.release(bi, etok)
                p.barrier("gpsimd", ("T", "A"))
                w2tok = p.dma("gpsimd", self.scra[:, 8192:16384].rearrange("p (hfb o) -> p hfb o", o=1024),
                              w2[e_, q * 1024:(q + 1) * 1024, :].rearrange("(hfb p) o -> p hfb o", p=128))
                p.wait("tensor", w2tok)
                p.wait("tensor", p.tok("A"))
                w2v = self.scra.rearrange("p (half hfb o) -> p half hfb o", half=2, o=1024)
                for tt in range(NT):
                    for oc in range(2):
                        bi, bank = self.pool.acquire()
                        for hfb in range(8):
                            p.emit("tensor", lambda e, hfb=hfb, tt=tt, oc=oc, bank=bank: e.matmul(
                                bank[:, 0:512],
                                hq[:, hfb * 2048 + tt * 128: hfb * 2048 + (tt + 1) * 128],
                                w2v[:, 1, hfb, oc * 512:(oc + 1) * 512],
                                start=(hfb == 0), stop=(hfb == 7)), inc="T" if hfb == 7 else None)
                        mtok = p.tok("T")
                        p.wait("vector", mtok)
                        dst = self.xres[:, tt * 1024 + oc * 512: tt * 1024 + (oc + 1) * 512]
                        etok = p.step("vector", lambda e, dst=dst, bank=bank, e_=e_, tt=tt: e.scalar_tensor_tensor(
                            dst, bank[:, 0:512], st[:, 256 + tt * 8 + e_:257 + tt * 8 + e_],
                            dst, ALU.mult, ALU.add))
                        self.pool.release(bi, etok)


    def emit_l2(self):
        p = self.p
        st = self.stats
        wb = self.wbuf
        phase = self.din("L2_attn_phase", [4, 1024])
        wq = self.din("L2_attn_Wq", [1024, 4096])
        bq = self.din("L2_attn_bq", [4096])
        wk = self.din("L2_attn_Wk", [1024, 4096])
        bk = self.din("L2_attn_bk", [4096])
        wv = self.din("L2_attn_Wv", [1024, 4096])
        bv = self.din("L2_attn_bv", [4096])
        wo = self.din("L2_attn_Wo", [4096, 1024])
        bo = self.din("L2_attn_bo", [1024])
        q_dram = self.dint("l2q", [2048, 4096], BF16)
        k_dram = self.dint("l2k", [2048, 4096], BF16)

        xn_tok = p.tok("A")
        p.wait("tensor", xn_tok)
        for which, wmat, bvec, dstd in (("q", wq, bq, q_dram), ("k", wk, bk, k_dram)):
            p.barrier("gpsimd", ("T", "A", "V"))
            btok = p.dma("gpsimd", self.rowb[0:1, 0:4096], bvec.rearrange("f -> () f"))
            for fc in range(8):
                p.wait_all("gpsimd", "T")
                wtok = p.dma("gpsimd", wb.rearrange("p (db f) -> p db f", f=512),
                             wmat[:, fc * 512:(fc + 1) * 512].rearrange("(db p) f -> p db f", p=128))
                p.wait("tensor", wtok)
                p.wait("tensor", btok)
                for tt in range(NT):
                    bi, bank = self.pool.acquire()
                    for db in range(ND):
                        p.emit("tensor", lambda e, db=db, tt=tt, bank=bank: e.matmul(
                            bank[:, 0:512],
                            self.xn_fm[:, db * 2048 + tt * 128:db * 2048 + (tt + 1) * 128],
                            wb[:, db * 512:(db + 1) * 512], start=(db == 0), stop=False))
                    p.emit("tensor", lambda e, fc=fc, bank=bank: e.matmul(
                        bank[:, 0:512], self.c_ones_row[:, :],
                        self.rowb[0:1, fc * 512:(fc + 1) * 512], start=False, stop=True), inc="T")
                    mtok = p.tok("T")
                    p.wait("scalar", mtok)
                    etok = p.step("scalar", lambda e, tt=tt, bank=bank: e.copy(
                        self.scra[:, tt * 512:(tt + 1) * 512], bank[:, 0:512]))
                    self.pool.release(bi, etok)
                fl = p.tok("A")
                p.wait("sync", fl)
                p.dma("sync", dstd.rearrange("(tt p) (fc f) -> p tt fc f", p=128, f=512)[:, :, fc, :],
                      self.scra[:, 0:8192].rearrange("p (tt f) -> p tt f", f=512))

        p.barrier("gpsimd", ("T", "A", "V"))
        for pr in range(16):
            p.dma("gpsimd", self.sst_f[pr:pr + 1, 0:1024], phase[pr // 4:pr // 4 + 1, :])
        ph_tok = p.tok("Dg")
        p.wait("vector", ph_tok)
        p.step("vector", lambda e: e.tensor_scalar(self.sst_f[0:16, 0:1024],
                                                   self.sst_f[0:16, 0:1024], -1.0, None, ALU.mult))
        neg = self.vflush()
        p.wait("gpsimd", neg)
        for pr in range(16):
            p.dma("gpsimd", self.sst_f[pr:pr + 1, 0:1024], phase[pr % 4:pr % 4 + 1, :],
                  accum_op=ALU.add)
        ph2 = p.tok("Dg")
        p.wait("scalar", ph2)
        p.step("scalar", lambda e: e.activation(self.sst_f[0:16, 0:1024], self.sst_f[0:16, 0:1024],
                                                AF.Sin, bias=self.fconst(math.pi / 2)))
        fl = p.tok("A")
        cosrow_d = self.dint("l2cd", [16, 1024], F32)
        p.wait("sync", fl)
        cd1 = p.dma("sync", cosrow_d[:, :], self.sst_f[0:16, 0:1024])
        p.wait("sync", cd1)
        for pr in range(16):
            rtok = p.dma("sync", self.rowf[0:1, 0:1024], cosrow_d[pr:pr + 1, :])
            p.wait("tensor", rtok)
            for dc in range(2):
                bi, bank = self.pool.acquire()
                p.step("tensor", lambda e, dc=dc, bank=bank: e.matmul(
                    bank[:, 0:512], self.c_ones_rowf[:, :], self.rowf[0:1, dc * 512:(dc + 1) * 512],
                    start=True, stop=True))
                mtok = p.tok("T")
                p.wait("scalar", mtok)
                etok = p.step("scalar", lambda e, pr=pr, dc=dc, bank=bank: e.copy(
                    self.scrb[:, pr * 1024 + dc * 512: pr * 1024 + (dc + 1) * 512], bank[:, 0:512]))
                self.pool.release(bi, etok)
            p.barrier("sync", ("T",))

        SC = 1.0 / 32.0
        for tt in range(NT):
            p.barrier("sync", ("V",))
            qtok = p.dma("sync", self.scra[:, 0:4096], q_dram[tt * 128:(tt + 1) * 128, :])
            ktok = p.dma("sync", self.scra[:, 4096:8192], k_dram[tt * 128:(tt + 1) * 128, :])
            p.wait("vector", ktok)
            base = 256 + tt * 16
            for pr in range(16):
                h, g = pr // 4, pr % 4
                for dh in range(2):
                    p.step("vector", lambda e, h=h, g=g, dh=dh: e.tensor_tensor(
                        self.sm[0][:, 0:512],
                        self.scra[:, h * 1024 + dh * 512: h * 1024 + dh * 512 + 512],
                        self.scra[:, 4096 + g * 1024 + dh * 512: 4096 + g * 1024 + dh * 512 + 512],
                        ALU.mult))
                    p.step("vector", lambda e, pr=pr, dh=dh: e.scalar_tensor_tensor(
                        self.sm[1][:, 0:512], self.sm[0][:, 0:512], SC,
                        self.scrb[:, pr * 1024 + dh * 512:pr * 1024 + (dh + 1) * 512],
                        ALU.mult, ALU.mult,
                        accum_out=st[:, 248 + dh:249 + dh]))
                p.step("vector", lambda e, pr=pr, base=base: e.tensor_tensor(
                    st[:, base + pr:base + pr + 1], st[:, 248:249], st[:, 249:250], ALU.add))
            p.step("vector", lambda e, base=base: e.tensor_reduce(
                st[:, 240:244], st[:, base:base + 16].rearrange("p (h g) -> p h g", g=4),
                AX.X, ALU.max))
            p.step("vector", lambda e, base=base: e.tensor_tensor(
                st[:, 224:240].rearrange("p (h g) -> p h g", g=4),
                st[:, base:base + 16].rearrange("p (h g) -> p h g", g=4),
                bcast_free(st[:, 240:244], 4), ALU.subtract))
            p.wait("scalar", p.tok("V"))
            p.step("scalar", lambda e: e.activation(st[:, 224:240], st[:, 224:240], AF.Exp))
            p.wait("vector", p.tok("A"))
            p.step("vector", lambda e: e.tensor_reduce(
                st[:, 240:244], st[:, 224:240].rearrange("p (h g) -> p h g", g=4), AX.X, ALU.add))
            p.step("vector", lambda e: e.reciprocal(st[:, 244:248], st[:, 240:244]))
            p.step("vector", lambda e, base=base: e.tensor_tensor(
                st[:, base:base + 16].rearrange("p (h g) -> p h g", g=4),
                st[:, 224:240].rearrange("p (h g) -> p h g", g=4),
                bcast_free(st[:, 244:248], 4), ALU.mult))
        self.vflush()
        self.dump("l2attn", self.stats[:, 256:512])

        bo_tok = p.dma("gpsimd", self.rowb[0:1, 4096:5120], bo.rearrange("f -> () f"))
        omix_dram = self.dint("l2om_in", [2048, 4096], BF16)
        omix_ago = self.dint("l2om_out", [8192, 4096], BF16)
        gsel = self.din("c_groupsel", [128, 4])
        gtok = p.dma("sync", st[:, 716:720], gsel[:, :])
        for dc in range(8):
            p.barrier("gpsimd", ("T", "A"))
            for hh in range(4):
                wvtok = p.dma("gpsimd",
                              wb.rearrange("p (db h j) -> p db h j", db=8, h=4)[:, :, hh, :],
                              wv.rearrange("(db p) (h s j) -> p db h s j", p=128, h=4, s=8)[:, :, hh, dc, :])
            bvtok = p.dma("gpsimd", self.rowb[0:1, 0:512].rearrange("o (h j) -> o h j", j=128),
                          bv.rearrange("(h s j) -> h s j", h=4, s=8)[:, dc, :].rearrange("h j -> () h j"))
            p.wait("tensor", wvtok)
            p.wait("tensor", bvtok)
            for tt in range(NT):
                bi, bank = self.pool.acquire()
                for db in range(ND):
                    p.emit("tensor", lambda e, db=db, tt=tt, bank=bank: e.matmul(
                        bank[:, 0:512],
                        self.xn_fm[:, db * 2048 + tt * 128:db * 2048 + (tt + 1) * 128],
                        wb[:, db * 512:(db + 1) * 512], start=(db == 0), stop=False))
                p.emit("tensor", lambda e, bank=bank: e.matmul(
                    bank[:, 0:512], self.c_ones_row[:, :], self.rowb[0:1, 0:512],
                    start=False, stop=True), inc="T")
                mtok = p.tok("T")
                p.wait("vector", mtok)
                etok = p.step("vector", lambda e, bank=bank: e.tensor_copy(
                    self.och[:, 0:512], bank[:, 0:512]))
                self.pool.release(bi, etok)
                for h in range(4):
                    for g in range(4):
                        sc_ap = st[:, 256 + tt * 16 + h * 4 + g: 256 + tt * 16 + h * 4 + g + 1]
                        if g == 0:
                            p.step("vector", lambda e, h=h, g=g, sc_ap=sc_ap: e.tensor_scalar(
                                self.och[:, 1024 + h * 128:1024 + (h + 1) * 128],
                                self.och[:, g * 128:(g + 1) * 128], sc_ap, None, ALU.mult))
                        else:
                            p.step("vector", lambda e, h=h, g=g, sc_ap=sc_ap: e.scalar_tensor_tensor(
                                self.och[:, 1024 + h * 128:1024 + (h + 1) * 128],
                                self.och[:, g * 128:(g + 1) * 128], sc_ap,
                                self.och[:, 1024 + h * 128:1024 + (h + 1) * 128], ALU.mult, ALU.add))
                omtok = self.vflush()
                p.wait("gpsimd", omtok)
                p.dma("gpsimd",
                      omix_dram[tt * 128:(tt + 1) * 128, :].rearrange("p (h s j) -> p h s j", h=4, s=8)[:, :, dc, :],
                      self.och[:, 1024:1536].rearrange("p (h j) -> p h j", j=128))
        fl = p.tok("Dg")
        ag_tok = self.ag_go(omix_dram, omix_ago, fl)
        p.wait("sync", ag_tok)
        p.wait("vector", gtok)
        for oc in range(2):
            p.barrier("gpsimd", ("T", "A", "V"))
            wotok = p.dma("gpsimd", self.scrb[:, 0:16384].rearrange("p (fb o) -> p fb o", o=512),
                          wo.rearrange("(fb p) o -> p fb o", p=128)[:, :, oc * 512:(oc + 1) * 512])
            p.wait("tensor", wotok)
            for tt in range(NT):
                p.barrier("sync", ("T", "V"))
                for r in range(4):
                    p.dma("sync", self.scra[:, r * 4096:(r + 1) * 4096].rearrange("p (j d) -> p j d", d=1024),
                          omix_ago.rearrange("(r q) (h d) -> r q h d", r=4, h=4)[tt // 4,
                          (tt % 4) * 512:(tt % 4 + 1) * 512, r, :]
                          .rearrange("(p j) d -> p j d", p=128))
                dtok = p.tok("Ds")
                p.wait("vector", dtok)
                for r in range(4):
                    sc_ap = st[:, 716 + r:717 + r]
                    op = "init" if r == 0 else "acc"
                    for dsti, (dst, lo, ln) in enumerate(
                            ((self.och[:, 0:2048], 0, 2048),
                             (self.sm[0][:, 0:1024], 2048, 1024),
                             (self.sm[1][:, 0:1024], 3072, 1024))):
                        if r == 0:
                            p.step("vector", lambda e, r=r, sc_ap=sc_ap, dst=dst, lo=lo, ln=ln: e.tensor_scalar(
                                dst, self.scra[:, r * 4096 + lo:r * 4096 + lo + ln], sc_ap, None, ALU.mult))
                        else:
                            p.step("vector", lambda e, r=r, sc_ap=sc_ap, dst=dst, lo=lo, ln=ln: e.scalar_tensor_tensor(
                                dst, self.scra[:, r * 4096 + lo:r * 4096 + lo + ln], sc_ap,
                                dst, ALU.mult, ALU.add))
                seltok = self.vflush()
                p.wait("tensor", seltok)
                srcs = [self.och[:, 0:2048], self.sm[0][:, 0:1024], self.sm[1][:, 0:1024]]
                def piece(fb):
                    if fb < 16:
                        return srcs[0].rearrange("p (a b) -> p a b", b=128)[:, fb, :]
                    if fb < 24:
                        return srcs[1].rearrange("p (a b) -> p a b", b=128)[:, fb - 16, :]
                    return srcs[2].rearrange("p (a b) -> p a b", b=128)[:, fb - 24, :]
                for fb in range(32):
                    self.transpose(piece(fb), self.scra[:, fb * 128:(fb + 1) * 128])
                p.wait("tensor", p.tok("A"))
                p.wait("tensor", bo_tok)
                bi, bank = self.pool.acquire()
                for fb in range(32):
                    p.emit("tensor", lambda e, fb=fb, bank=bank: e.matmul(
                        bank[:, 0:512], self.scra[:, fb * 128:(fb + 1) * 128],
                        self.scrb[:, fb * 512:(fb + 1) * 512],
                        start=(fb == 0), stop=False))
                p.emit("tensor", lambda e, oc=oc, bank=bank: e.matmul(
                    bank[:, 0:512], self.c_ones_row[:, :],
                    self.rowb[0:1, 4096 + oc * 512:4096 + (oc + 1) * 512], start=False, stop=True), inc="T")
                mtok = p.tok("T")
                p.wait("vector", mtok)
                dst = self.xres[:, tt * 1024 + oc * 512: tt * 1024 + (oc + 1) * 512]
                etok = p.step("vector", lambda e, dst=dst, bank=bank: e.tensor_tensor(
                    dst, dst, bank[:, 0:512], ALU.add))
                self.pool.release(bi, etok)

    def emit_l2_ffn(self):
        p = self.p
        st = self.stats
        wb = self.wbuf
        gate_w = self.din("L2_ffn_gate_w", [1024, 8])
        gate_b = self.din("L2_ffn_gate_b", [8])
        ew = self.din("L2_ffn_expert_w", [8, 1024, 2048])
        eb = self.din("L2_ffn_expert_b", [8, 2048])
        ow = self.din("L2_ffn_out_w", [2048, 1024])
        ob = self.din("L2_ffn_out_b", [1024])
        sel_dram = self.dint("l2sel", [2048, 2048], BF16)

        self.dump("l2xres_a", self.xres[:, :])
        p.barrier("gpsimd", ("T", "A", "V"))
        gwtok = p.dma("gpsimd", wb[:, 0:64].rearrange("p (db f) -> p db f", f=8),
                      gate_w.rearrange("(db p) f -> p db f", p=128))
        gbtok = p.dma("gpsimd", self.rowb[0:1, 0:8], gate_b.rearrange("f -> () f"))
        ebtok = p.dma("gpsimd", self.scrb[0:8, 0:2048], eb[:, :])
        p.step("vector", lambda e: e.tensor_scalar(st[:, 640:648], self.c_iota8[:, :], 1e-6, None, ALU.mult))
        p.wait("tensor", gwtok)
        p.wait("tensor", gbtok)
        for tt in range(NT):
            bi, bank = self.pool.acquire()
            for db in range(ND):
                p.emit("tensor", lambda e, db=db, tt=tt, bank=bank: e.matmul(
                    bank[:, 0:8], self.xn_fm[:, db * 2048 + tt * 128:db * 2048 + (tt + 1) * 128],
                    wb[:, db * 8:(db + 1) * 8], start=(db == 0), stop=False))
            p.emit("tensor", lambda e, bank=bank: e.matmul(
                bank[:, 0:8], self.c_ones_row[:, :], self.rowb[0:1, 0:8],
                start=False, stop=True), inc="T")
            mtok = p.tok("T")
            p.wait("vector", mtok)
            p.step("vector", lambda e, bank=bank: e.tensor_tensor(st[:, 664:672], bank[:, 0:8], st[:, 640:648], ALU.subtract))
            p.step("vector", lambda e: e.reduce_max(st[:, 672:673], st[:, 664:672], AX.X))
            etok = p.step("vector", lambda e, tt=tt: e.tensor_scalar(
                st[:, 256 + tt * 8:264 + tt * 8], st[:, 664:672], st[:, 672:673], None, ALU.is_equal))
            self.pool.release(bi, etok)
        self.vflush()

        self.dump("l2coef", self.stats[:, 256:384])
        for e_ in range(8):
            p.barrier("gpsimd", ("T", "Ds"))
            wtok = p.dma("gpsimd", self.scra[:, 0:16384].rearrange("p (db f) -> p db f", f=2048),
                         ew[e_, :, :].rearrange("(db p) f -> p db f", p=128))
            p.wait("tensor", wtok)
            for tt in range(NT):
                if e_ == 0:
                    p.step("vector", lambda e, tt=tt: e.tensor_copy(
                        self.sm[1][:, 0:8], st[:, 256 + tt * 8:264 + tt * 8]))
                    p.wait("tensor", p.tok("V"))
                    self.transpose(self.sm[1][:, 0:8], None, evac="vector",
                                   evac_fn=lambda pt: p.step("vector", lambda e, pt=pt: e.tensor_copy(
                                       self.scrb[0:8, 2048:2176], pt)))
                for hc in range(4):
                    bi, bank = self.pool.acquire()
                    for db in range(ND):
                        p.emit("tensor", lambda e, db=db, tt=tt, hc=hc, bank=bank: e.matmul(
                            bank[:, 0:512],
                            self.xn_fm[:, db * 2048 + tt * 128:db * 2048 + (tt + 1) * 128],
                            self.scra[:, db * 2048 + hc * 512:db * 2048 + (hc + 1) * 512],
                            start=(db == 0), stop=(db == 7)), inc="T" if db == 7 else None)
                    mtok = p.tok("T")
                    p.wait("vector", mtok)
                    sc_ap = st[:, 256 + tt * 8 + e_:257 + tt * 8 + e_]
                    p.step("vector", lambda e, hc=hc, bank=bank, sc_ap=sc_ap: e.tensor_scalar(
                        self.och[:, hc * 512:(hc + 1) * 512], bank[:, 0:512], sc_ap, None, ALU.mult))
                    etok = p.tok("V")
                    self.pool.release(bi, etok)
                if e_ == 0:
                    for hc in range(4):
                        bi, bank = self.pool.acquire()
                        src = self.scrb[0:8, hc * 512:(hc + 1) * 512]
                        p.step("tensor", lambda e, bank=bank, src=src: e.matmul(
                            bank[:, 0:512], self.scrb[0:8, 2048:2176], src, start=True, stop=True))
                        mtok = p.tok("T")
                        p.wait("vector", mtok)
                        etok = p.step("vector", lambda e, hc=hc, bank=bank: e.tensor_tensor(
                            self.och[:, hc * 512:(hc + 1) * 512], self.och[:, hc * 512:(hc + 1) * 512],
                            bank[:, 0:512], ALU.add))
                        self.pool.release(bi, etok)
                fl = self.vflush()
                p.wait("gpsimd", fl)
                if e_ == 0:
                    p.dma("gpsimd", sel_dram[tt * 128:(tt + 1) * 128, :], self.och[:, 0:2048])
                else:
                    p.dma("gpsimd", sel_dram[tt * 128:(tt + 1) * 128, :], self.och[:, 0:2048],
                          accum_op=ALU.add)
        p.barrier("gpsimd", ("T", "A", "V", "Dg"))
        owtok = p.dma("gpsimd", self.scrb[:, 0:16384].rearrange("p (hb o) -> p hb o", o=1024),
                      ow.rearrange("(hb p) o -> p hb o", p=128))
        obtok = p.dma("gpsimd", self.rowb[0:1, 0:1024], ob.rearrange("f -> () f"))
        p.wait("tensor", owtok)
        p.wait("tensor", obtok)
        for tt in range(NT):
            p.barrier("sync", ("T", "Dg"))
            ltok = p.dma("sync", self.och[:, 0:2048], sel_dram[tt * 128:(tt + 1) * 128, :])
            p.wait("tensor", ltok)
            for fb in range(16):
                dst = self.sm[fb // 8][:, (fb % 8) * 128:(fb % 8 + 1) * 128]
                self.transpose(self.och[:, fb * 128:(fb + 1) * 128], dst)
            p.wait("tensor", p.tok("A"))
            for oc in range(2):
                bi, bank = self.pool.acquire()
                for fb in range(16):
                    lhs = self.sm[fb // 8][:, (fb % 8) * 128:(fb % 8 + 1) * 128]
                    p.emit("tensor", lambda e, fb=fb, oc=oc, bank=bank, lhs=lhs: e.matmul(
                        bank[:, 0:512], lhs,
                        self.scrb[:, fb * 1024 + oc * 512: fb * 1024 + (oc + 1) * 512],
                        start=(fb == 0), stop=False))
                p.emit("tensor", lambda e, oc=oc, bank=bank: e.matmul(
                    bank[:, 0:512], self.c_ones_row[:, :],
                    self.rowb[0:1, oc * 512:(oc + 1) * 512], start=False, stop=True), inc="T")
                mtok = p.tok("T")
                p.wait("vector", mtok)
                dst = self.xres[:, tt * 1024 + oc * 512: tt * 1024 + (oc + 1) * 512]
                etok = p.step("vector", lambda e, dst=dst, bank=bank: e.tensor_tensor(
                    dst, dst, bank[:, 0:512], ALU.add))
                self.pool.release(bi, etok)


    def emit_l3(self, xnd, bdi, bdo):
        p = self.p
        st = self.stats
        wb = self.wbuf
        w_write = self.din("L3_attn_W_write", [1024, 1024])
        w_gate = self.din("L3_attn_W_gate", [16, 1024])
        w_beta = self.din("L3_attn_W_beta", [16, 1024])
        w_alpha = self.din("L3_attn_W_alpha", [16, 1024])
        w_out = self.din("L3_attn_W_out", [1024, 1024])
        a_log = self.din("L3_attn_A_log", [16])
        dt_bias = self.din("L3_attn_dt_bias", [16])
        psel = self.din("c_prevsel", [128, 4])
        rmask = self.din("c_rankmask", [128, 4])
        agi = self.dint("ag_l3s_in", [64, 1040])
        ago = self.dint("ag_l3s_out", [256, 1040])
        o_dram = self.dint("l3o", [2048, 1024], BF16)
        decd = self.dint("l3decd", [256, 128])
        bdc = self.dint("l3bdc", [1, 1024], BF16)
        k_tm = self.scra
        v_tm = self.scrb

        bf_tok = p.dma("sync", bdi[0:1, :], xnd[2047:2048, :])
        ag_tok = self.ag_go(bdi, bdo, bf_tok)
        ps_tok = p.dma("sync", st[:, 720:724], psel[:, :])

        p.barrier("sync", ("T", "V", "A"))
        for tt in range(NT):
            p.dma("sync", k_tm[1:128, tt * 1024:(tt + 1) * 1024],
                  xnd[tt * 128:tt * 128 + 127, :])
            if tt > 0:
                p.dma("sync", k_tm[0:1, tt * 1024:(tt + 1) * 1024],
                      xnd[tt * 128 - 1:tt * 128, :])
        p.barrier("gpsimd", ("T", "V", "A"))
        p.wait("gpsimd", ag_tok)
        p.wait("vector", ps_tok)
        for r in range(4):
            bd_tok = p.dma("gpsimd", self.och[0:1, 0:1024], bdo[r:r + 1, :])
            p.wait("vector", bd_tok)
            if r == 0:
                p.step("vector", lambda e, r=r: e.tensor_scalar(
                    self.sm[0][0:1, 0:1024], self.och[0:1, 0:1024],
                    st[0:1, 720 + r:721 + r], None, ALU.mult))
            else:
                p.step("vector", lambda e, r=r: e.scalar_tensor_tensor(
                    self.sm[0][0:1, 0:1024], self.och[0:1, 0:1024],
                    st[0:1, 720 + r:721 + r], self.sm[0][0:1, 0:1024], ALU.mult, ALU.add))
            fl = self.vflush()
            p.wait("gpsimd", fl)
        bfl = self.vflush()
        p.wait("sync", bfl)
        bc_tok = p.dma("sync", bdc[0:1, :], self.sm[0][0:1, 0:1024])
        p.wait("sync", bc_tok)
        p.dma("sync", k_tm[0:1, 0:1024], bdc[0:1, :])
        s8_tok = p.dma("sync", self.sm[1][0:8, 0:128],
                       bdc.rearrange("o (db pp) -> (o db) pp", pp=128))
        p.wait("tensor", s8_tok)
        self.transpose(self.sm[1][0:8, 0:128], self.och[:, 1024:1032])

        p.barrier("gpsimd", ("T", "A", "V"))
        for wi, wmat in enumerate((w_gate, w_beta, w_alpha)):
            stok = p.dma("gpsimd", self.sm[0][0:16, 0:1024], wmat[:, :])
            p.wait("tensor", stok)
            for db in range(ND):
                self.transpose(self.sm[0][0:16, db * 128:(db + 1) * 128],
                               wb[:, db * 48 + wi * 16: db * 48 + (wi + 1) * 16])
            p.barrier("gpsimd", ("T",))
        p.step("vector", lambda e: e.memset(self.rowb[0:1, 0:48], 0.0))
        fl = self.vflush()
        p.wait("gpsimd", fl)
        p.dma("gpsimd", self.rowb[0:1, 32:48], dt_bias.rearrange("f -> () f"))
        al_tok = p.dma("sync", st[0:1, 144:160], a_log.rearrange("f -> () f"))
        p.wait("scalar", al_tok)
        p.step("scalar", lambda e: e.activation(st[0:1, 144:160], st[0:1, 144:160], AF.Exp))
        p.wait("vector", p.tok("A"))
        p.step("vector", lambda e: e.tensor_scalar(st[0:1, 144:160], st[0:1, 144:160], -1.0, None, ALU.mult))
        fl = self.vflush()
        p.wait("tensor", fl)
        bi, bank = self.pool.acquire()
        p.step("tensor", lambda e, bank=bank: e.matmul(
            bank[:, 0:16], self.c_ones_rowf[:, :], st[0:1, 144:160], start=True, stop=True))
        p.wait("vector", p.tok("T"))
        et = p.step("vector", lambda e, bank=bank: e.tensor_copy(st[:, 160:176], bank[:, 0:16]))
        self.pool.release(bi, et)

        p.wait("tensor", p.tok("Dg"))
        p.wait("tensor", p.tok("A"))
        for tt in range(NT):
            bi, bank = self.pool.acquire()
            for db in range(ND):
                p.emit("tensor", lambda e, db=db, tt=tt, bank=bank: e.matmul(
                    bank[:, 0:48], self.xn_fm[:, db * 2048 + tt * 128:db * 2048 + (tt + 1) * 128],
                    wb[:, db * 48:(db + 1) * 48], start=(db == 0), stop=False))
            p.emit("tensor", lambda e, bank=bank: e.matmul(
                bank[:, 0:48], self.c_ones_row[:, :], self.rowb[0:1, 0:48],
                start=False, stop=True), inc="T")
            mtok = p.tok("T")
            p.wait("scalar", mtok)
            p.step("scalar", lambda e, tt=tt, bank=bank: e.activation(
                self.sst_f[:, 768 + tt * 16:784 + tt * 16], bank[:, 0:16], AF.Sigmoid))
            p.step("scalar", lambda e, tt=tt, bank=bank: e.activation(
                self.och[:, tt * 16:(tt + 1) * 16], bank[:, 16:32], AF.Sigmoid))
            p.wait("vector", p.tok("A"))
            p.step("vector", lambda e, bank=bank: e.tensor_scalar(
                st[:, 128:144], bank[:, 32:48], -1.0, None, ALU.mult))
            p.step("vector", lambda e, bank=bank: e.tensor_tensor(
                st[:, 176:192], st[:, 128:144], bank[:, 32:48], ALU.min))
            p.wait("scalar", p.tok("V"))
            p.step("scalar", lambda e: e.activation(st[:, 176:192], st[:, 176:192], AF.Exp))
            p.step("scalar", lambda e: e.activation(st[:, 176:192], st[:, 176:192],
                                                    AF.Ln, bias=self.fconst(1.0)))
            p.wait("vector", p.tok("A"))
            p.step("vector", lambda e, bank=bank: e.tensor_scalar(
                st[:, 128:144], bank[:, 32:48], 0.0, None, ALU.max))
            p.step("vector", lambda e: e.tensor_tensor(st[:, 128:144], st[:, 128:144],
                                                       st[:, 176:192], ALU.add))
            p.step("vector", lambda e: e.tensor_tensor(st[:, 176:192], st[:, 128:144],
                                                       st[:, 160:176], ALU.mult))
            fl2 = self.vflush()
            self.pool.release(bi, fl2)
            p.wait("tensor", fl2)
            bi2, bank2 = self.pool.acquire()
            p.step("tensor", lambda e, bank2=bank2: e.matmul(
                bank2[:, 0:16], self.c_trilf[:, :], st[:, 176:192], start=True, stop=True))
            p.wait("vector", p.tok("T"))
            et2 = p.step("vector", lambda e, tt=tt, bank2=bank2: e.tensor_copy(
                st[:, 256 + tt * 16:272 + tt * 16], bank2[:, 0:16]))
            self.pool.release(bi2, et2)
            p.wait("tensor", et2)
            tfl = self.transpose_f32(st[:, 256 + tt * 16:272 + tt * 16],
                                     self.sst_f[0:16, 1024:1152])
            p.wait("sync", tfl)
            p.dma("sync", decd[tt * 16:(tt + 1) * 16, :], self.sst_f[0:16, 1024:1152])

        def build_wT_l3(wsrc, f0, nfb, dst, stage):
            for fb in range(nfb):
                p.barrier("gpsimd", ("T", "A", "V"))
                stok = p.dma("gpsimd", stage, wsrc[f0 + fb * 128:f0 + (fb + 1) * 128, :])
                p.wait("tensor", stok)
                for db in range(ND):
                    self.transpose(stage[:, db * 128:(db + 1) * 128],
                                   dst[:, db * (nfb * 128) + fb * 128: db * (nfb * 128) + (fb + 1) * 128])
        for fc in range(2):
            build_wT_l3(w_write, fc * 512, 4, wb[:, 0:4096], self.sm[0][:, 0:1024])
            p.wait("tensor", p.tok("A"))
            for tt in range(NT):
                bi, bank = self.pool.acquire()
                for db in range(ND):
                    p.emit("tensor", lambda e, db=db, tt=tt, bank=bank: e.matmul(
                        bank[:, 0:512],
                        self.xn_fm[:, db * 2048 + tt * 128:db * 2048 + (tt + 1) * 128],
                        wb[:, db * 512:(db + 1) * 512], start=(db == 0), stop=(db == 7)),
                        inc="T" if db == 7 else None)
                mtok = p.tok("T")
                p.wait("vector", mtok)
                etok = p.step("vector", lambda e, tt=tt, fc=fc, bank=bank: e.tensor_tensor(
                    v_tm[:, tt * 1024 + fc * 512: tt * 1024 + (fc + 1) * 512].rearrange(
                        "pp (h d) -> pp h d", d=64),
                    bank[:, 0:512].rearrange("pp (h d) -> pp h d", d=64),
                    bcast_free(self.och[:, tt * 16 + fc * 8:tt * 16 + (fc + 1) * 8], 64), ALU.mult))
                self.pool.release(bi, etok)
        self.vflush()

        def dec_batch(tt):
            p.barrier("sync", ("T",))
            rtok = p.dma("sync", self.rowf[0:1, 0:2048],
                         decd[tt * 16:(tt + 1) * 16, :].rearrange("(o r) c -> o (r c)", o=1))
            bi, bank = self.pool.acquire()
            p.wait("tensor", rtok)
            dlast = self.rowf[0:1, 0:2048].rearrange("o (h c) -> o h c", c=128)[:, :, 127:128]
            p.step("tensor", lambda e, bank=bank, dlast=dlast: e.matmul(
                bank[:, 0:16].rearrange("pp (h o) -> pp h o", o=1), self.c_ones_rowf[:, :],
                dlast, start=True, stop=True))
            p.wait("vector", p.tok("T"))
            p.step("vector", lambda e, bank=bank: e.tensor_copy(st[:, 192:208], bank[:, 0:16]))
            fl = self.vflush()
            self.pool.release(bi, fl)
            p.wait("scalar", fl)
            p.step("scalar", lambda e: e.activation(st[:, 208:224], st[:, 192:208], AF.Exp))
            p.wait("vector", p.tok("A"))
            p.step("vector", lambda e, tt=tt: e.tensor_tensor(
                st[:, 224:240], st[:, 192:208], st[:, 256 + tt * 16:272 + tt * 16], ALU.subtract))
            fl = self.vflush()
            p.wait("scalar", fl)
            p.step("scalar", lambda e: e.activation(st[:, 224:240], st[:, 224:240], AF.Exp))
            p.step("scalar", lambda e, tt=tt: e.activation(
                self.sst_f[:, 640:656], st[:, 256 + tt * 16:272 + tt * 16], AF.Exp))
            p.wait("vector", p.tok("A"))
            return rtok

        p.step("vector", lambda e: e.memset(self.sst_f[:, 0:512], 0.0))
        p.step("vector", lambda e: e.memset(st[0:1, 240:256], 0.0))
        self.vflush()
        for tt in range(NT):
            dec_batch(tt)
            p.step("vector", lambda e: e.tensor_tensor(
                st[0:1, 240:256], st[0:1, 240:256], st[0:1, 192:208], ALU.add))
            for h in range(16):
                b0 = (h % 2) * 64
                hb = h // 2
                p.step("vector", lambda e, tt=tt, h=h: e.tensor_scalar(
                    self.sm[0][:, 0:64], k_tm[:, tt * 1024 + h * 64: tt * 1024 + (h + 1) * 64],
                    st[:, 224 + h:225 + h], None, ALU.mult))
                fl = self.vflush()
                p.wait("tensor", fl)
                bi, bank = self.pool.acquire()
                p.step("tensor", lambda e, tt=tt, h=h, b0=b0, bank=bank: e.matmul(
                    bank[b0:b0 + 64, 0:64], self.sm[0][:, 0:64],
                    v_tm[:, tt * 1024 + h * 64: tt * 1024 + (h + 1) * 64], start=True, stop=True))
                mtok = p.tok("T")
                p.wait("vector", mtok)
                etok = p.step("vector", lambda e, b0=b0, hb=hb, h=h, bank=bank: e.scalar_tensor_tensor(
                    self.sst_f[b0:b0 + 64, hb * 64:(hb + 1) * 64],
                    self.sst_f[b0:b0 + 64, hb * 64:(hb + 1) * 64],
                    st[b0:b0 + 64, 208 + h:209 + h],
                    bank[b0:b0 + 64, 0:64], ALU.mult, ALU.add))
                self.pool.release(bi, etok)
        fl = self.vflush()
        p.wait("gpsimd", fl)
        f1 = p.dma("gpsimd", agi[:, 0:512], self.sst_f[0:64, 0:512])
        f2 = p.dma("gpsimd", agi[:, 512:1024], self.sst_f[64:128, 0:512])
        f3 = p.dma("gpsimd", agi[0:1, 1024:1040], st[0:1, 240:256])
        ag2 = self.ag_go(agi, ago, f3)

        rm_tok = p.dma("sync", st[:, 160:164], rmask[:, :])
        p.wait("vector", ag2)
        p.wait("vector", rm_tok)
        p.step("vector", lambda e: e.memset(self.sst_f[:, 0:512], 0.0))
        for r in range(4):
            p.wait_all("gpsimd", "V")
            g3 = p.dma("gpsimd", st[0:1, 144:160], ago[r * 64:r * 64 + 1, 1024:1040])
            p.wait("scalar", g3)
            p.step("scalar", lambda e: e.activation(st[0:1, 176:192], st[0:1, 144:160], AF.Exp))
            p.wait("vector", p.tok("A"))
            p.step("vector", lambda e: e.tensor_scalar(
                st[0:1, 176:192], st[0:1, 176:192], -1.0, None, ALU.add))
            p.step("vector", lambda e, r=r: e.tensor_scalar(
                st[0:1, 176:192], st[0:1, 176:192], st[0:1, 160 + r:161 + r], 1.0,
                ALU.mult, ALU.add))
            fl = self.vflush()
            p.wait("tensor", fl)
            bi, bank = self.pool.acquire()
            p.step("tensor", lambda e, bank=bank: e.matmul(
                bank[:, 0:16], self.c_ones_rowf[:, :], st[0:1, 176:192], start=True, stop=True))
            p.wait("vector", p.tok("T"))
            p.step("vector", lambda e, bank=bank: e.tensor_copy(st[:, 176:192], bank[:, 0:16]))
            vv = self.vflush()
            self.pool.release(bi, vv)
            for half in range(2):
                p.wait_all("gpsimd", "V")
                g1 = p.dma("gpsimd", self.sst_f[0:64, 512:768],
                           ago[r * 64:(r + 1) * 64, half * 256:(half + 1) * 256])
                g2 = p.dma("gpsimd", self.sst_f[64:128, 512:768],
                           ago[r * 64:(r + 1) * 64, 512 + half * 256:512 + (half + 1) * 256])
                p.wait("vector", g1)
                p.wait("vector", g2)
                p.step("vector", lambda e, r=r: e.tensor_scalar(
                    self.sst_f[:, 512:768], self.sst_f[:, 512:768], st[:, 160 + r:161 + r],
                    None, ALU.mult))
                for hbh in range(4):
                    for par in range(2):
                        hb = half * 4 + hbh
                        h = hb * 2 + par
                        b0 = par * 64
                        p.step("vector", lambda e, b0=b0, hb=hb, hbh=hbh, h=h:
                               e.scalar_tensor_tensor(
                                   self.sst_f[b0:b0 + 64, hb * 64:(hb + 1) * 64],
                                   self.sst_f[b0:b0 + 64, hb * 64:(hb + 1) * 64],
                                   st[b0:b0 + 64, 176 + h:177 + h],
                                   self.sst_f[b0:b0 + 64, 512 + hbh * 64:512 + (hbh + 1) * 64],
                                   ALU.mult, ALU.add))
        p.step("vector", lambda e: e.tensor_copy(self.sst_b[:, 0:512], self.sst_f[:, 0:512]))
        self.vflush()

        for tt in range(NT):
            dec_batch(tt)
            for h in range(16):
                b0 = (h % 2) * 64
                hb = h // 2
                db = h // 2
                bi_b, bank_b = self.pool.acquire()
                p.step("tensor", lambda e, h=h, bank_b=bank_b: e.matmul(
                    bank_b[:, 0:128], self.c_ones_rowf[:, :], self.rowf[0:1, h * 128:(h + 1) * 128],
                    start=True, stop=True))
                p.wait("vector", p.tok("T"))
                p.step("vector", lambda e, tt=tt, h=h, bank_b=bank_b: e.tensor_scalar(
                    self.sst_f[:, 512:640], bank_b[:, 0:128],
                    st[:, 256 + tt * 16 + h:257 + tt * 16 + h], 0.0, ALU.subtract, ALU.min))
                fl = self.vflush()
                self.pool.release(bi_b, fl)
                p.wait("scalar", fl)
                p.step("scalar", lambda e: e.activation(self.sm[1][:, 0:128],
                                                        self.sst_f[:, 512:640], AF.Exp))
                p.wait("vector", p.tok("A"))
                p.step("vector", lambda e: e.tensor_tensor(
                    self.sm[1][:, 128:256], self.sm[1][:, 0:128], self.c_l3strict[:, :], ALU.mult))
                if tt == 0:
                    p.step("vector", lambda e, db=db, b0=b0: e.tensor_copy(
                        self.sm[0][b0:b0 + 64, 385:512],
                        self.xn_fm[b0:b0 + 64, db * 2048: db * 2048 + 127]))
                    p.step("vector", lambda e, db=db, b0=b0: e.tensor_copy(
                        self.sm[0][b0:b0 + 64, 384:385], self.och[b0:b0 + 64, 1024 + db:1025 + db]))
                    wk_ap = self.sm[0][b0:b0 + 64, 384:512]
                else:
                    wk_ap = self.xn_fm[b0:b0 + 64,
                                       db * 2048 + tt * 128 - 1: db * 2048 + (tt + 1) * 128 - 1]
                fl = self.vflush()
                p.wait("tensor", fl)
                bi_i, bank_i = self.pool.acquire()
                p.step("tensor", lambda e, wk_ap=wk_ap, db=db, b0=b0, tt=tt, bank_i=bank_i: e.matmul(
                    bank_i[:, 0:128], wk_ap,
                    self.xn_fm[b0:b0 + 64, db * 2048 + tt * 128: db * 2048 + (tt + 1) * 128],
                    start=True, stop=True))
                p.wait("vector", p.tok("T"))
                p.step("vector", lambda e, bank_i=bank_i: e.tensor_tensor(
                    self.sm[0][:, 0:128], bank_i[:, 0:128], self.sm[1][:, 128:256], ALU.mult))
                ifl = self.vflush()
                self.pool.release(bi_i, ifl)
                p.wait("tensor", ifl)
                bi_n, bank_n = self.pool.acquire()
                p.step("tensor", lambda e, tt=tt, h=h, bank_n=bank_n: e.matmul(
                    bank_n[:, 0:64], self.sm[0][:, 0:128],
                    v_tm[:, tt * 1024 + h * 64: tt * 1024 + (h + 1) * 64], start=True, stop=True))
                bi_n2, bank_n2 = self.pool.acquire()
                p.step("tensor", lambda e, db=db, b0=b0, hb=hb, tt=tt, bank_n2=bank_n2: e.matmul(
                    bank_n2[:, 0:64],
                    self.xn_fm[b0:b0 + 64, db * 2048 + tt * 128: db * 2048 + (tt + 1) * 128],
                    self.sst_b[b0:b0 + 64, hb * 64:(hb + 1) * 64], start=True, stop=True))
                ntok = p.tok("T")
                p.wait("vector", ntok)
                p.step("vector", lambda e, h=h, bank_n2=bank_n2: e.tensor_scalar(
                    self.sm[0][:, 256:320], bank_n2[:, 0:64], self.sst_f[:, 640 + h:641 + h],
                    None, ALU.mult))
                p.step("vector", lambda e, bank_n=bank_n: e.tensor_tensor(
                    self.sm[0][:, 256:320], self.sm[0][:, 256:320], bank_n[:, 0:64], ALU.add))
                otok = p.step("vector", lambda e, tt=tt, h=h: e.tensor_scalar(
                    self.och[:, h * 64:(h + 1) * 64], self.sm[0][:, 256:320],
                    self.sst_f[:, 768 + tt * 16 + h:769 + tt * 16 + h], None, ALU.mult))
                self.pool.release(bi_n, otok)
                self.pool.release(bi_n2, otok)
                p.step("vector", lambda e, tt=tt, h=h: e.tensor_scalar(
                    self.sm[0][:, 320:384], k_tm[:, tt * 1024 + h * 64: tt * 1024 + (h + 1) * 64],
                    st[:, 224 + h:225 + h], None, ALU.mult))
                ufl = self.vflush()
                p.wait("tensor", ufl)
                bi_u, bank_u = self.pool.acquire()
                p.step("tensor", lambda e, tt=tt, h=h, b0=b0, bank_u=bank_u: e.matmul(
                    bank_u[b0:b0 + 64, 0:64], self.sm[0][:, 320:384],
                    v_tm[:, tt * 1024 + h * 64: tt * 1024 + (h + 1) * 64], start=True, stop=True))
                p.wait("vector", p.tok("T"))
                p.step("vector", lambda e, b0=b0, hb=hb, h=h, bank_u=bank_u: e.scalar_tensor_tensor(
                    self.sst_f[b0:b0 + 64, hb * 64:(hb + 1) * 64],
                    self.sst_f[b0:b0 + 64, hb * 64:(hb + 1) * 64],
                    st[b0:b0 + 64, 208 + h:209 + h],
                    bank_u[b0:b0 + 64, 0:64], ALU.mult, ALU.add))
                cfl = p.step("vector", lambda e, b0=b0, hb=hb: e.tensor_copy(
                    self.sst_b[b0:b0 + 64, hb * 64:(hb + 1) * 64],
                    self.sst_f[b0:b0 + 64, hb * 64:(hb + 1) * 64]))
                self.pool.release(bi_u, cfl)
            fl = self.vflush()
            p.wait("sync", fl)
            p.dma("sync", o_dram[tt * 128:(tt + 1) * 128, :], self.och[:, 0:1024])
        self.dump("l3_o", self.och[:, 0:1024])

        for ob in range(8):
            build_wT_l3(w_out, ob * 128, 1, k_tm[:, ob * 1024: ob * 1024 + 1024],
                        self.sm[0][:, 0:1024])
        woutT = k_tm.rearrange("pp (ob eb j) -> pp ob eb j", eb=8, j=128)
        for tt in range(NT):
            p.barrier("sync", ("T", "V"))
            ltok = p.dma("sync", self.och[:, 0:1024], o_dram[tt * 128:(tt + 1) * 128, :])
            xtok = p.dma("sync", self.sm[0][:, 0:1024], xnd[tt * 128:(tt + 1) * 128, :])
            p.wait("vector", xtok)
            dstx = self.xres[:, tt * 1024:(tt + 1) * 1024]
            p.step("vector", lambda e, dstx=dstx: e.tensor_tensor(
                dstx, dstx, self.sm[0][:, 0:1024], ALU.add))
            p.wait("tensor", ltok)
            for eb in range(8):
                self.transpose(self.och[:, eb * 128:(eb + 1) * 128],
                               self.och[:, 1024 + eb * 128: 1024 + (eb + 1) * 128])
            p.wait("tensor", p.tok("A"))
            for oc in range(2):
                bi, bank = self.pool.acquire()
                for eb in range(8):
                    p.emit("tensor", lambda e, eb=eb, oc=oc, bank=bank: e.matmul(
                        bank[:, 0:512], self.och[:, 1024 + eb * 128: 1024 + (eb + 1) * 128],
                        woutT[:, oc * 4:(oc + 1) * 4, eb, :],
                        start=(eb == 0), stop=(eb == 7)), inc="T" if eb == 7 else None)
                mtok = p.tok("T")
                p.wait("vector", mtok)
                dst = self.xres[:, tt * 1024 + oc * 512: tt * 1024 + (oc + 1) * 512]
                etok = p.step("vector", lambda e, dst=dst, bank=bank: e.tensor_tensor(
                    dst, dst, bank[:, 0:512], ALU.add))
                self.pool.release(bi, etok)
        self.dump("l3_xres", self.xres[:, :])

    def emit_l3_ffn_gate(self):
        p = self.p
        st = self.stats
        w1 = self.din("L3_ffn_W1", [1024, 512])
        b1 = self.din("L3_ffn_b1", [512])
        w2 = self.din("L3_ffn_W2", [512, 8])
        b2 = self.din("L3_ffn_b2", [8])
        p.barrier("sync", ("T", "V", "A"))
        b1tok = p.dma("sync", self.rowf[0:1, 768:1280], b1.rearrange("f -> () f"))
        b2tok = p.dma("sync", self.rowf[0:1, 1280:1288], b2.rearrange("f -> () f"))
        w2tok = p.dma("sync", st[:, 640:672].rearrange("pp (fb o) -> pp fb o", o=8),
                      w2.rearrange("(fb pp) o -> pp fb o", pp=128))
        p.wait("vector", p.tok("Ds"))
        p.step("vector", lambda e: e.tensor_scalar(st[:, 672:680], self.c_iota8[:, :],
                                                   1e-6, None, ALU.mult))
        bi, bank = self.pool.acquire()
        for db in range(ND):
            wtok = p.dma("sync", self.sst_f[:, 128:640], w1[db * 128:(db + 1) * 128, :])
            p.wait("tensor", wtok)
            mt = p.emit("tensor", lambda e, db=db, bank=bank: e.matmul(
                bank[0:1, 0:512], self.c_ones_colf[:, :], self.sst_f[:, 128:640],
                start=(db == 0), stop=(db == 7)), inc="T")
            p.wait("sync", mt)
        p.wait("vector", p.tok("T"))
        cse = p.step("vector", lambda e, bank=bank: e.tensor_copy(
            self.rowf[0:1, 256:768], bank[0:1, 0:512]))
        self.pool.release(bi, cse)
        self.vflush()
        for tt in range(NT):
            bi, bank = self.pool.acquire()
            for db in range(ND):
                wtok = p.dma("sync", self.sst_f[:, 128:640], w1[db * 128:(db + 1) * 128, :])
                fl = self.transpose_f32(
                    self.xres[:, tt * 1024 + db * 128: tt * 1024 + (db + 1) * 128],
                    self.sst_f[:, 0:128])
                p.wait("tensor", fl)
                p.wait("tensor", wtok)
                mt = p.emit("tensor", lambda e, db=db, bank=bank: e.matmul(
                    bank[:, 0:512], self.sst_f[:, 0:128], self.sst_f[:, 128:640],
                    start=(db == 0), stop=False), inc="T")
                p.wait("sync", mt)
            p.step("vector", lambda e, tt=tt: e.tensor_scalar(
                st[:, 728:729], st[:, 32 + tt:33 + tt], -1.0, None, ALU.mult))
            p.wait("tensor", p.tok("V"))
            t1 = self.transpose_f32(st[:, 728:729], self.rowf[0:1, 0:128])
            t2 = self.transpose_f32(st[:, 96 + tt:97 + tt], self.rowf[0:1, 128:256])
            p.wait("tensor", t1)
            p.wait("tensor", t2)
            p.wait("tensor", b1tok)
            p.emit("tensor", lambda e, bank=bank: e.matmul(
                bank[:, 0:512], self.rowf[0:1, 0:128], self.rowf[0:1, 256:768],
                start=False, stop=False))
            p.emit("tensor", lambda e, bank=bank: e.matmul(
                bank[:, 0:512], self.rowf[0:1, 128:256], self.rowf[0:1, 768:1280],
                start=False, stop=True), inc="T")
            mtok = p.tok("T")
            p.wait("vector", mtok)
            p.step("vector", lambda e, tt=tt, bank=bank: e.tensor_scalar(
                self.sst_f[:, 640:1152], bank[:, 0:512], st[:, 112 + tt:113 + tt],
                0.0, ALU.mult, ALU.max))
            h1fl = self.vflush()
            self.pool.release(bi, h1fl)
            p.wait("tensor", h1fl)
            p.wait("tensor", w2tok)
            bi, bank = self.pool.acquire()
            for fb in range(4):
                fl = self.transpose_f32(self.sst_f[:, 640 + fb * 128:640 + (fb + 1) * 128],
                                        self.sst_f[:, 0:128])
                p.wait("tensor", fl)
                p.emit("tensor", lambda e, fb=fb, bank=bank: e.matmul(
                    bank[:, 0:8], self.sst_f[:, 0:128], st[:, 640 + fb * 8:648 + fb * 8],
                    start=(fb == 0), stop=False))
            p.wait("tensor", b2tok)
            p.emit("tensor", lambda e, bank=bank: e.matmul(
                bank[:, 0:8], self.c_ones_rowf[:, :], self.rowf[0:1, 1280:1288],
                start=False, stop=True), inc="T")
            mtok = p.tok("T")
            p.wait("vector", mtok)
            p.step("vector", lambda e, bank=bank: e.tensor_tensor(
                st[:, 680:688], bank[:, 0:8], st[:, 672:680], ALU.subtract))
            p.step("vector", lambda e: e.reduce_max(st[:, 688:689], st[:, 680:688], AX.X))
            cfe = p.step("vector", lambda e, tt=tt: e.tensor_scalar(
                st[:, 512 + tt * 8:520 + tt * 8], st[:, 680:688], st[:, 688:689],
                None, ALU.is_equal))
            self.pool.release(bi, cfe)
        self.vflush()
        self.dump("l3_coef", st[:, 512:640])

    def emit_l3_ffn_exp(self):
        p = self.p
        st = self.stats
        ew = self.din("L3_ffn_expert_W", [8, 1024, 1024])
        eb = self.din("L3_ffn_expert_b", [8, 1024])
        p.barrier("gpsimd", ("T", "V", "A"))
        ebtok = p.dma("gpsimd", self.sst_b[0:8, 0:1024], eb[:, :])
        for e_ in range(8):
            p.barrier("gpsimd", ("T",))
            wtok = p.dma("gpsimd", self.scrb[:, 0:8192].rearrange("pp (db f) -> pp db f", f=1024),
                         ew[e_, :, :].rearrange("(db pp) f -> pp db f", pp=128))
            p.wait("tensor", wtok)
            if e_ == 0:
                p.wait("tensor", ebtok)
            for tt in range(NT):
                if e_ == 0:
                    p.step("vector", lambda e, tt=tt: e.tensor_copy(
                        self.sm[1][:, 0:8], st[:, 512 + tt * 8:520 + tt * 8]))
                    fl = self.vflush()
                    p.wait("tensor", fl)
                    self.transpose(self.sm[1][:, 0:8], None, evac="vector",
                                   evac_fn=lambda pt_ap: p.step("vector",
                                       lambda e, pt_ap=pt_ap: e.tensor_copy(
                                           self.sm[1][0:8, 128:256], pt_ap)))
                for oc in range(2):
                    bi, bank = self.pool.acquire()
                    for db in range(ND):
                        p.emit("tensor", lambda e, db=db, tt=tt, oc=oc, bank=bank: e.matmul(
                            bank[:, 0:512],
                            self.xn_fm[:, db * 2048 + tt * 128:db * 2048 + (tt + 1) * 128],
                            self.scrb[:, db * 1024 + oc * 512:db * 1024 + (oc + 1) * 512],
                            start=(db == 0), stop=(db == 7)), inc="T" if db == 7 else None)
                    mtok = p.tok("T")
                    p.wait("vector", mtok)
                    sc_ap = st[:, 512 + tt * 8 + e_:513 + tt * 8 + e_]
                    dst = self.xres[:, tt * 1024 + oc * 512: tt * 1024 + (oc + 1) * 512]
                    etok = p.step("vector", lambda e, dst=dst, bank=bank, sc_ap=sc_ap:
                                  e.scalar_tensor_tensor(dst, bank[:, 0:512], sc_ap, dst,
                                                         ALU.mult, ALU.add))
                    self.pool.release(bi, etok)
                if e_ == 0:
                    for oc in range(2):
                        bi, bank = self.pool.acquire()
                        p.step("tensor", lambda e, oc=oc, bank=bank: e.matmul(
                            bank[:, 0:512], self.sm[1][0:8, 128:256],
                            self.sst_b[0:8, oc * 512:(oc + 1) * 512], start=True, stop=True))
                        mtok = p.tok("T")
                        p.wait("vector", mtok)
                        dst = self.xres[:, tt * 1024 + oc * 512: tt * 1024 + (oc + 1) * 512]
                        etok = p.step("vector", lambda e, dst=dst, bank=bank: e.tensor_tensor(
                            dst, dst, bank[:, 0:512], ALU.add))
                        self.pool.release(bi, etok)
        self.dump("l3f_xres", self.xres[:, :])



def build_layers(layers, dbg=False):
    nc = bass.Bass("TRN2", target_bir_lowering=False)
    p = Prog()
    mk = MK(nc, p, dbg=dbg)
    stk = ExitStack()
    with stk:
        mk.alloc(stk)
        x_in = mk.din("x_in", [T, 1024])
        y_out = mk.dout("y_out", [T, 1024])
        mk.load_consts()
        ld = mk.emit_load_x(x_in)
        for eng in ("tensor", "vector", "scalar"):
            p.wait_all(eng, "Ds")
            p.wait_all(eng, "Dg")
        for li in layers:
            if li == 0:
                ag16i, ag16o = mk.allgather("ag_x16", 16, 1024, BF16)
                agt = mk.emit_ln(xres_tok=ld, ag16=ag16i)
                ag16_tok = mk.ag_go(ag16i, ag16o, agt)
                ctx = mk.emit_l0_attn_pre()
                mk.emit_l0_attn_post(ctx)
                mk.emit_l0_ffn(ag16o, ag16_tok)
            elif li == 1:
                mk.emit_ln(xres_tok=ld)
                mk.emit_l1()
                mk.emit_l1_ffn()
            elif li == 2:
                mk.emit_ln(xres_tok=ld)
                mk.emit_l2()
                mk.emit_l2_ffn()
            elif li == 3:
                xnd = mk.dint("xn3_tm", [T, 1024], BF16)
                bdi, bdo = mk.allgather("ag_l3b", 1, 1024, BF16)
                mk.emit_ln(xres_tok=ld, write_dram=xnd)
                mk.emit_l3_ffn_gate()
                mk.emit_l3(xnd, bdi, bdo)
                mk.emit_l3_ffn_exp()
        mk.emit_store_x(y_out)
        p.realize(nc, stk)
    return nc, mk


def shard_inputs(full, core):
    return {}


_BUILD_CACHE = {}


def _get_prog(key, layers):
    if key not in _BUILD_CACHE:
        _BUILD_CACHE[key] = build_layers(layers)
    return _BUILD_CACHE[key]


def kernel(**inputs):
    consts = host_consts()
    x = np.ascontiguousarray(inputs["L0_attn_x"].reshape(16384, 1024))
    stages = [[0], [1], [2], [3]]
    cur = x
    for si, layers in enumerate(stages):
        nc, mk = _get_prog(tuple(layers), layers)
        in_maps = []
        fv = np.zeros((128, 64), np.float32)
        fv[:, :len(mk.fvals)] = np.array(mk.fvals, np.float32)[None, :]
        for c in range(NCORE):
            m = {"x_in": np.ascontiguousarray(cur[c * T:(c + 1) * T])}
            rm = np.zeros((128, 4), np.float32)
            rm[:, :c % 4] = 1.0
            rm8 = np.zeros((128, 8), np.float32)
            rm8[:, :c] = 1.0
            for name in mk.input_names:
                if name == "c_fvals":
                    m[name] = fv
                elif name == "c_rankmask":
                    m[name] = rm
                elif name == "c_rankmask8":
                    m[name] = rm8
                elif name == "c_prevsel":
                    pm = np.zeros((128, 4), np.float32)
                    if c % 4 > 0:
                        pm[:, c % 4 - 1] = 1.0
                    m[name] = pm
                elif name == "c_groupsel":
                    gs = np.zeros((128, 4), np.float32)
                    gs[:, c % 4] = 1.0
                    m[name] = gs
                elif name in consts:
                    m[name] = consts[name]
                elif name in inputs:
                    a = inputs[name]
                    if a.ndim == 0:
                        a = a.reshape(1)
                    m[name] = np.ascontiguousarray(a)
                elif name == "x_in":
                    pass
                else:
                    raise KeyError(name)
            in_maps.append(m)
        res = run_bass_kernel_spmd(nc, in_maps, core_ids=list(range(NCORE)))
        cur = np.concatenate([res.results[c]["y_out"] for c in range(NCORE)], axis=0)
    return cur.reshape(2, 8192, 1024)


# version 02
# speedup vs baseline: 1.0088x; 1.0088x over previous
import math
from contextlib import ExitStack

import numpy as np
import ml_dtypes
import concourse.bass as bass
import concourse.mybir as mybir
from concourse.bass_utils import run_bass_kernel_spmd

F32 = mybir.dt.float32
BF16 = mybir.dt.bfloat16
AF = mybir.ActivationFunctionType
ALU = mybir.AluOpType
AX = mybir.AxisListType

NCORE = 8
T = 2048
NT = 16
D = 1024
ND = 8
BF = ml_dtypes.bfloat16
RG = [[0, 1, 2, 3], [4, 5, 6, 7]]

ENGS = ("tensor", "vector", "scalar", "sync", "gpsimd")
SEMS = {"T": 1, "V": 1, "A": 1, "G": 1, "Dg": 16, "Ds": 16, "Da": 16, "C": 1}
ENG_SEM = {"tensor": "T", "vector": "V", "scalar": "A", "gpsimd": "G"}
ENG_DMA_SEM = {"gpsimd": "Dg", "sync": "Ds", "scalar": "Da"}


STRICT = True
STRICT_ENGS = ("vector", "scalar")
STRICT_SEMS = tuple(SEMS)


class Prog:
    def __init__(self):
        self.q = {e: [] for e in ENGS}
        self.cnt = {s: 0 for s in SEMS}

    def emit(self, eng, fn, inc=None):
        if STRICT and eng in STRICT_ENGS:
            for s2 in STRICT_SEMS:
                self.q[eng].append(("w", s2, self.cnt[s2]))
            if inc is None:
                inc = ENG_SEM.get(eng)
                if eng in ("sync",):
                    inc = None
        tok = None
        if inc is not None:
            self.cnt[inc] += SEMS[inc]
            tok = (inc, self.cnt[inc])
        self.q[eng].append(("i", fn, inc))
        return tok

    def step(self, eng, fn):
        return self.emit(eng, fn, inc=ENG_SEM[eng])

    def dma(self, eng, out, in_, **kw):
        return self.emit(eng, lambda e: e.dma_start(out=out, in_=in_, **kw),
                         inc=ENG_DMA_SEM[eng])

    def wait(self, eng, tok):
        if tok is None:
            return
        sem, val = tok
        assert val <= self.cnt[sem], f"wait on future {sem}={val} > {self.cnt[sem]}"
        self.q[eng].append(("w", sem, val))

    def wait_all(self, eng, sem):
        self.q[eng].append(("w", sem, self.cnt[sem]))

    def barrier(self, eng, sems=("T", "V", "A")):
        for s in sems:
            self.wait_all(eng, s)

    def tok(self, sem):
        return (sem, self.cnt[sem])

    def realize(self, nc, stk):
        semh = {s: stk.enter_context(nc.semaphore(f"sem_{s}")) for s in SEMS}
        blk = stk.enter_context(nc.Block())
        for eng in ENGS:
            items = self.q[eng]

            def make(eng_items):
                def body(e):
                    for it in eng_items:
                        if it[0] == "w":
                            e.wait_ge(semh[it[1]], it[2])
                        else:
                            inst = it[1](e)
                            if it[2] is not None:
                                inst.then_inc(semh[it[2]], SEMS[it[2]])
                return body

            getattr(blk, eng)(make(items))


class PsumPool:
    def __init__(self, p, banks):
        self.p = p
        self.banks = banks
        self.freetok = [None] * len(banks)
        self.i = 0

    def acquire(self, eng="tensor"):
        i = self.i
        self.i = (self.i + 1) % len(self.banks)
        self.p.wait(eng, self.freetok[i])
        return i, self.banks[i]

    def release(self, i, tok):
        self.freetok[i] = tok


def host_consts():
    c = {}
    c["c_ident"] = np.eye(128, dtype=BF)
    m = np.zeros((128, 2, 256), dtype=BF)
    for r in range(2):
        for pp in range(128):
            m[pp, r, :] = (np.arange(256) >= (r * 128 + pp))
    c["c_l1mask"] = np.ascontiguousarray(m.reshape(128, 512))
    s = np.zeros((128, 128), dtype=BF)
    for k in range(128):
        s[k, k + 1:] = 1
    c["c_l3strict"] = s
    tr = np.zeros((128, 128), dtype=BF)
    for k in range(128):
        tr[k, k:] = 1
    c["c_tril"] = tr
    c["c_ones_row"] = np.ones((1, 128), dtype=BF)
    c["c_ones_rowf"] = np.ones((1, 128), dtype=np.float32)
    c["c_ones_col"] = np.ones((128, 1), dtype=BF)
    c["c_iota8"] = np.tile(np.arange(8, dtype=np.float32)[None, :], (128, 1))
    c["c_identf"] = np.eye(128, dtype=np.float32)
    c["c_ones_colf"] = np.ones((128, 1), dtype=np.float32)
    c["c_trilf"] = np.tril(np.ones((128, 128), np.float32)).T.copy()
    return c


CONST_SPECS = {"c_ident": ([128, 128], BF16), "c_l1mask": ([128, 512], BF16),
               "c_l3strict": ([128, 128], BF16), "c_tril": ([128, 128], BF16),
               "c_ones_row": ([1, 128], BF16), "c_ones_rowf": ([1, 128], F32),
               "c_ones_col": ([128, 1], BF16), "c_iota8": ([128, 8], F32),
               "c_identf": ([128, 128], F32), "c_ones_colf": ([128, 1], F32),
               "c_trilf": ([128, 128], F32)}


def bcast_free(ap, n):
    return bass.AP(ap.tensor, ap.offset, list(ap.ap) + [[0, n]])


class MK:
    def __init__(self, nc, p, dbg=False):
        self.nc = nc
        self.p = p
        self.dbg = dbg
        self.dbg_outs = []
        self.input_names = []
        self._tp_i = 0
        self.fvals = []
        self._fmap = {}
        self._din_c = {}
        self._tpf_i = 0

    def fconst(self, v, dt=None):
        for d in ([dt] if dt is not None else [F32, BF16]):
            key = (d, float(v))
            if key not in self._fmap:
                if float(v) not in self.fvals:
                    self.fvals.append(float(v))
                i = self.fvals.index(float(v))
                ap = self.cvals[:, i:i + 1] if d == F32 else self.cvalsb[:, i:i + 1]
                self.nc.const_aps.register(d, float(v), ap)
                self._fmap[key] = True
        return float(v)

    def din(self, name, shape, dtype=F32):
        if name in self._din_c:
            return self._din_c[name]
        t = self.nc.dram_tensor(name, list(shape), dtype, kind="ExternalInput")
        self.input_names.append(name)
        self._din_c[name] = t.ap()
        return self._din_c[name]

    def dout(self, name, shape, dtype=F32):
        return self.nc.dram_tensor(name, list(shape), dtype, kind="ExternalOutput").ap()

    def dint(self, name, shape, dtype=F32):
        return self.nc.dram_tensor(name, list(shape), dtype, kind="Internal").ap()

    def dump(self, name, sb_ap, dtype=None):
        if not self.dbg:
            return
        p = self.p
        shape = [d[1] for d in sb_ap.ap]
        dt = dtype or sb_ap.dtype
        o = self.dout("dbg_" + name, shape, dt)
        self.dbg_outs.append("dbg_" + name)
        eng = "sync" if dt == sb_ap.dtype else "gpsimd"
        p.barrier(eng, ("T", "V", "A", "G", "Dg", "Ds", "Da", "C"))
        tok = p.dma(eng, o, sb_ap)
        for e2 in ("tensor", "vector", "scalar", "sync", "gpsimd"):
            p.wait(e2, tok)

    def alloc(self, stk):
        nc = self.nc
        S = lambda name, shape, dt: stk.enter_context(nc.sbuf_tensor(name, shape, dt))
        self.xres = S("xres", [128, NT * 1024], F32)
        self.xn_fm = S("xn_fm", [128, ND * 2048], BF16)
        self.scra = S("scra", [128, 16384], BF16)
        self.scrb = S("scrb", [128, 16384], BF16)
        self.stats = S("stats", [128, 768], F32)
        self.rowb = S("rowb", [1, 5120], BF16)
        self.rowf = S("rowf", [1, 2048], F32)
        self.wbuf = S("wbuf", [128, 4096], BF16)
        self.sm = [S("sm1", [128, 1024], BF16), S("sm2", [128, 1024], BF16)]
        self.sst_f = S("sst_f", [128, 1152], F32)
        self.sst_b = S("sst_b", [128, 1024], BF16)
        self.och = S("och", [128, 2048], BF16)
        for name, (shape, dt) in CONST_SPECS.items():
            setattr(self, name, S(name + "_sb", shape, dt))
        self.cvals = S("cvals", [128, 64], F32)
        self.cvalsb = S("cvalsb", [128, 64], BF16)
        P = lambda name, shape, dt: stk.enter_context(nc.psum_tensor(name, shape, dt))
        self.pb = [P(f"pb{i}", [128, 512], F32) for i in range(6)]
        self.pt = [P(f"pt{i}", [128, 1024], BF16) for i in range(2)]
        self.pt_tok = [None, None]
        self.pool = PsumPool(self.p, self.pb[:5])
        self.ptf = self.pb[5]
        self.ptf_tok = [None, None]

    def load_consts(self):
        for name in CONST_SPECS:
            shape, dt = CONST_SPECS[name]
            ap = self.din(name, shape, dt)
            self.p.dma("sync", getattr(self, name)[:, :], ap[:, :])
        fv = self.din("c_fvals", [128, 64], F32)
        self.p.dma("sync", self.cvals[:, :], fv[:, :])
        self.p.dma("gpsimd", self.cvalsb[:, :], fv[:, :])
        return self.p.tok("Ds")

    def transpose(self, src, dst=None, evac="scalar", evac_fn=None, out_base=0):
        p = self.p
        pp = src.ap[0][1]
        ff = src.ap[-1][1]
        b = self._tp_i
        self._tp_i = 1 - b
        p.wait("tensor", self.pt_tok[b])
        p.step("tensor", lambda e: e.transpose(self.pt[b][out_base:out_base + ff, 0:pp], src,
                                               self.c_ident[0:pp, 0:pp]))
        ttok = p.tok("T")
        pt_ap = self.pt[b][out_base:out_base + ff, 0:pp]
        p.wait(evac, ttok)
        if evac_fn is not None:
            etok = evac_fn(pt_ap)
        elif evac == "scalar":
            etok = p.step("scalar", lambda e: e.copy(dst, pt_ap))
        else:
            etok = p.step("vector", lambda e: e.tensor_copy(dst, pt_ap))
        self.pt_tok[b] = etok
        return etok

    def vflush(self):
        return self.p.step("vector", lambda e: e.tensor_copy(
            self.stats[0:1, 767:768], self.cvals[0:1, 0:1]))

    def transpose_f32(self, src, dst, evac="vector"):
        p = self.p
        pp = src.ap[0][1]
        ff = src.ap[-1][1]
        b = self._tpf_i
        self._tpf_i = 1 - b
        p.wait("tensor", self.ptf_tok[b])
        bank_ap = self.ptf[0:ff, b * 128:b * 128 + pp]
        p.step("tensor", lambda e: e.transpose(bank_ap, src, self.c_identf[0:pp, 0:pp]))
        ttok = p.tok("T")
        p.wait(evac, ttok)
        if evac == "scalar":
            etok = p.step("scalar", lambda e: e.copy(dst, bank_ap))
        else:
            etok = p.step("vector", lambda e: e.tensor_copy(dst, bank_ap))
        self.ptf_tok[b] = etok
        return etok

    def gate_stage1_fp32(self, tt, rhs_fn, nout, cs_row, svb_row):
        p = self.p
        st = self.stats
        p.step("vector", lambda e, tt=tt: e.tensor_scalar(
            st[:, 720 + 0:721 + 0], st[:, 32 + tt:33 + tt], -1.0, None, ALU.mult))
        p.wait("tensor", p.tok("V"))
        t1 = self.transpose_f32(st[:, 720:721], self.rowf[0:1, 0:128])
        t2 = self.transpose_f32(st[:, 96 + tt:97 + tt], self.rowf[0:1, 128:256])
        p.wait("tensor", t1)
        p.wait("tensor", t2)
        bi, bank = self.pool.acquire()
        for db in range(ND):
            fl = self.transpose_f32(self.xres[:, tt * 1024 + db * 128: tt * 1024 + (db + 1) * 128],
                                    self.sst_f[:, 0:128])
            p.wait("tensor", fl)
            p.emit("tensor", lambda e, db=db, bank=bank, rhs=rhs_fn(db), nout=nout: e.matmul(
                bank[:, 0:nout], self.sst_f[:, 0:128], rhs, start=(db == 0), stop=False))
        p.emit("tensor", lambda e, bank=bank, nout=nout, cs_row=cs_row: e.matmul(
            bank[:, 0:nout], self.rowf[0:1, 0:128], cs_row,
            start=False, stop=(svb_row is None)), inc="T" if svb_row is None else None)
        if svb_row is not None:
            p.emit("tensor", lambda e, bank=bank, nout=nout, svb_row=svb_row: e.matmul(
                bank[:, 0:nout], self.rowf[0:1, 128:256], svb_row,
                start=False, stop=True), inc="T")
        return bi, bank, p.tok("T")

    def allgather(self, name, rows, cols, dtype=F32):
        i = self.dint(name + "_in", [rows, cols], dtype)
        o = self.dint(name + "_out", [rows * 4, cols], dtype)
        return i, o

    def ag_go(self, in_ap, out_ap, fill_tok, kind="AllGather", groups=None):
        p = self.p
        p.wait("gpsimd", fill_tok)
        op = ALU.bypass if kind == "AllGather" else ALU.add
        rg = groups if groups is not None else RG
        return p.emit("gpsimd", lambda g: g.collective_compute(
            kind, op, replica_groups=rg,
            ins=[in_ap[:, :]], outs=[out_ap[:, :]]), inc="C")

    def emit_load_x(self, x_in):
        return self.p.dma("sync", self.xres.rearrange("p (t d) -> p t d", d=1024),
                          x_in.rearrange("(t p) d -> p t d", p=128))

    def emit_store_x(self, y_out):
        p = self.p
        p.barrier("sync", ("V", "T", "A"))
        tok = p.dma("sync", y_out.rearrange("(t p) d -> p t d", p=128),
                    self.xres.rearrange("p (t d) -> p t d", d=1024))
        p.wait("sync", tok)

    def emit_ln(self, xres_tok=None, write_dram=None, ag16=None):
        p = self.p
        st = self.stats
        if xres_tok is not None:
            p.wait("vector", xres_tok)
            p.wait("scalar", xres_tok)
        self.dump("xres_in", self.xres[:, :])
        dumps = [self.sm[0][:, :], self.sm[1][:, :],
                 self.wbuf[:, 0:1024], self.wbuf[:, 1024:2048]]
        for t in range(NT):
            xs = self.xres[:, t * 1024:(t + 1) * 1024]
            p.step("vector", lambda e, xs=xs, t=t: e.tensor_reduce(
                st[:, t:t + 1], xs, AX.X, ALU.add))
            p.step("scalar", lambda e, xs=xs, t=t: e.activation(
                dumps[t % 4], xs, AF.Square, accum_out=st[:, 16 + t:17 + t]))
        p.wait("vector", p.tok("A"))
        p.step("vector", lambda e: e.tensor_scalar(st[:, 32:48], st[:, 0:16], 1.0 / 1024, None, ALU.mult))
        p.step("vector", lambda e: e.tensor_scalar(st[:, 48:64], st[:, 16:32], 1.0 / 1024, None, ALU.mult))
        p.step("vector", lambda e: e.tensor_tensor(st[:, 64:80], st[:, 32:48], st[:, 32:48], ALU.mult))
        p.step("vector", lambda e: e.tensor_tensor(st[:, 80:96], st[:, 48:64], st[:, 64:80], ALU.subtract))
        p.wait("scalar", p.tok("V"))
        p.step("scalar", lambda e: e.activation(st[:, 96:112], st[:, 80:96], AF.Sqrt, bias=self.fconst(1e-5)))
        p.wait("vector", p.tok("A"))
        p.step("vector", lambda e: e.reciprocal(st[:, 112:128], st[:, 96:112]))
        sm_war = [p.tok("A"), p.tok("A")]
        ag_tok = None
        for t in range(NT):
            sm = self.sm[t % 2]
            p.wait("vector", sm_war[t % 2])
            p.step("vector", lambda e, t=t, sm=sm: e.tensor_scalar(
                sm[:, :], self.xres[:, t * 1024:(t + 1) * 1024],
                st[:, 32 + t:33 + t], st[:, 112 + t:113 + t], ALU.subtract, ALU.mult))
            vt = self.vflush()
            if t == 0 and ag16 is not None:
                p.wait("gpsimd", vt)
                ag_tok = p.dma("gpsimd", ag16[:, :], sm[1:17, :])
            if write_dram is not None:
                p.wait("sync", vt)
                p.dma("sync", write_dram[t * 128:(t + 1) * 128, :], sm[:, :])
            p.wait("tensor", vt)
            if t == 0 and ag16 is not None:
                p.wait("tensor", ag_tok)
            last = None
            for db in range(ND):
                last = self.transpose(
                    sm[:, db * 128:(db + 1) * 128],
                    self.xn_fm[:, db * 2048 + t * 128: db * 2048 + (t + 1) * 128])
            sm_war[t % 2] = last
            if write_dram is not None:
                sm_war[t % 2] = None
        if write_dram is not None:
            p.wait_all("vector", "Ds")
        self.dump("stats_ln", self.stats[:, :])
        self.dump("xnfm", self.xn_fm[:, :])
        return ag_tok

    def emit_l0_attn_pre(self):
        p = self.p
        wq = self.din("L0_attn_qkv_w", [1024, 3072])
        bq = self.din("L0_attn_qkv_b", [3072])
        wp = self.din("L0_attn_proj_w", [1024, 1024])
        bp = self.din("L0_attn_proj_b", [1024])
        agi = self.dint("ag_l0kv_in", [64, 1024])
        ago = self.dint("ag_l0kv_out", [64, 1024])
        p.barrier("gpsimd", ("A", "V"))
        bq_tok = p.dma("gpsimd", self.rowb[0:1, 0:3072], bq.rearrange("f -> () f"))
        xn_tok = p.tok("A")

        k_tm = self.scra
        v_tm = self.scrb
        wslab = self.wbuf.rearrange("p (db f) -> p db f", f=512)

        def project_tm(fcs, dest):
            for fc in fcs:
                p.wait_all("gpsimd", "A")
                wtok = p.dma("gpsimd", wslab,
                             wq[:, fc * 512:(fc + 1) * 512].rearrange("(db p) f -> p db f", p=128))
                p.wait("tensor", wtok)
                p.wait("tensor", bq_tok)
                for tt in range(NT):
                    bi, bank = self.pool.acquire()
                    for db in range(ND):
                        p.emit("tensor", lambda e, db=db, tt=tt, bank=bank: e.matmul(
                            bank[:, 0:512],
                            self.xn_fm[:, db * 2048 + tt * 128:db * 2048 + (tt + 1) * 128],
                            wslab[:, db, :], start=(db == 0), stop=False))
                    p.step("tensor", lambda e, fc=fc, bank=bank: e.matmul(
                        bank[:, 0:512], self.c_ones_row[:, :],
                        self.rowb[0:1, fc * 512:(fc + 1) * 512], start=False, stop=True))
                    mtok = p.tok("T")
                    p.wait("scalar", mtok)
                    etok = p.step("scalar", lambda e, d=dest(fc, tt), bank=bank: e.copy(d, bank[:, 0:512]))
                    self.pool.release(bi, etok)

        def l2norm_tiles(buf):
            st = self.stats
            p.wait_all("scalar", "A")
            for t in range(NT):
                ks = buf[:, t * 1024:(t + 1) * 1024]
                sm = self.sm[t % 2]
                c0 = 128 + 64 * (t % 2)
                p.wait_all("scalar", "V")
                for h in range(16):
                    p.step("scalar", lambda e, ks=ks, sm=sm, h=h, c0=c0: e.activation(
                        sm[:, h * 64:(h + 1) * 64],
                        bass.AP(ks.tensor, ks.offset + h * 64, [list(ks.ap[0]), [1, 64]]),
                        AF.Square, accum_out=st[:, c0 + h:c0 + h + 1]))
                p.wait("scalar", p.tok("A"))
                p.step("scalar", lambda e, c0=c0: e.activation(
                    st[:, c0 + 16:c0 + 32], st[:, c0:c0 + 16], AF.Sqrt))
                p.wait("vector", p.tok("A"))
                p.step("vector", lambda e, c0=c0: e.tensor_scalar(
                    st[:, c0 + 16:c0 + 32], st[:, c0 + 16:c0 + 32], 1e-12, None, ALU.max))
                p.step("vector", lambda e, c0=c0: e.reciprocal(
                    st[:, c0 + 32:c0 + 48], st[:, c0 + 16:c0 + 32]))
                p.step("vector", lambda e, ks=ks, sm=sm, c0=c0: e.tensor_tensor(
                    sm.rearrange("p (h d) -> p h d", d=64),
                    ks.rearrange("p (h d) -> p h d", d=64),
                    bcast_free(st[:, c0 + 32:c0 + 48], 64), ALU.mult))
                p.step("vector", lambda e, ks=ks, sm=sm: e.tensor_copy(ks, sm[:, :]))
            self.vflush()

        p.wait("tensor", xn_tok)
        project_tm([2, 3], lambda fc, tt: k_tm[:, tt * 1024 + (fc - 2) * 512: tt * 1024 + (fc - 1) * 512])
        project_tm([4, 5], lambda fc, tt: v_tm[:, tt * 1024 + (fc - 4) * 512: tt * 1024 + (fc - 3) * 512])
        l2norm_tiles(k_tm)
        self.dump("stats_l2", self.stats[:, :])
        self.dump("ktm", k_tm[:, :])
        self.dump("vtm", v_tm[:, :])

        p.wait_all("tensor", "V")
        for h in range(16):
            bi, bank = self.pool.acquire()
            for tt in range(NT):
                p.emit("tensor", lambda e, h=h, tt=tt, bank=bank: e.matmul(
                    bank[0:64, 0:64],
                    k_tm[:, tt * 1024 + h * 64: tt * 1024 + (h + 1) * 64],
                    v_tm[:, tt * 1024 + h * 64: tt * 1024 + (h + 1) * 64],
                    start=(tt == 0), stop=(tt == 15)),
                    inc="T" if tt == 15 else None)
            mtok = p.tok("T")
            p.wait("vector", mtok)
            etok = p.step("vector", lambda e, h=h, bank=bank: e.tensor_copy(
                self.sst_f[0:64, h * 64:(h + 1) * 64], bank[0:64, 0:64]))
            self.pool.release(bi, etok)
        fill = None
        p.wait("gpsimd", self.vflush())
        fill = p.dma("gpsimd", agi[:, :], self.sst_f[0:64, 0:1024])
        ag_tok = self.ag_go(agi, ago, fill, kind="AllReduce")

        project_tm([0, 1], lambda fc, tt: k_tm[:, tt * 1024 + fc * 512: tt * 1024 + (fc + 1) * 512])
        l2norm_tiles(k_tm)
        p.wait("tensor", p.tok("V"))
        for t in range(NT):
            for pbk in range(8):
                self.transpose(k_tm[:, t * 1024 + pbk * 128: t * 1024 + (pbk + 1) * 128],
                               self.scrb[:, pbk * 2048 + t * 128: pbk * 2048 + (t + 1) * 128])
        self.dump("qfm", self.scrb[:, :])
        p.barrier("gpsimd", ("T",))
        bp_tok = p.dma("gpsimd", self.rowb[0:1, 4096:5120], bp.rearrange("f -> () f"))
        return dict(ag=ag_tok, ago=ago, bp=bp_tok, wp=wp)

    def emit_l0_attn_post(self, ctx):
        p = self.p
        ago = ctx["ago"]
        p.wait("gpsimd", ctx["ag"])
        p.wait_all("gpsimd", "V")
        tok = p.dma("gpsimd", self.sst_f[0:64, 0:1024], ago[0:64, :])
        p.wait("sync", tok)
        self.dump("kvsum", self.sst_f[0:64, 0:1024])
        p.wait("vector", tok)
        p.step("vector", lambda e: e.memset(self.sst_b[:, 0:1024], 0.0))
        vw = lambda buf, p0, p1, sub: buf[p0:p1, 0:1024].rearrange(
            "p (h two c) -> p h two c", two=2, c=64)[:, :, sub, :]
        p.step("vector", lambda e: e.tensor_copy(
            vw(self.sst_b, 0, 64, 0), vw(self.sst_f, 0, 64, 0)))
        vtok = self.vflush()
        p.wait("gpsimd", vtok)
        kv2_tok = p.dma("gpsimd", vw(self.sst_b, 64, 128, 1), vw(self.sst_f, 0, 64, 1))
        p.wait("tensor", vtok)
        p.wait("tensor", kv2_tok)
        ao = self.scra
        for pr in range(8):
            for tc in range(4):
                bi, bank = self.pool.acquire()
                p.step("tensor", lambda e, pr=pr, tc=tc, bank=bank: e.matmul(
                    bank[:, 0:512], self.sst_b[:, pr * 128:(pr + 1) * 128],
                    self.scrb[:, pr * 2048 + tc * 512: pr * 2048 + (tc + 1) * 512],
                    start=True, stop=True))
                mtok = p.tok("T")
                p.wait("scalar", mtok)
                etok = p.step("scalar", lambda e, pr=pr, tc=tc, bank=bank: e.copy(
                    ao[:, pr * 2048 + tc * 512: pr * 2048 + (tc + 1) * 512], bank[:, 0:512]))
                self.pool.release(bi, etok)
        ao_tok = p.tok("A")
        wslab = self.wbuf.rearrange("p (db f) -> p db f", f=512)
        wp = ctx["wp"]
        p.wait("tensor", ao_tok)
        for oc in range(2):
            p.wait_all("gpsimd", "T")
            wtok = p.dma("gpsimd", wslab,
                         wp[:, oc * 512:(oc + 1) * 512].rearrange("(db p) f -> p db f", p=128))
            p.wait("tensor", wtok)
            p.wait("tensor", ctx["bp"])
            for tt in range(NT):
                bi, bank = self.pool.acquire()
                for cb in range(8):
                    p.emit("tensor", lambda e, cb=cb, tt=tt, bank=bank: e.matmul(
                        bank[:, 0:512], ao[:, cb * 2048 + tt * 128: cb * 2048 + (tt + 1) * 128],
                        wslab[:, cb, :], start=(cb == 0), stop=False))
                p.step("tensor", lambda e, oc=oc, bank=bank: e.matmul(
                    bank[:, 0:512], self.c_ones_row[:, :],
                    self.rowb[0:1, 4096 + oc * 512: 4096 + (oc + 1) * 512], start=False, stop=True))
                mtok = p.tok("T")
                p.wait("vector", mtok)
                dst = self.xres[:, tt * 1024 + oc * 512: tt * 1024 + (oc + 1) * 512]
                etok = p.step("vector", lambda e, dst=dst, bank=bank: e.tensor_tensor(
                    dst, dst, bank[:, 0:512], ALU.add))
                self.pool.release(bi, etok)

    def emit_l0_ffn(self, ago16, ag16_tok):
        p = self.p
        w1 = self.din("L0_ffn_W1", [4096, 1024])
        b1 = self.din("L0_ffn_b1", [4096])
        w2 = self.din("L0_ffn_W2", [1024, 4096])
        b2 = self.din("L0_ffn_b2", [1024])
        scale = self.din("L0_ffn_scale", [1])
        cdram = self.dint("l0c", [4, 4, 2048], BF16)
        hbuf = self.scra
        scrb = self.scrb
        wb = self.wbuf

        def scol(ap2d, start, step, count):
            return bass.AP(ap2d.tensor, ap2d.offset + start, [list(ap2d.ap[0]), [step, count]])

        sct = p.dma("sync", self.stats[0:1, 513:514], scale.rearrange("a -> () a"))
        p.wait("tensor", sct)
        bi, bank = self.pool.acquire()
        p.step("tensor", lambda e, bank=bank: e.matmul(
            bank[:, 0:1], self.c_ones_rowf[:, :], self.stats[0:1, 513:514], start=True, stop=True))
        p.wait("vector", p.tok("T"))
        et = p.step("vector", lambda e, bank=bank: e.tensor_copy(self.stats[:, 512:513], bank[:, 0:1]))
        self.pool.release(bi, et)

        p.barrier("gpsimd", ("T", "V", "A"))
        bt = p.dma("gpsimd", self.sm[0][0:32, 0:128], b1.rearrange("(a b) -> a b", b=128))
        p.wait("tensor", bt)
        self.transpose(self.sm[0][0:32, 0:128], evac="vector",
                       evac_fn=lambda pt: p.step("vector", lambda e, pt=pt: e.tensor_copy(
                           self.stats[:, 544:576], pt)))

        p.wait("gpsimd", ag16_tok)
        p.wait("sync", ag16_tok)
        ev1_tok = p.dma("gpsimd", self.rowb[0:1, 0:4096].rearrange("o (r f) -> o r f", r=4),
                        ago16.rearrange("(r two) f -> r two f", two=2)[0:4, 0, :])
        b2_tok = p.dma("gpsimd", self.rowb[0:1, 4096:5120], b2.rearrange("f -> () f"))
        p.barrier("sync", ("V", "T", "A"))
        x16_tok = p.dma("sync", wb[0:16, 2048:3072], ago16[0:16, :])
        p.wait("tensor", x16_tok)
        for db in range(ND):
            self.transpose(wb[0:16, 2048 + db * 128: 2048 + (db + 1) * 128],
                           wb[:, 3072 + db * 16: 3072 + (db + 1) * 16])
        xf_tok = p.tok("A")
        p.wait("sync", xf_tok)
        for i in range(4):
            st_tok = p.dma("sync", self.sm[1][0:128, 0:8].rearrange("p (h j) -> p h j", h=2),
                           ago16[8 + 2 * i, :].rearrange("(h p j) -> p h j", h=2, p=128))
            p.wait("tensor", st_tok)
            for h in range(2):
                self.transpose(self.sm[1][0:128, h * 4:(h + 1) * 4],
                               wb[0:4, 2048 + i * 256 + h * 128: 2048 + i * 256 + (h + 1) * 128])
            p.wait("sync", p.tok("T"))
        e2_tok = p.tok("A")

        p.wait("tensor", xf_tok)
        for tc in range(4):
            bi, bank = self.pool.acquire()
            for db in range(ND):
                lhs = scol(wb[:, 3072 + db * 16: 3072 + (db + 1) * 16], 1, 2, 4)
                p.emit("tensor", lambda e, lhs=lhs, db=db, tc=tc, bank=bank: e.matmul(
                    bank[0:4, 0:512], lhs,
                    self.xn_fm[:, db * 2048 + tc * 512: db * 2048 + (tc + 1) * 512],
                    start=(db == 0), stop=(db == 7)), inc="T" if db == 7 else None)
            mtok = p.tok("T")
            p.wait("vector", mtok)
            etok = p.step("vector", lambda e, tc=tc, bank=bank: e.tensor_scalar(
                wb[0:4, tc * 512:(tc + 1) * 512], bank[0:4, 0:512],
                self.stats[0:4, 512:513], None, ALU.mult))
            self.pool.release(bi, etok)
        p.wait("tensor", ev1_tok)
        p.wait("tensor", self.vflush())

        for j in range(4):
            if j >= 1:
                p.wait("gpsimd", p.tok("T"))
                at = p.dma("gpsimd", wb[0:1, 0:2048], wb[j:j + 1, 0:2048])
                p.wait("tensor", at)
            for hb in range(8):
                pbg = j * 8 + hb
                sb = hb % 2
                stage = scrb[:, 2048 + sb * 1024: 2048 + (sb + 1) * 1024]
                w1t = scrb[:, sb * 1024: (sb + 1) * 1024]
                p.barrier("gpsimd", ("T",))
                stok = p.dma("gpsimd", stage, w1[pbg * 128:(pbg + 1) * 128, :])
                p.wait("tensor", stok)
                for db in range(ND):
                    self.transpose(stage[:, db * 128:(db + 1) * 128],
                                   w1t[:, db * 128:(db + 1) * 128])
                p.wait("tensor", p.tok("A"))
                for tc in range(4):
                    bi, bank = self.pool.acquire()
                    for db in range(ND):
                        p.emit("tensor", lambda e, db=db, tc=tc, bank=bank, w1t=w1t: e.matmul(
                            bank[:, 0:512], w1t[:, db * 128:(db + 1) * 128],
                            self.xn_fm[:, db * 2048 + tc * 512: db * 2048 + (tc + 1) * 512],
                            start=(db == 0), stop=False))
                    lora_lhs = self.rowb[0:1, j * 1024 + hb * 128: j * 1024 + (hb + 1) * 128]
                    p.step("tensor", lambda e, lora_lhs=lora_lhs, tc=tc, bank=bank: e.matmul(
                        bank[:, 0:512], lora_lhs, wb[0:1, tc * 512:(tc + 1) * 512],
                        start=False, stop=True))
                    mtok = p.tok("T")
                    p.wait("scalar", mtok)
                    etok = p.step("scalar", lambda e, hb=hb, tc=tc, bank=bank, pbg=pbg: e.activation(
                        hbuf[:, hb * 2048 + tc * 512: hb * 2048 + (tc + 1) * 512],
                        bank[:, 0:512], AF.Gelu, bias=self.stats[:, 544 + pbg:545 + pbg]))
                    self.pool.release(bi, etok)
            h_tok = p.tok("A")
            if j == 0:
                self.dump("h0", hbuf[:, :])
                self.dump("aT", wb[0:4, 0:2048])
                self.dump("x16fm", wb[:, 3072:3200])
                self.dump("evens", self.rowb[0:1, 0:5120])

            p.wait("tensor", h_tok)
            p.wait("tensor", e2_tok)
            for tc in range(4):
                bi, bank = self.pool.acquire()
                for hb in range(ND):
                    lhs = scol(wb[:, 3072 + hb * 16: 3072 + (hb + 1) * 16], 9, 2, 4)
                    p.emit("tensor", lambda e, lhs=lhs, hb=hb, tc=tc, bank=bank: e.matmul(
                        bank[0:4, 0:512], lhs,
                        hbuf[:, hb * 2048 + tc * 512: hb * 2048 + (tc + 1) * 512],
                        start=(hb == 0), stop=(hb == 7)), inc="T" if hb == 7 else None)
                mtok = p.tok("T")
                p.wait("vector", mtok)
                etok = p.step("vector", lambda e, tc=tc, bank=bank: e.tensor_scalar(
                    scrb[0:4, 6144 + tc * 512: 6144 + (tc + 1) * 512], bank[0:4, 0:512],
                    self.stats[0:4, 512:513], None, ALU.mult))
                self.pool.release(bi, etok)
            p.wait("sync", self.vflush())
            c_tok = p.dma("sync", cdram[j, :, :], scrb[0:4, 6144:8192])

            for ob in range(8):
                sb = ob % 2
                stage = scrb[:, 2048 + sb * 1024: 2048 + (sb + 1) * 1024]
                p.barrier("gpsimd", ("T",))
                stok = p.dma("gpsimd", stage, w2[ob * 128:(ob + 1) * 128, j * 1024:(j + 1) * 1024])
                p.wait("tensor", stok)
                for hb in range(8):
                    self.transpose(stage[:, hb * 128:(hb + 1) * 128],
                                   scrb[:, 8192 + hb * 1024 + ob * 128: 8192 + hb * 1024 + (ob + 1) * 128])
            w2_tok = p.tok("A")

            cslot = [scrb[0:4, 4096 + 0:4096 + 2048], scrb[0:4, 0 + 0:0 + 2048]]
            if j == 3:
                p.wait("sync", c_tok)
                p.barrier("sync", ("T",))
                cs_tok = {}
                for i in range(2):
                    cs_tok[i] = p.dma("sync", cslot[i][:, :], cdram[:, i, :])

            p.wait("tensor", w2_tok)
            p.wait("tensor", b2_tok)
            for oc in range(2):
                if j == 3 and oc == 1:
                    for i in range(2):
                        p.wait("sync", p.tok("T"))
                        cs_tok[i] = p.dma("sync", cslot[i][:, :], cdram[:, 2 + i, :])
                for tt in range(NT):
                    bi, bank = self.pool.acquire()
                    mms = []
                    for hb in range(8):
                        mms.append((bank[:, 0:512],
                                    hbuf[:, hb * 2048 + tt * 128: hb * 2048 + (tt + 1) * 128],
                                    scrb[:, 8192 + hb * 1024 + oc * 512: 8192 + hb * 1024 + (oc + 1) * 512]))
                    if j == 0:
                        mms.append((bank[:, 0:512], self.c_ones_row[:, :],
                                    self.rowb[0:1, 4096 + oc * 512: 4096 + (oc + 1) * 512]))
                    if j == 3:
                        for isub in range(2):
                            i = 2 * oc + isub
                            p.wait("tensor", cs_tok[isub])
                            mms.append((bank[:, isub * 256:(isub + 1) * 256],
                                        cslot[isub][:, tt * 128:(tt + 1) * 128],
                                        wb[0:4, 2048 + i * 256: 2048 + (i + 1) * 256]))
                    for mi, (o_, l_, r_) in enumerate(mms):
                        p.emit("tensor", lambda e, o_=o_, l_=l_, r_=r_, mi=mi, n=len(mms): e.matmul(
                            o_, l_, r_, start=(mi == 0), stop=(mi == n - 1)),
                            inc="T" if mi == len(mms) - 1 else None)
                    mtok = p.tok("T")
                    p.wait("vector", mtok)
                    dst = self.xres[:, tt * 1024 + oc * 512: tt * 1024 + (oc + 1) * 512]
                    etok = p.step("vector", lambda e, dst=dst, bank=bank: e.tensor_tensor(
                        dst, dst, bank[:, 0:512], ALU.add))
                    self.pool.release(bi, etok)

    def emit_l1(self):
        p = self.p
        wqkv = self.din("L1_attn_W_qkv", [3072, 1024])
        wout = self.din("L1_attn_W_out", [1024, 1024])
        q_dram = self.dint("l1q", [1024, 2048], BF16)
        agi = self.dint("ag_l1s_in", [64, 1040])
        ago = self.dint("ag_l1s_out", [256, 1040])
        rmask = self.din("c_rankmask", [128, 4])
        rm_tok = p.dma("sync", self.stats[:, 700:704], rmask[:, :])

        k_tm = self.scra
        v_tm = self.scrb
        wb = self.wbuf
        st = self.stats
        LN_S = self.fconst(math.log(0.125))

        def build_wT(wsrc, f0, nfb, dst, stage):
            for fb in range(nfb):
                p.barrier("gpsimd", ("T", "A", "V"))
                stok = p.dma("gpsimd", stage, wsrc[f0 + fb * 128:f0 + (fb + 1) * 128, :])
                p.wait("tensor", stok)
                for db in range(ND):
                    self.transpose(stage[:, db * 128:(db + 1) * 128],
                                   dst[:, db * (nfb * 128) + fb * 128: db * (nfb * 128) + (fb + 1) * 128])

        def elu_epi(bank, dest, scale_ln, do_elu=True):
            mtok = p.tok("T")
            p.wait("vector", mtok)
            if not do_elu:
                return p.step("vector", lambda e: e.tensor_copy(dest, bank[:, 0:512]))
            p.step("vector", lambda e: e.tensor_scalar(self.sm[1][:, 0:512], bank[:, 0:512], 0.0, None, ALU.min))
            p.step("scalar", lambda e: e.activation(self.och[:, 0:512], self.sm[1][:, 0:512], AF.Exp,
                                                    bias=scale_ln))
            p.step("vector", lambda e: e.tensor_scalar(self.sm[1][:, 512:1024], bank[:, 0:512],
                                                       math.exp(scale_ln), 0.0, ALU.mult, ALU.max))
            return p.step("vector", lambda e: e.tensor_tensor(dest, self.och[:, 0:512],
                                                              self.sm[1][:, 512:1024], ALU.add))

        xn_tok = p.tok("A")
        p.wait("tensor", xn_tok)
        for fb in range(8):
            build_wT(wqkv, fb * 128, 1, wb[:, 0:1024], wb[:, 1024:2048])
            p.wait("tensor", p.tok("A"))
            for tc in range(4):
                bi, bank = self.pool.acquire()
                for db in range(ND):
                    p.emit("tensor", lambda e, db=db, tc=tc, bank=bank: e.matmul(
                        bank[:, 0:512], wb[:, db * 128:(db + 1) * 128],
                        self.xn_fm[:, db * 2048 + tc * 512: db * 2048 + (tc + 1) * 512],
                        start=(db == 0), stop=(db == 7)), inc="T" if db == 7 else None)
                etok = elu_epi(bank, v_tm[:, fb * 2048 + tc * 512: fb * 2048 + (tc + 1) * 512], LN_S)
                self.pool.release(bi, etok)
        p.wait("sync", p.tok("V"))
        qd_tok = p.dma("sync", q_dram.rearrange("(pb p) t -> p pb t", p=128),
                       v_tm.rearrange("p (pb t) -> p pb t", t=2048))

        for part, dst, do_elu in ((1, k_tm, True), (2, v_tm, True)):
            for fc in range(2):
                f0 = part * 1024 + fc * 512
                if part == 2 and fc == 0:
                    p.wait("gpsimd", qd_tok)
                build_wT(wqkv, f0, 4, wb[:, 0:4096], self.och[:, 0:1024])
                p.wait("tensor", p.tok("A"))
                for tt in range(NT):
                    bi, bank = self.pool.acquire()
                    for db in range(ND):
                        p.emit("tensor", lambda e, db=db, tt=tt, bank=bank: e.matmul(
                            bank[:, 0:512],
                            self.xn_fm[:, db * 2048 + tt * 128:db * 2048 + (tt + 1) * 128],
                            wb[:, db * 512:(db + 1) * 512], start=(db == 0), stop=(db == 7)),
                            inc="T" if db == 7 else None)
                    etok = elu_epi(bank, dst[:, tt * 1024 + fc * 512: tt * 1024 + (fc + 1) * 512],
                                   0.0, do_elu=(part == 1))
                    self.pool.release(bi, etok)

        self.vflush()
        p.wait_all("tensor", "V")
        for h in range(16):
            bi, bank = self.pool.acquire()
            for tt in range(NT):
                p.emit("tensor", lambda e, h=h, tt=tt, bank=bank: e.matmul(
                    bank[0:64, 0:64],
                    k_tm[:, tt * 1024 + h * 64: tt * 1024 + (h + 1) * 64],
                    v_tm[:, tt * 1024 + h * 64: tt * 1024 + (h + 1) * 64],
                    start=(tt == 0), stop=False))
                p.emit("tensor", lambda e, h=h, tt=tt, bank=bank: e.matmul(
                    bank[0:64, 64:65],
                    k_tm[:, tt * 1024 + h * 64: tt * 1024 + (h + 1) * 64],
                    self.c_ones_col[:, :], start=False, stop=(tt == 15)),
                    inc="T" if tt == 15 else None)
            mtok = p.tok("T")
            p.wait("vector", mtok)
            etok = p.step("vector", lambda e, h=h, bank=bank: e.tensor_copy(
                self.sst_f[(h % 2) * 64:(h % 2) * 64 + 64, 520 + (h // 2) * 65: 520 + (h // 2) * 65 + 65],
                bank[0:64, 0:65]))
            self.pool.release(bi, etok)
        fl = self.vflush()
        p.wait("gpsimd", fl)
        agi3 = agi.rearrange("d (hb two c) -> d hb two c", two=2, c=65)
        f2a = p.dma("gpsimd", agi3[:, :, 0, :], self.sst_f[0:64, 520:1040].rearrange("d (hb c) -> d hb c", c=65))
        p.wait("gpsimd", f2a)
        f2 = p.dma("gpsimd", agi3[:, :, 1, :], self.sst_f[64:128, 520:1040].rearrange("d (hb c) -> d hb c", c=65))
        ag_tok = self.ag_go(agi, ago, f2)

        p.wait("vector", rm_tok)
        p.step("vector", lambda e: e.memset(self.sst_f[:, 0:520], 0.0))
        for r in range(4):
            p.wait("gpsimd", ag_tok)
            p.wait_all("gpsimd", "V")
            g1 = p.dma("gpsimd", self.sst_f[0:64, 520:1040].rearrange("d (hb c) -> d hb c", c=65),
                       ago.rearrange("(r d) (hb two c) -> r d hb two c", r=4, two=2, c=65)[r, :, :, 0, :])
            g2 = p.dma("gpsimd", self.sst_f[64:128, 520:1040].rearrange("d (hb c) -> d hb c", c=65),
                       ago.rearrange("(r d) (hb two c) -> r d hb two c", r=4, two=2, c=65)[r, :, :, 1, :])
            p.wait("vector", g2)
            p.step("vector", lambda e, r=r: e.scalar_tensor_tensor(
                self.sst_f[:, 0:520], self.sst_f[:, 520:1040], st[:, 700 + r:701 + r],
                self.sst_f[:, 0:520], ALU.mult, ALU.add))
        p.step("vector", lambda e: e.tensor_copy(self.sst_b[:, 0:520], self.sst_f[:, 0:520]))
        self.vflush()

        och_dram = self.dint("l1o", [2048, 1024], BF16)
        for c in range(8):
            p.barrier("sync", ("T",))
            qtok = p.dma("sync", wb[:, 0:2048].rearrange("p (pb t) -> p pb t", t=256),
                         q_dram.rearrange("(pb p) t -> p pb t", p=128)[:, :, c * 256:(c + 1) * 256])
            p.wait("tensor", qtok)
            for h in range(16):
                b0 = (h % 2) * 64
                hb = h // 2
                for mt in range(2):
                    tt = 2 * c + mt
                    self.transpose(k_tm[:, tt * 1024 + h * 64: tt * 1024 + (h + 1) * 64],
                                   self.sm[0][b0:b0 + 64, mt * 128:(mt + 1) * 128], out_base=b0)
                p.wait("tensor", p.tok("A"))
                bi_s, bank_s = self.pool.acquire()
                for mt in range(2):
                    p.emit("tensor", lambda e, mt=mt, b0=b0, hb=hb, bank_s=bank_s: e.matmul(
                        bank_s[0:128, mt * 256:(mt + 1) * 256],
                        self.sm[0][b0:b0 + 64, mt * 128:(mt + 1) * 128],
                        wb[b0:b0 + 64, hb * 256:(hb + 1) * 256],
                        start=(mt == 0), stop=True), inc="T" if mt == 1 else None)
                mtok = p.tok("T")
                p.wait("vector", mtok)
                p.step("vector", lambda e, bank_s=bank_s: e.tensor_tensor(
                    self.sm[1][:, 0:512], bank_s[:, 0:512], self.c_l1mask[:, :], ALU.mult))
                sv = self.vflush()
                self.pool.release(bi_s, sv)
                p.wait("tensor", sv)
                bi_n, bank_n = self.pool.acquire()
                nm = []
                for lt2 in range(2):
                    for mt in range(2):
                        tt = 2 * c + mt
                        nm.append((bank_n[:, lt2 * 65:lt2 * 65 + 64],
                                   self.sm[1][0:128, mt * 256 + lt2 * 128: mt * 256 + (lt2 + 1) * 128],
                                   v_tm[:, tt * 1024 + h * 64: tt * 1024 + (h + 1) * 64]))
                        nm.append((bank_n[:, lt2 * 65 + 64:lt2 * 65 + 65],
                                   self.sm[1][0:128, mt * 256 + lt2 * 128: mt * 256 + (lt2 + 1) * 128],
                                   self.c_ones_col[:, :]))
                for mi, (o_, l_, r_) in enumerate(nm):
                    p.emit("tensor", lambda e, o_=o_, l_=l_, r_=r_, mi=mi: e.matmul(
                        o_, l_, r_, start=(mi == 0), stop=False))
                for lt2 in range(2):
                    p.emit("tensor", lambda e, lt2=lt2, b0=b0, hb=hb, bank_n=bank_n: e.matmul(
                        bank_n[:, lt2 * 65:(lt2 + 1) * 65],
                        wb[b0:b0 + 64, hb * 256 + lt2 * 128: hb * 256 + (lt2 + 1) * 128],
                        self.sst_b[b0:b0 + 64, hb * 65:(hb + 1) * 65],
                        start=False, stop=(lt2 == 1)), inc="T" if lt2 == 1 else None)
                ntok = p.tok("T")
                p.wait("vector", ntok)
                for lt2 in range(2):
                    p.step("vector", lambda e, lt2=lt2, bank_n=bank_n: e.tensor_scalar(
                        st[:, 704 + lt2:705 + lt2], bank_n[:, lt2 * 65 + 64:lt2 * 65 + 65],
                        1e-6, None, ALU.max))
                    p.step("vector", lambda e, lt2=lt2: e.reciprocal(
                        st[:, 706 + lt2:707 + lt2], st[:, 704 + lt2:705 + lt2]))
                    dtok = p.step("vector", lambda e, lt2=lt2, h=h, bank_n=bank_n: e.tensor_scalar(
                        self.och[:, lt2 * 1024 + h * 64: lt2 * 1024 + (h + 1) * 64],
                        bank_n[:, lt2 * 65:lt2 * 65 + 64],
                        st[:, 706 + lt2:707 + lt2], None, ALU.mult))
                self.pool.release(bi_n, p.tok("V"))
                bi_u, bank_u = self.pool.acquire()
                for mt in range(2):
                    tt = 2 * c + mt
                    p.emit("tensor", lambda e, mt=mt, tt=tt, h=h, bank_u=bank_u: e.matmul(
                        bank_u[0:64, 0:64],
                        k_tm[:, tt * 1024 + h * 64: tt * 1024 + (h + 1) * 64],
                        v_tm[:, tt * 1024 + h * 64: tt * 1024 + (h + 1) * 64],
                        start=(mt == 0), stop=False))
                    p.emit("tensor", lambda e, mt=mt, tt=tt, h=h, bank_u=bank_u: e.matmul(
                        bank_u[0:64, 64:65],
                        k_tm[:, tt * 1024 + h * 64: tt * 1024 + (h + 1) * 64],
                        self.c_ones_col[:, :], start=False, stop=(mt == 1)),
                        inc="T" if mt == 1 else None)
                utok = p.tok("T")
                p.wait("vector", utok)
                p.step("vector", lambda e, b0=b0, hb=hb, bank_u=bank_u: e.tensor_tensor(
                    self.sst_f[b0:b0 + 64, hb * 65:(hb + 1) * 65],
                    self.sst_f[b0:b0 + 64, hb * 65:(hb + 1) * 65],
                    bank_u[0:64, 0:65], ALU.add))
                ctok = p.step("vector", lambda e, b0=b0, hb=hb: e.tensor_copy(
                    self.sst_b[b0:b0 + 64, hb * 65:(hb + 1) * 65],
                    self.sst_f[b0:b0 + 64, hb * 65:(hb + 1) * 65]))
                self.pool.release(bi_u, ctok)
            fl2 = self.vflush()
            p.wait("sync", fl2)
            p.dma("sync", och_dram[c * 256:(c + 1) * 256, :].rearrange("(lt p) o -> p lt o", p=128),
                  self.och[:, 0:2048].rearrange("p (lt o) -> p lt o", o=1024))
        for ob in range(8):
            build_wT(wout, ob * 128, 1, k_tm[:, ob * 1024: ob * 1024 + 1024], self.och[:, 0:1024])
        woutT = k_tm.rearrange("p (ob eb j) -> p ob eb j", eb=8, j=128)
        for tt in range(NT):
            p.barrier("sync", ("T",))
            ltok = p.dma("sync", self.och[:, 0:1024],
                         och_dram[tt * 128:(tt + 1) * 128, :])
            p.wait("tensor", ltok)
            for eb in range(8):
                self.transpose(self.och[:, eb * 128:(eb + 1) * 128],
                               self.och[:, 1024 + eb * 128: 1024 + (eb + 1) * 128])
            p.wait("tensor", p.tok("A"))
            for oc in range(2):
                bi, bank = self.pool.acquire()
                for eb in range(8):
                    p.emit("tensor", lambda e, eb=eb, oc=oc, tt=tt, bank=bank: e.matmul(
                        bank[:, 0:512], self.och[:, 1024 + eb * 128: 1024 + (eb + 1) * 128],
                        woutT[:, oc * 4:(oc + 1) * 4, eb, :],
                        start=(eb == 0), stop=(eb == 7)), inc="T" if eb == 7 else None)
                mtok = p.tok("T")
                p.wait("vector", mtok)
                dst = self.xres[:, tt * 1024 + oc * 512: tt * 1024 + (oc + 1) * 512]
                etok = p.step("vector", lambda e, dst=dst, bank=bank: e.tensor_tensor(
                    dst, dst, bank[:, 0:512], ALU.add))
                self.pool.release(bi, etok)


    def emit_l1_ffn(self):
        p = self.p
        rtr_w = self.din("L1_ffn_router_w", [8, 1024])
        w1 = self.din("L1_ffn_w1", [8, 1024, 4096])
        w2 = self.din("L1_ffn_w2", [8, 4096, 1024])
        agci = self.dint("ag_l1c_in", [1, 16])
        agco = self.dint("ag_l1c_out", [8, 16])
        rmask8 = self.din("c_rankmask8", [128, 8])
        rm8_tok = p.dma("sync", self.stats[:, 708:716], rmask8[:, :])
        st = self.stats
        wb = self.wbuf
        C_CAP = 5120.0

        p.barrier("gpsimd", ("T", "A", "V"))
        stok = p.dma("gpsimd", self.sm[0][0:8, 0:1024], rtr_w[:, :])
        p.wait("tensor", stok)
        for db in range(ND):
            self.transpose(self.sm[0][0:8, db * 128:(db + 1) * 128], wb[:, db * 8:(db + 1) * 8])
        p.step("vector", lambda e: e.tensor_scalar(st[:, 640:648], self.c_iota8[:, :], 1e-6, None, ALU.mult))
        p.step("vector", lambda e: e.memset(st[0:1, 648:664], 0.0))
        p.wait("tensor", p.tok("A"))
        p.wait("tensor", p.tok("V"))

        for tt in range(NT):
            bi, bank = self.pool.acquire()
            for db in range(ND):
                p.emit("tensor", lambda e, db=db, tt=tt, bank=bank: e.matmul(
                    bank[:, 0:8], self.xn_fm[:, db * 2048 + tt * 128:db * 2048 + (tt + 1) * 128],
                    wb[:, db * 8:(db + 1) * 8], start=(db == 0), stop=(db == 7)),
                    inc="T" if db == 7 else None)
            mtok = p.tok("T")
            p.wait("vector", mtok)
            p.step("vector", lambda e, bank=bank: e.tensor_tensor(st[:, 664:672], bank[:, 0:8], st[:, 640:648], ALU.subtract))
            p.step("vector", lambda e: e.reduce_max(st[:, 672:673], st[:, 664:672], AX.X))
            p.step("vector", lambda e, tt=tt: e.tensor_scalar(
                self.och[:, tt * 8:(tt + 1) * 8], st[:, 664:672], st[:, 672:673], None, ALU.is_equal))
            p.step("vector", lambda e, tt=tt: e.scalar_tensor_tensor(
                st[:, 680:688], self.och[:, tt * 8:(tt + 1) * 8], -1e30, st[:, 664:672], ALU.mult, ALU.add))
            p.step("vector", lambda e: e.reduce_max(st[:, 673:674], st[:, 680:688], AX.X))
            p.step("vector", lambda e, tt=tt: e.tensor_scalar(
                self.och[:, 128 + tt * 8:128 + (tt + 1) * 8], st[:, 680:688], st[:, 673:674], None, ALU.is_equal))
            p.step("vector", lambda e: e.tensor_tensor(st[:, 674:675], st[:, 672:673], st[:, 673:674], ALU.subtract))
            p.wait("scalar", p.tok("V"))
            p.step("scalar", lambda e, tt=tt: e.activation(st[:, 448 + tt:449 + tt], st[:, 674:675], AF.Sigmoid))
            p.wait("vector", p.tok("A"))
            p.step("vector", lambda e, tt=tt: e.tensor_scalar(
                st[:, 464 + tt:465 + tt], st[:, 448 + tt:449 + tt], -1.0, 1.0, ALU.mult, ALU.add))
            for k in range(2):
                ohap = self.och[:, k * 128 + tt * 8: k * 128 + (tt + 1) * 8]
                p.wait("vector", mtok)
                bi2, bank2 = self.pool.acquire()
                p.wait("tensor", p.tok("V"))
                p.emit("tensor", lambda e, ohap=ohap, bank2=bank2: e.matmul(
                    bank2[:, 0:8], self.c_tril[:, :], ohap, start=True, stop=False))
                p.emit("tensor", lambda e, k=k, bank2=bank2: e.matmul(
                    bank2[:, 0:8], self.c_ones_rowf[:, :], st[0:1, 648 + k * 8:656 + k * 8],
                    start=False, stop=False))
                p.emit("tensor", lambda e, ohap=ohap, bank2=bank2: e.matmul(
                    bank2[0:1, 8:16], self.c_ones_col[:, :], ohap, start=False, stop=True),
                    inc="T")
                m2tok = p.tok("T")
                p.wait("vector", m2tok)
                p.step("vector", lambda e, ohap=ohap, bank2=bank2: e.tensor_tensor(
                    st[:, 680:688], bank2[:, 0:8], ohap, ALU.mult))
                p.step("vector", lambda e: e.reduce_sum(st[:, 688:689], st[:, 680:688], AX.X))
                p.step("vector", lambda e, k=k, tt=tt: e.tensor_scalar(
                    st[:, 480 + k * 16 + tt:481 + k * 16 + tt], st[:, 688:689], -1.0, None, ALU.add))
                etok = p.step("vector", lambda e, k=k, bank2=bank2: e.tensor_tensor(
                    st[0:1, 648 + k * 8:656 + k * 8], st[0:1, 648 + k * 8:656 + k * 8],
                    bank2[0:1, 8:16], ALU.add))
                self.pool.release(bi2, etok)
            self.pool.release(bi, p.tok("V"))
        fl = self.vflush()
        p.wait("gpsimd", fl)
        cf = p.dma("gpsimd", agci[:, :], st[0:1, 648:664])
        agc_tok = self.ag_go(agci, agco, cf, groups=[[0, 1, 2, 3, 4, 5, 6, 7]])
        p.wait("gpsimd", agc_tok)
        p.wait("vector", rm8_tok)
        p.step("vector", lambda e: e.memset(st[0:1, 624:640], 0.0))
        for r in range(8):
            p.wait_all("gpsimd", "V")
            g = p.dma("gpsimd", st[0:1, 608:624], agco[r:r + 1, :])
            p.wait("vector", g)
            p.step("vector", lambda e, r=r: e.scalar_tensor_tensor(
                st[0:1, 624:640], st[0:1, 608:624], st[0:1, 708 + r:709 + r],
                st[0:1, 624:640], ALU.mult, ALU.add))
        p.step("vector", lambda e: e.tensor_scalar(st[0:1, 624:640], st[0:1, 624:640], -1.0, C_CAP, ALU.mult, ALU.add))
        fl2 = self.vflush()
        p.wait("tensor", fl2)
        bi, bank = self.pool.acquire()
        for k in range(2):
            p.emit("tensor", lambda e, k=k, bank=bank: e.matmul(
                bank[:, k * 8:(k + 1) * 8], self.c_ones_rowf[:, :], st[0:1, 624 + k * 8:632 + k * 8],
                start=(k == 0), stop=(k == 1)), inc="T" if k == 1 else None)
        thr_tok = p.tok("T")
        p.wait("vector", thr_tok)
        p.step("vector", lambda e, bank=bank: e.tensor_copy(st[:, 576:592], bank[:, 0:16]))
        self.pool.release(bi, p.tok("V"))
        for tt in range(NT):
            for k in range(2):
                ohap = self.och[:, k * 128 + tt * 8: k * 128 + (tt + 1) * 8]
                p.step("vector", lambda e, ohap=ohap, k=k: e.tensor_tensor(
                    st[:, 680:688], ohap, st[:, 576 + k * 8:584 + k * 8], ALU.mult))
                p.step("vector", lambda e: e.reduce_sum(st[:, 688:689], st[:, 680:688], AX.X))
                p.step("vector", lambda e, k=k, tt=tt: e.tensor_tensor(
                    st[:, 689:690], st[:, 480 + k * 16 + tt:481 + k * 16 + tt], st[:, 688:689], ALU.is_lt))
                p.step("vector", lambda e, k=k, tt=tt: e.tensor_tensor(
                    st[:, 690:691], st[:, (448 if k == 0 else 464) + tt:(449 if k == 0 else 465) + tt],
                    st[:, 689:690], ALU.mult))
                if k == 0:
                    p.step("vector", lambda e, ohap=ohap, tt=tt: e.tensor_scalar(
                        st[:, 256 + tt * 8:264 + tt * 8], ohap, st[:, 690:691], None, ALU.mult))
                else:
                    p.step("vector", lambda e, ohap=ohap, tt=tt: e.scalar_tensor_tensor(
                        st[:, 256 + tt * 8:264 + tt * 8], ohap, st[:, 690:691],
                        st[:, 256 + tt * 8:264 + tt * 8], ALU.mult, ALU.add))
        self.vflush()

        hq = self.scrb
        for e_ in range(8):
            for q in range(4):
                p.barrier("gpsimd", ("T",))
                w1tok = p.dma("gpsimd", self.scra[:, 0:8192].rearrange("p (db f) -> p db f", f=1024),
                              w1[e_, :, q * 1024:(q + 1) * 1024].rearrange("(db p) f -> p db f", p=128))
                p.wait("tensor", w1tok)
                for hb in range(8):
                    for tc in range(4):
                        bi, bank = self.pool.acquire()
                        for db in range(ND):
                            p.emit("tensor", lambda e, db=db, hb=hb, tc=tc, bank=bank: e.matmul(
                                bank[:, 0:512],
                                self.scra[:, db * 1024 + hb * 128: db * 1024 + (hb + 1) * 128],
                                self.xn_fm[:, db * 2048 + tc * 512: db * 2048 + (tc + 1) * 512],
                                start=(db == 0), stop=(db == 7)), inc="T" if db == 7 else None)
                        mtok = p.tok("T")
                        p.wait("scalar", mtok)
                        etok = p.step("scalar", lambda e, hb=hb, tc=tc, bank=bank: e.activation(
                            hq[:, hb * 2048 + tc * 512: hb * 2048 + (tc + 1) * 512],
                            bank[:, 0:512], AF.Gelu))
                        self.pool.release(bi, etok)
                p.barrier("gpsimd", ("T", "A"))
                w2tok = p.dma("gpsimd", self.scra[:, 8192:16384].rearrange("p (hfb o) -> p hfb o", o=1024),
                              w2[e_, q * 1024:(q + 1) * 1024, :].rearrange("(hfb p) o -> p hfb o", p=128))
                p.wait("tensor", w2tok)
                p.wait("tensor", p.tok("A"))
                w2v = self.scra.rearrange("p (half hfb o) -> p half hfb o", half=2, o=1024)
                for tt in range(NT):
                    for oc in range(2):
                        bi, bank = self.pool.acquire()
                        for hfb in range(8):
                            p.emit("tensor", lambda e, hfb=hfb, tt=tt, oc=oc, bank=bank: e.matmul(
                                bank[:, 0:512],
                                hq[:, hfb * 2048 + tt * 128: hfb * 2048 + (tt + 1) * 128],
                                w2v[:, 1, hfb, oc * 512:(oc + 1) * 512],
                                start=(hfb == 0), stop=(hfb == 7)), inc="T" if hfb == 7 else None)
                        mtok = p.tok("T")
                        p.wait("vector", mtok)
                        dst = self.xres[:, tt * 1024 + oc * 512: tt * 1024 + (oc + 1) * 512]
                        etok = p.step("vector", lambda e, dst=dst, bank=bank, e_=e_, tt=tt: e.scalar_tensor_tensor(
                            dst, bank[:, 0:512], st[:, 256 + tt * 8 + e_:257 + tt * 8 + e_],
                            dst, ALU.mult, ALU.add))
                        self.pool.release(bi, etok)


    def emit_l2(self):
        p = self.p
        st = self.stats
        wb = self.wbuf
        phase = self.din("L2_attn_phase", [4, 1024])
        wq = self.din("L2_attn_Wq", [1024, 4096])
        bq = self.din("L2_attn_bq", [4096])
        wk = self.din("L2_attn_Wk", [1024, 4096])
        bk = self.din("L2_attn_bk", [4096])
        wv = self.din("L2_attn_Wv", [1024, 4096])
        bv = self.din("L2_attn_bv", [4096])
        wo = self.din("L2_attn_Wo", [4096, 1024])
        bo = self.din("L2_attn_bo", [1024])
        q_dram = self.dint("l2q", [2048, 4096], BF16)
        k_dram = self.dint("l2k", [2048, 4096], BF16)

        xn_tok = p.tok("A")
        p.wait("tensor", xn_tok)
        for which, wmat, bvec, dstd in (("q", wq, bq, q_dram), ("k", wk, bk, k_dram)):
            p.barrier("gpsimd", ("T", "A", "V"))
            btok = p.dma("gpsimd", self.rowb[0:1, 0:4096], bvec.rearrange("f -> () f"))
            for fc in range(8):
                p.wait_all("gpsimd", "T")
                wtok = p.dma("gpsimd", wb.rearrange("p (db f) -> p db f", f=512),
                             wmat[:, fc * 512:(fc + 1) * 512].rearrange("(db p) f -> p db f", p=128))
                p.wait("tensor", wtok)
                p.wait("tensor", btok)
                for tt in range(NT):
                    bi, bank = self.pool.acquire()
                    for db in range(ND):
                        p.emit("tensor", lambda e, db=db, tt=tt, bank=bank: e.matmul(
                            bank[:, 0:512],
                            self.xn_fm[:, db * 2048 + tt * 128:db * 2048 + (tt + 1) * 128],
                            wb[:, db * 512:(db + 1) * 512], start=(db == 0), stop=False))
                    p.emit("tensor", lambda e, fc=fc, bank=bank: e.matmul(
                        bank[:, 0:512], self.c_ones_row[:, :],
                        self.rowb[0:1, fc * 512:(fc + 1) * 512], start=False, stop=True), inc="T")
                    mtok = p.tok("T")
                    p.wait("scalar", mtok)
                    etok = p.step("scalar", lambda e, tt=tt, bank=bank: e.copy(
                        self.scra[:, tt * 512:(tt + 1) * 512], bank[:, 0:512]))
                    self.pool.release(bi, etok)
                fl = p.tok("A")
                p.wait("sync", fl)
                p.dma("sync", dstd.rearrange("(tt p) (fc f) -> p tt fc f", p=128, f=512)[:, :, fc, :],
                      self.scra[:, 0:8192].rearrange("p (tt f) -> p tt f", f=512))

        p.barrier("gpsimd", ("T", "A", "V"))
        for pr in range(16):
            p.dma("gpsimd", self.sst_f[pr:pr + 1, 0:1024], phase[pr // 4:pr // 4 + 1, :])
        ph_tok = p.tok("Dg")
        p.wait("vector", ph_tok)
        p.step("vector", lambda e: e.tensor_scalar(self.sst_f[0:16, 0:1024],
                                                   self.sst_f[0:16, 0:1024], -1.0, None, ALU.mult))
        neg = self.vflush()
        p.wait("gpsimd", neg)
        for pr in range(16):
            p.dma("gpsimd", self.sst_f[pr:pr + 1, 0:1024], phase[pr % 4:pr % 4 + 1, :],
                  accum_op=ALU.add)
        ph2 = p.tok("Dg")
        p.wait("scalar", ph2)
        p.step("scalar", lambda e: e.activation(self.sst_f[0:16, 0:1024], self.sst_f[0:16, 0:1024],
                                                AF.Sin, bias=self.fconst(math.pi / 2)))
        fl = p.tok("A")
        cosrow_d = self.dint("l2cd", [16, 1024], F32)
        p.wait("sync", fl)
        cd1 = p.dma("sync", cosrow_d[:, :], self.sst_f[0:16, 0:1024])
        p.wait("sync", cd1)
        for pr in range(16):
            rtok = p.dma("sync", self.rowf[0:1, 0:1024], cosrow_d[pr:pr + 1, :])
            p.wait("tensor", rtok)
            for dc in range(2):
                bi, bank = self.pool.acquire()
                p.step("tensor", lambda e, dc=dc, bank=bank: e.matmul(
                    bank[:, 0:512], self.c_ones_rowf[:, :], self.rowf[0:1, dc * 512:(dc + 1) * 512],
                    start=True, stop=True))
                mtok = p.tok("T")
                p.wait("scalar", mtok)
                etok = p.step("scalar", lambda e, pr=pr, dc=dc, bank=bank: e.copy(
                    self.scrb[:, pr * 1024 + dc * 512: pr * 1024 + (dc + 1) * 512], bank[:, 0:512]))
                self.pool.release(bi, etok)
            p.barrier("sync", ("T",))

        SC = 1.0 / 32.0
        for tt in range(NT):
            p.barrier("sync", ("V",))
            qtok = p.dma("sync", self.scra[:, 0:4096], q_dram[tt * 128:(tt + 1) * 128, :])
            ktok = p.dma("sync", self.scra[:, 4096:8192], k_dram[tt * 128:(tt + 1) * 128, :])
            p.wait("vector", ktok)
            base = 256 + tt * 16
            for pr in range(16):
                h, g = pr // 4, pr % 4
                for dh in range(2):
                    p.step("vector", lambda e, h=h, g=g, dh=dh: e.tensor_tensor(
                        self.sm[0][:, 0:512],
                        self.scra[:, h * 1024 + dh * 512: h * 1024 + dh * 512 + 512],
                        self.scra[:, 4096 + g * 1024 + dh * 512: 4096 + g * 1024 + dh * 512 + 512],
                        ALU.mult))
                    p.step("vector", lambda e, pr=pr, dh=dh: e.scalar_tensor_tensor(
                        self.sm[1][:, 0:512], self.sm[0][:, 0:512], SC,
                        self.scrb[:, pr * 1024 + dh * 512:pr * 1024 + (dh + 1) * 512],
                        ALU.mult, ALU.mult,
                        accum_out=st[:, 248 + dh:249 + dh]))
                p.step("vector", lambda e, pr=pr, base=base: e.tensor_tensor(
                    st[:, base + pr:base + pr + 1], st[:, 248:249], st[:, 249:250], ALU.add))
            p.step("vector", lambda e, base=base: e.tensor_reduce(
                st[:, 240:244], st[:, base:base + 16].rearrange("p (h g) -> p h g", g=4),
                AX.X, ALU.max))
            p.step("vector", lambda e, base=base: e.tensor_tensor(
                st[:, 224:240].rearrange("p (h g) -> p h g", g=4),
                st[:, base:base + 16].rearrange("p (h g) -> p h g", g=4),
                bcast_free(st[:, 240:244], 4), ALU.subtract))
            p.wait("scalar", p.tok("V"))
            p.step("scalar", lambda e: e.activation(st[:, 224:240], st[:, 224:240], AF.Exp))
            p.wait("vector", p.tok("A"))
            p.step("vector", lambda e: e.tensor_reduce(
                st[:, 240:244], st[:, 224:240].rearrange("p (h g) -> p h g", g=4), AX.X, ALU.add))
            p.step("vector", lambda e: e.reciprocal(st[:, 244:248], st[:, 240:244]))
            p.step("vector", lambda e, base=base: e.tensor_tensor(
                st[:, base:base + 16].rearrange("p (h g) -> p h g", g=4),
                st[:, 224:240].rearrange("p (h g) -> p h g", g=4),
                bcast_free(st[:, 244:248], 4), ALU.mult))
        self.vflush()
        self.dump("l2attn", self.stats[:, 256:512])

        bo_tok = p.dma("gpsimd", self.rowb[0:1, 4096:5120], bo.rearrange("f -> () f"))
        omix_dram = self.dint("l2om_in", [2048, 4096], BF16)
        omix_ago = self.dint("l2om_out", [8192, 4096], BF16)
        gsel = self.din("c_groupsel", [128, 4])
        gtok = p.dma("sync", st[:, 716:720], gsel[:, :])
        for dc in range(8):
            p.barrier("gpsimd", ("T", "A"))
            for hh in range(4):
                wvtok = p.dma("gpsimd",
                              wb.rearrange("p (db h j) -> p db h j", db=8, h=4)[:, :, hh, :],
                              wv.rearrange("(db p) (h s j) -> p db h s j", p=128, h=4, s=8)[:, :, hh, dc, :])
            bvtok = p.dma("gpsimd", self.rowb[0:1, 0:512].rearrange("o (h j) -> o h j", j=128),
                          bv.rearrange("(h s j) -> h s j", h=4, s=8)[:, dc, :].rearrange("h j -> () h j"))
            p.wait("tensor", wvtok)
            p.wait("tensor", bvtok)
            for tt in range(NT):
                bi, bank = self.pool.acquire()
                for db in range(ND):
                    p.emit("tensor", lambda e, db=db, tt=tt, bank=bank: e.matmul(
                        bank[:, 0:512],
                        self.xn_fm[:, db * 2048 + tt * 128:db * 2048 + (tt + 1) * 128],
                        wb[:, db * 512:(db + 1) * 512], start=(db == 0), stop=False))
                p.emit("tensor", lambda e, bank=bank: e.matmul(
                    bank[:, 0:512], self.c_ones_row[:, :], self.rowb[0:1, 0:512],
                    start=False, stop=True), inc="T")
                mtok = p.tok("T")
                p.wait("vector", mtok)
                etok = p.step("vector", lambda e, bank=bank: e.tensor_copy(
                    self.och[:, 0:512], bank[:, 0:512]))
                self.pool.release(bi, etok)
                for h in range(4):
                    for g in range(4):
                        sc_ap = st[:, 256 + tt * 16 + h * 4 + g: 256 + tt * 16 + h * 4 + g + 1]
                        if g == 0:
                            p.step("vector", lambda e, h=h, g=g, sc_ap=sc_ap: e.tensor_scalar(
                                self.och[:, 1024 + h * 128:1024 + (h + 1) * 128],
                                self.och[:, g * 128:(g + 1) * 128], sc_ap, None, ALU.mult))
                        else:
                            p.step("vector", lambda e, h=h, g=g, sc_ap=sc_ap: e.scalar_tensor_tensor(
                                self.och[:, 1024 + h * 128:1024 + (h + 1) * 128],
                                self.och[:, g * 128:(g + 1) * 128], sc_ap,
                                self.och[:, 1024 + h * 128:1024 + (h + 1) * 128], ALU.mult, ALU.add))
                omtok = self.vflush()
                p.wait("gpsimd", omtok)
                p.dma("gpsimd",
                      omix_dram[tt * 128:(tt + 1) * 128, :].rearrange("p (h s j) -> p h s j", h=4, s=8)[:, :, dc, :],
                      self.och[:, 1024:1536].rearrange("p (h j) -> p h j", j=128))
        fl = p.tok("Dg")
        ag_tok = self.ag_go(omix_dram, omix_ago, fl)
        p.wait("sync", ag_tok)
        p.wait("vector", gtok)
        for oc in range(2):
            p.barrier("gpsimd", ("T", "A", "V"))
            wotok = p.dma("gpsimd", self.scrb[:, 0:16384].rearrange("p (fb o) -> p fb o", o=512),
                          wo.rearrange("(fb p) o -> p fb o", p=128)[:, :, oc * 512:(oc + 1) * 512])
            p.wait("tensor", wotok)
            for tt in range(NT):
                p.barrier("sync", ("T", "V"))
                for r in range(4):
                    p.dma("sync", self.scra[:, r * 4096:(r + 1) * 4096].rearrange("p (j d) -> p j d", d=1024),
                          omix_ago.rearrange("(r q) (h d) -> r q h d", r=4, h=4)[tt // 4,
                          (tt % 4) * 512:(tt % 4 + 1) * 512, r, :]
                          .rearrange("(p j) d -> p j d", p=128))
                dtok = p.tok("Ds")
                p.wait("vector", dtok)
                for r in range(4):
                    sc_ap = st[:, 716 + r:717 + r]
                    op = "init" if r == 0 else "acc"
                    for dsti, (dst, lo, ln) in enumerate(
                            ((self.och[:, 0:2048], 0, 2048),
                             (self.sm[0][:, 0:1024], 2048, 1024),
                             (self.sm[1][:, 0:1024], 3072, 1024))):
                        if r == 0:
                            p.step("vector", lambda e, r=r, sc_ap=sc_ap, dst=dst, lo=lo, ln=ln: e.tensor_scalar(
                                dst, self.scra[:, r * 4096 + lo:r * 4096 + lo + ln], sc_ap, None, ALU.mult))
                        else:
                            p.step("vector", lambda e, r=r, sc_ap=sc_ap, dst=dst, lo=lo, ln=ln: e.scalar_tensor_tensor(
                                dst, self.scra[:, r * 4096 + lo:r * 4096 + lo + ln], sc_ap,
                                dst, ALU.mult, ALU.add))
                seltok = self.vflush()
                p.wait("tensor", seltok)
                srcs = [self.och[:, 0:2048], self.sm[0][:, 0:1024], self.sm[1][:, 0:1024]]
                def piece(fb):
                    if fb < 16:
                        return srcs[0].rearrange("p (a b) -> p a b", b=128)[:, fb, :]
                    if fb < 24:
                        return srcs[1].rearrange("p (a b) -> p a b", b=128)[:, fb - 16, :]
                    return srcs[2].rearrange("p (a b) -> p a b", b=128)[:, fb - 24, :]
                for fb in range(32):
                    self.transpose(piece(fb), self.scra[:, fb * 128:(fb + 1) * 128])
                p.wait("tensor", p.tok("A"))
                p.wait("tensor", bo_tok)
                bi, bank = self.pool.acquire()
                for fb in range(32):
                    p.emit("tensor", lambda e, fb=fb, bank=bank: e.matmul(
                        bank[:, 0:512], self.scra[:, fb * 128:(fb + 1) * 128],
                        self.scrb[:, fb * 512:(fb + 1) * 512],
                        start=(fb == 0), stop=False))
                p.emit("tensor", lambda e, oc=oc, bank=bank: e.matmul(
                    bank[:, 0:512], self.c_ones_row[:, :],
                    self.rowb[0:1, 4096 + oc * 512:4096 + (oc + 1) * 512], start=False, stop=True), inc="T")
                mtok = p.tok("T")
                p.wait("vector", mtok)
                dst = self.xres[:, tt * 1024 + oc * 512: tt * 1024 + (oc + 1) * 512]
                etok = p.step("vector", lambda e, dst=dst, bank=bank: e.tensor_tensor(
                    dst, dst, bank[:, 0:512], ALU.add))
                self.pool.release(bi, etok)

    def emit_l2_ffn(self):
        p = self.p
        st = self.stats
        wb = self.wbuf
        gate_w = self.din("L2_ffn_gate_w", [1024, 8])
        gate_b = self.din("L2_ffn_gate_b", [8])
        ew = self.din("L2_ffn_expert_w", [8, 1024, 2048])
        eb = self.din("L2_ffn_expert_b", [8, 2048])
        ow = self.din("L2_ffn_out_w", [2048, 1024])
        ob = self.din("L2_ffn_out_b", [1024])
        sel_dram = self.dint("l2sel", [2048, 2048], BF16)

        self.dump("l2xres_a", self.xres[:, :])
        p.barrier("gpsimd", ("T", "A", "V"))
        gwtok = p.dma("gpsimd", wb[:, 0:64].rearrange("p (db f) -> p db f", f=8),
                      gate_w.rearrange("(db p) f -> p db f", p=128))
        gbtok = p.dma("gpsimd", self.rowb[0:1, 0:8], gate_b.rearrange("f -> () f"))
        ebtok = p.dma("gpsimd", self.scrb[0:8, 0:2048], eb[:, :])
        p.step("vector", lambda e: e.tensor_scalar(st[:, 640:648], self.c_iota8[:, :], 1e-6, None, ALU.mult))
        p.wait("tensor", gwtok)
        p.wait("tensor", gbtok)
        for tt in range(NT):
            bi, bank = self.pool.acquire()
            for db in range(ND):
                p.emit("tensor", lambda e, db=db, tt=tt, bank=bank: e.matmul(
                    bank[:, 0:8], self.xn_fm[:, db * 2048 + tt * 128:db * 2048 + (tt + 1) * 128],
                    wb[:, db * 8:(db + 1) * 8], start=(db == 0), stop=False))
            p.emit("tensor", lambda e, bank=bank: e.matmul(
                bank[:, 0:8], self.c_ones_row[:, :], self.rowb[0:1, 0:8],
                start=False, stop=True), inc="T")
            mtok = p.tok("T")
            p.wait("vector", mtok)
            p.step("vector", lambda e, bank=bank: e.tensor_tensor(st[:, 664:672], bank[:, 0:8], st[:, 640:648], ALU.subtract))
            p.step("vector", lambda e: e.reduce_max(st[:, 672:673], st[:, 664:672], AX.X))
            etok = p.step("vector", lambda e, tt=tt: e.tensor_scalar(
                st[:, 256 + tt * 8:264 + tt * 8], st[:, 664:672], st[:, 672:673], None, ALU.is_equal))
            self.pool.release(bi, etok)
        self.vflush()

        self.dump("l2coef", self.stats[:, 256:384])
        for e_ in range(8):
            p.barrier("gpsimd", ("T", "Ds"))
            wtok = p.dma("gpsimd", self.scra[:, 0:16384].rearrange("p (db f) -> p db f", f=2048),
                         ew[e_, :, :].rearrange("(db p) f -> p db f", p=128))
            p.wait("tensor", wtok)
            for tt in range(NT):
                if e_ == 0:
                    p.step("vector", lambda e, tt=tt: e.tensor_copy(
                        self.sm[1][:, 0:8], st[:, 256 + tt * 8:264 + tt * 8]))
                    p.wait("tensor", p.tok("V"))
                    self.transpose(self.sm[1][:, 0:8], None, evac="vector",
                                   evac_fn=lambda pt: p.step("vector", lambda e, pt=pt: e.tensor_copy(
                                       self.scrb[0:8, 2048:2176], pt)))
                for hc in range(4):
                    bi, bank = self.pool.acquire()
                    for db in range(ND):
                        p.emit("tensor", lambda e, db=db, tt=tt, hc=hc, bank=bank: e.matmul(
                            bank[:, 0:512],
                            self.xn_fm[:, db * 2048 + tt * 128:db * 2048 + (tt + 1) * 128],
                            self.scra[:, db * 2048 + hc * 512:db * 2048 + (hc + 1) * 512],
                            start=(db == 0), stop=(db == 7)), inc="T" if db == 7 else None)
                    mtok = p.tok("T")
                    p.wait("vector", mtok)
                    sc_ap = st[:, 256 + tt * 8 + e_:257 + tt * 8 + e_]
                    p.step("vector", lambda e, hc=hc, bank=bank, sc_ap=sc_ap: e.tensor_scalar(
                        self.och[:, hc * 512:(hc + 1) * 512], bank[:, 0:512], sc_ap, None, ALU.mult))
                    etok = p.tok("V")
                    self.pool.release(bi, etok)
                if e_ == 0:
                    for hc in range(4):
                        bi, bank = self.pool.acquire()
                        src = self.scrb[0:8, hc * 512:(hc + 1) * 512]
                        p.step("tensor", lambda e, bank=bank, src=src: e.matmul(
                            bank[:, 0:512], self.scrb[0:8, 2048:2176], src, start=True, stop=True))
                        mtok = p.tok("T")
                        p.wait("vector", mtok)
                        etok = p.step("vector", lambda e, hc=hc, bank=bank: e.tensor_tensor(
                            self.och[:, hc * 512:(hc + 1) * 512], self.och[:, hc * 512:(hc + 1) * 512],
                            bank[:, 0:512], ALU.add))
                        self.pool.release(bi, etok)
                fl = self.vflush()
                p.wait("gpsimd", fl)
                if e_ == 0:
                    p.dma("gpsimd", sel_dram[tt * 128:(tt + 1) * 128, :], self.och[:, 0:2048])
                else:
                    p.dma("gpsimd", sel_dram[tt * 128:(tt + 1) * 128, :], self.och[:, 0:2048],
                          accum_op=ALU.add)
        p.barrier("gpsimd", ("T", "A", "V", "Dg"))
        owtok = p.dma("gpsimd", self.scrb[:, 0:16384].rearrange("p (hb o) -> p hb o", o=1024),
                      ow.rearrange("(hb p) o -> p hb o", p=128))
        obtok = p.dma("gpsimd", self.rowb[0:1, 0:1024], ob.rearrange("f -> () f"))
        p.wait("tensor", owtok)
        p.wait("tensor", obtok)
        for tt in range(NT):
            p.barrier("sync", ("T", "Dg"))
            ltok = p.dma("sync", self.och[:, 0:2048], sel_dram[tt * 128:(tt + 1) * 128, :])
            p.wait("tensor", ltok)
            for fb in range(16):
                dst = self.sm[fb // 8][:, (fb % 8) * 128:(fb % 8 + 1) * 128]
                self.transpose(self.och[:, fb * 128:(fb + 1) * 128], dst)
            p.wait("tensor", p.tok("A"))
            for oc in range(2):
                bi, bank = self.pool.acquire()
                for fb in range(16):
                    lhs = self.sm[fb // 8][:, (fb % 8) * 128:(fb % 8 + 1) * 128]
                    p.emit("tensor", lambda e, fb=fb, oc=oc, bank=bank, lhs=lhs: e.matmul(
                        bank[:, 0:512], lhs,
                        self.scrb[:, fb * 1024 + oc * 512: fb * 1024 + (oc + 1) * 512],
                        start=(fb == 0), stop=False))
                p.emit("tensor", lambda e, oc=oc, bank=bank: e.matmul(
                    bank[:, 0:512], self.c_ones_row[:, :],
                    self.rowb[0:1, oc * 512:(oc + 1) * 512], start=False, stop=True), inc="T")
                mtok = p.tok("T")
                p.wait("vector", mtok)
                dst = self.xres[:, tt * 1024 + oc * 512: tt * 1024 + (oc + 1) * 512]
                etok = p.step("vector", lambda e, dst=dst, bank=bank: e.tensor_tensor(
                    dst, dst, bank[:, 0:512], ALU.add))
                self.pool.release(bi, etok)


    def emit_l3(self, xnd, bdi, bdo):
        p = self.p
        st = self.stats
        wb = self.wbuf
        w_write = self.din("L3_attn_W_write", [1024, 1024])
        w_gate = self.din("L3_attn_W_gate", [16, 1024])
        w_beta = self.din("L3_attn_W_beta", [16, 1024])
        w_alpha = self.din("L3_attn_W_alpha", [16, 1024])
        w_out = self.din("L3_attn_W_out", [1024, 1024])
        a_log = self.din("L3_attn_A_log", [16])
        dt_bias = self.din("L3_attn_dt_bias", [16])
        psel = self.din("c_prevsel", [128, 4])
        rmask = self.din("c_rankmask", [128, 4])
        agi = self.dint("ag_l3s_in", [64, 1040])
        ago = self.dint("ag_l3s_out", [256, 1040])
        o_dram = self.dint("l3o", [2048, 1024], BF16)
        decd = self.dint("l3decd", [256, 128])
        bdc = self.dint("l3bdc", [1, 1024], BF16)
        k_tm = self.scra
        v_tm = self.scrb

        bf_tok = p.dma("sync", bdi[0:1, :], xnd[2047:2048, :])
        ag_tok = self.ag_go(bdi, bdo, bf_tok)
        ps_tok = p.dma("sync", st[:, 720:724], psel[:, :])

        p.barrier("sync", ("T", "V", "A"))
        for tt in range(NT):
            p.dma("sync", k_tm[1:128, tt * 1024:(tt + 1) * 1024],
                  xnd[tt * 128:tt * 128 + 127, :])
            if tt > 0:
                p.dma("sync", k_tm[0:1, tt * 1024:(tt + 1) * 1024],
                      xnd[tt * 128 - 1:tt * 128, :])
        p.barrier("gpsimd", ("T", "V", "A"))
        p.wait("gpsimd", ag_tok)
        p.wait("vector", ps_tok)
        for r in range(4):
            bd_tok = p.dma("gpsimd", self.och[0:1, 0:1024], bdo[r:r + 1, :])
            p.wait("vector", bd_tok)
            if r == 0:
                p.step("vector", lambda e, r=r: e.tensor_scalar(
                    self.sm[0][0:1, 0:1024], self.och[0:1, 0:1024],
                    st[0:1, 720 + r:721 + r], None, ALU.mult))
            else:
                p.step("vector", lambda e, r=r: e.scalar_tensor_tensor(
                    self.sm[0][0:1, 0:1024], self.och[0:1, 0:1024],
                    st[0:1, 720 + r:721 + r], self.sm[0][0:1, 0:1024], ALU.mult, ALU.add))
            fl = self.vflush()
            p.wait("gpsimd", fl)
        bfl = self.vflush()
        p.wait("sync", bfl)
        bc_tok = p.dma("sync", bdc[0:1, :], self.sm[0][0:1, 0:1024])
        p.wait("sync", bc_tok)
        p.dma("sync", k_tm[0:1, 0:1024], bdc[0:1, :])
        s8_tok = p.dma("sync", self.sm[1][0:8, 0:128],
                       bdc.rearrange("o (db pp) -> (o db) pp", pp=128))
        p.wait("tensor", s8_tok)
        self.transpose(self.sm[1][0:8, 0:128], self.och[:, 1024:1032])

        p.barrier("gpsimd", ("T", "A", "V"))
        for wi, wmat in enumerate((w_gate, w_beta, w_alpha)):
            stok = p.dma("gpsimd", self.sm[0][0:16, 0:1024], wmat[:, :])
            p.wait("tensor", stok)
            for db in range(ND):
                self.transpose(self.sm[0][0:16, db * 128:(db + 1) * 128],
                               wb[:, db * 48 + wi * 16: db * 48 + (wi + 1) * 16])
            p.barrier("gpsimd", ("T",))
        p.step("vector", lambda e: e.memset(self.rowb[0:1, 0:48], 0.0))
        fl = self.vflush()
        p.wait("gpsimd", fl)
        p.dma("gpsimd", self.rowb[0:1, 32:48], dt_bias.rearrange("f -> () f"))
        al_tok = p.dma("sync", st[0:1, 144:160], a_log.rearrange("f -> () f"))
        p.wait("scalar", al_tok)
        p.step("scalar", lambda e: e.activation(st[0:1, 144:160], st[0:1, 144:160], AF.Exp))
        p.wait("vector", p.tok("A"))
        p.step("vector", lambda e: e.tensor_scalar(st[0:1, 144:160], st[0:1, 144:160], -1.0, None, ALU.mult))
        fl = self.vflush()
        p.wait("tensor", fl)
        bi, bank = self.pool.acquire()
        p.step("tensor", lambda e, bank=bank: e.matmul(
            bank[:, 0:16], self.c_ones_rowf[:, :], st[0:1, 144:160], start=True, stop=True))
        p.wait("vector", p.tok("T"))
        et = p.step("vector", lambda e, bank=bank: e.tensor_copy(st[:, 160:176], bank[:, 0:16]))
        self.pool.release(bi, et)

        p.wait("tensor", p.tok("Dg"))
        p.wait("tensor", p.tok("A"))
        for tt in range(NT):
            bi, bank = self.pool.acquire()
            for db in range(ND):
                p.emit("tensor", lambda e, db=db, tt=tt, bank=bank: e.matmul(
                    bank[:, 0:48], self.xn_fm[:, db * 2048 + tt * 128:db * 2048 + (tt + 1) * 128],
                    wb[:, db * 48:(db + 1) * 48], start=(db == 0), stop=False))
            p.emit("tensor", lambda e, bank=bank: e.matmul(
                bank[:, 0:48], self.c_ones_row[:, :], self.rowb[0:1, 0:48],
                start=False, stop=True), inc="T")
            mtok = p.tok("T")
            p.wait("scalar", mtok)
            p.step("scalar", lambda e, tt=tt, bank=bank: e.activation(
                self.sst_f[:, 768 + tt * 16:784 + tt * 16], bank[:, 0:16], AF.Sigmoid))
            p.step("scalar", lambda e, tt=tt, bank=bank: e.activation(
                self.och[:, tt * 16:(tt + 1) * 16], bank[:, 16:32], AF.Sigmoid))
            p.wait("vector", p.tok("A"))
            p.step("vector", lambda e, bank=bank: e.tensor_scalar(
                st[:, 128:144], bank[:, 32:48], -1.0, None, ALU.mult))
            p.step("vector", lambda e, bank=bank: e.tensor_tensor(
                st[:, 176:192], st[:, 128:144], bank[:, 32:48], ALU.min))
            p.wait("scalar", p.tok("V"))
            p.step("scalar", lambda e: e.activation(st[:, 176:192], st[:, 176:192], AF.Exp))
            p.step("scalar", lambda e: e.activation(st[:, 176:192], st[:, 176:192],
                                                    AF.Ln, bias=self.fconst(1.0)))
            p.wait("vector", p.tok("A"))
            p.step("vector", lambda e, bank=bank: e.tensor_scalar(
                st[:, 128:144], bank[:, 32:48], 0.0, None, ALU.max))
            p.step("vector", lambda e: e.tensor_tensor(st[:, 128:144], st[:, 128:144],
                                                       st[:, 176:192], ALU.add))
            p.step("vector", lambda e: e.tensor_tensor(st[:, 176:192], st[:, 128:144],
                                                       st[:, 160:176], ALU.mult))
            fl2 = self.vflush()
            self.pool.release(bi, fl2)
            p.wait("tensor", fl2)
            bi2, bank2 = self.pool.acquire()
            p.step("tensor", lambda e, bank2=bank2: e.matmul(
                bank2[:, 0:16], self.c_trilf[:, :], st[:, 176:192], start=True, stop=True))
            p.wait("vector", p.tok("T"))
            et2 = p.step("vector", lambda e, tt=tt, bank2=bank2: e.tensor_copy(
                st[:, 256 + tt * 16:272 + tt * 16], bank2[:, 0:16]))
            self.pool.release(bi2, et2)
            p.wait("tensor", et2)
            tfl = self.transpose_f32(st[:, 256 + tt * 16:272 + tt * 16],
                                     self.sst_f[0:16, 1024:1152])
            p.wait("sync", tfl)
            p.dma("sync", decd[tt * 16:(tt + 1) * 16, :], self.sst_f[0:16, 1024:1152])

        def build_wT_l3(wsrc, f0, nfb, dst, stage):
            for fb in range(nfb):
                p.barrier("gpsimd", ("T", "A", "V"))
                stok = p.dma("gpsimd", stage, wsrc[f0 + fb * 128:f0 + (fb + 1) * 128, :])
                p.wait("tensor", stok)
                for db in range(ND):
                    self.transpose(stage[:, db * 128:(db + 1) * 128],
                                   dst[:, db * (nfb * 128) + fb * 128: db * (nfb * 128) + (fb + 1) * 128])
        for fc in range(2):
            build_wT_l3(w_write, fc * 512, 4, wb[:, 0:4096], self.sm[0][:, 0:1024])
            p.wait("tensor", p.tok("A"))
            for tt in range(NT):
                bi, bank = self.pool.acquire()
                for db in range(ND):
                    p.emit("tensor", lambda e, db=db, tt=tt, bank=bank: e.matmul(
                        bank[:, 0:512],
                        self.xn_fm[:, db * 2048 + tt * 128:db * 2048 + (tt + 1) * 128],
                        wb[:, db * 512:(db + 1) * 512], start=(db == 0), stop=(db == 7)),
                        inc="T" if db == 7 else None)
                mtok = p.tok("T")
                p.wait("vector", mtok)
                etok = p.step("vector", lambda e, tt=tt, fc=fc, bank=bank: e.tensor_tensor(
                    v_tm[:, tt * 1024 + fc * 512: tt * 1024 + (fc + 1) * 512].rearrange(
                        "pp (h d) -> pp h d", d=64),
                    bank[:, 0:512].rearrange("pp (h d) -> pp h d", d=64),
                    bcast_free(self.och[:, tt * 16 + fc * 8:tt * 16 + (fc + 1) * 8], 64), ALU.mult))
                self.pool.release(bi, etok)
        self.vflush()

        def dec_batch(tt):
            p.barrier("sync", ("T",))
            rtok = p.dma("sync", self.rowf[0:1, 0:2048],
                         decd[tt * 16:(tt + 1) * 16, :].rearrange("(o r) c -> o (r c)", o=1))
            bi, bank = self.pool.acquire()
            p.wait("tensor", rtok)
            dlast = self.rowf[0:1, 0:2048].rearrange("o (h c) -> o h c", c=128)[:, :, 127:128]
            p.step("tensor", lambda e, bank=bank, dlast=dlast: e.matmul(
                bank[:, 0:16].rearrange("pp (h o) -> pp h o", o=1), self.c_ones_rowf[:, :],
                dlast, start=True, stop=True))
            p.wait("vector", p.tok("T"))
            p.step("vector", lambda e, bank=bank: e.tensor_copy(st[:, 192:208], bank[:, 0:16]))
            fl = self.vflush()
            self.pool.release(bi, fl)
            p.wait("scalar", fl)
            p.step("scalar", lambda e: e.activation(st[:, 208:224], st[:, 192:208], AF.Exp))
            p.wait("vector", p.tok("A"))
            p.step("vector", lambda e, tt=tt: e.tensor_tensor(
                st[:, 224:240], st[:, 192:208], st[:, 256 + tt * 16:272 + tt * 16], ALU.subtract))
            fl = self.vflush()
            p.wait("scalar", fl)
            p.step("scalar", lambda e: e.activation(st[:, 224:240], st[:, 224:240], AF.Exp))
            p.step("scalar", lambda e, tt=tt: e.activation(
                self.sst_f[:, 640:656], st[:, 256 + tt * 16:272 + tt * 16], AF.Exp))
            p.wait("vector", p.tok("A"))
            return rtok

        p.step("vector", lambda e: e.memset(self.sst_f[:, 0:512], 0.0))
        p.step("vector", lambda e: e.memset(st[0:1, 240:256], 0.0))
        self.vflush()
        for tt in range(NT):
            dec_batch(tt)
            p.step("vector", lambda e: e.tensor_tensor(
                st[0:1, 240:256], st[0:1, 240:256], st[0:1, 192:208], ALU.add))
            for h in range(16):
                b0 = (h % 2) * 64
                hb = h // 2
                p.step("vector", lambda e, tt=tt, h=h: e.tensor_scalar(
                    self.sm[0][:, 0:64], k_tm[:, tt * 1024 + h * 64: tt * 1024 + (h + 1) * 64],
                    st[:, 224 + h:225 + h], None, ALU.mult))
                fl = self.vflush()
                p.wait("tensor", fl)
                bi, bank = self.pool.acquire()
                p.step("tensor", lambda e, tt=tt, h=h, b0=b0, bank=bank: e.matmul(
                    bank[b0:b0 + 64, 0:64], self.sm[0][:, 0:64],
                    v_tm[:, tt * 1024 + h * 64: tt * 1024 + (h + 1) * 64], start=True, stop=True))
                mtok = p.tok("T")
                p.wait("vector", mtok)
                etok = p.step("vector", lambda e, b0=b0, hb=hb, h=h, bank=bank: e.scalar_tensor_tensor(
                    self.sst_f[b0:b0 + 64, hb * 64:(hb + 1) * 64],
                    self.sst_f[b0:b0 + 64, hb * 64:(hb + 1) * 64],
                    st[b0:b0 + 64, 208 + h:209 + h],
                    bank[b0:b0 + 64, 0:64], ALU.mult, ALU.add))
                self.pool.release(bi, etok)
        fl = self.vflush()
        p.wait("gpsimd", fl)
        f1 = p.dma("gpsimd", agi[:, 0:512], self.sst_f[0:64, 0:512])
        f2 = p.dma("gpsimd", agi[:, 512:1024], self.sst_f[64:128, 0:512])
        f3 = p.dma("gpsimd", agi[0:1, 1024:1040], st[0:1, 240:256])
        ag2 = self.ag_go(agi, ago, f3)

        rm_tok = p.dma("sync", st[:, 160:164], rmask[:, :])
        p.wait("vector", ag2)
        p.wait("vector", rm_tok)
        p.step("vector", lambda e: e.memset(self.sst_f[:, 0:512], 0.0))
        for r in range(4):
            p.wait_all("gpsimd", "V")
            g3 = p.dma("gpsimd", st[0:1, 144:160], ago[r * 64:r * 64 + 1, 1024:1040])
            p.wait("scalar", g3)
            p.step("scalar", lambda e: e.activation(st[0:1, 176:192], st[0:1, 144:160], AF.Exp))
            p.wait("vector", p.tok("A"))
            p.step("vector", lambda e: e.tensor_scalar(
                st[0:1, 176:192], st[0:1, 176:192], -1.0, None, ALU.add))
            p.step("vector", lambda e, r=r: e.tensor_scalar(
                st[0:1, 176:192], st[0:1, 176:192], st[0:1, 160 + r:161 + r], 1.0,
                ALU.mult, ALU.add))
            fl = self.vflush()
            p.wait("tensor", fl)
            bi, bank = self.pool.acquire()
            p.step("tensor", lambda e, bank=bank: e.matmul(
                bank[:, 0:16], self.c_ones_rowf[:, :], st[0:1, 176:192], start=True, stop=True))
            p.wait("vector", p.tok("T"))
            p.step("vector", lambda e, bank=bank: e.tensor_copy(st[:, 176:192], bank[:, 0:16]))
            vv = self.vflush()
            self.pool.release(bi, vv)
            for half in range(2):
                p.wait_all("gpsimd", "V")
                g1 = p.dma("gpsimd", self.sst_f[0:64, 512:768],
                           ago[r * 64:(r + 1) * 64, half * 256:(half + 1) * 256])
                g2 = p.dma("gpsimd", self.sst_f[64:128, 512:768],
                           ago[r * 64:(r + 1) * 64, 512 + half * 256:512 + (half + 1) * 256])
                p.wait("vector", g1)
                p.wait("vector", g2)
                p.step("vector", lambda e, r=r: e.tensor_scalar(
                    self.sst_f[:, 512:768], self.sst_f[:, 512:768], st[:, 160 + r:161 + r],
                    None, ALU.mult))
                for hbh in range(4):
                    for par in range(2):
                        hb = half * 4 + hbh
                        h = hb * 2 + par
                        b0 = par * 64
                        p.step("vector", lambda e, b0=b0, hb=hb, hbh=hbh, h=h:
                               e.scalar_tensor_tensor(
                                   self.sst_f[b0:b0 + 64, hb * 64:(hb + 1) * 64],
                                   self.sst_f[b0:b0 + 64, hb * 64:(hb + 1) * 64],
                                   st[b0:b0 + 64, 176 + h:177 + h],
                                   self.sst_f[b0:b0 + 64, 512 + hbh * 64:512 + (hbh + 1) * 64],
                                   ALU.mult, ALU.add))
        p.step("vector", lambda e: e.tensor_copy(self.sst_b[:, 0:512], self.sst_f[:, 0:512]))
        self.vflush()

        for tt in range(NT):
            dec_batch(tt)
            for h in range(16):
                b0 = (h % 2) * 64
                hb = h // 2
                db = h // 2
                bi_b, bank_b = self.pool.acquire()
                p.step("tensor", lambda e, h=h, bank_b=bank_b: e.matmul(
                    bank_b[:, 0:128], self.c_ones_rowf[:, :], self.rowf[0:1, h * 128:(h + 1) * 128],
                    start=True, stop=True))
                p.wait("vector", p.tok("T"))
                p.step("vector", lambda e, tt=tt, h=h, bank_b=bank_b: e.tensor_scalar(
                    self.sst_f[:, 512:640], bank_b[:, 0:128],
                    st[:, 256 + tt * 16 + h:257 + tt * 16 + h], 0.0, ALU.subtract, ALU.min))
                fl = self.vflush()
                self.pool.release(bi_b, fl)
                p.wait("scalar", fl)
                p.step("scalar", lambda e: e.activation(self.sm[1][:, 0:128],
                                                        self.sst_f[:, 512:640], AF.Exp))
                p.wait("vector", p.tok("A"))
                p.step("vector", lambda e: e.tensor_tensor(
                    self.sm[1][:, 128:256], self.sm[1][:, 0:128], self.c_l3strict[:, :], ALU.mult))
                if tt == 0:
                    p.step("vector", lambda e, db=db, b0=b0: e.tensor_copy(
                        self.sm[0][b0:b0 + 64, 385:512],
                        self.xn_fm[b0:b0 + 64, db * 2048: db * 2048 + 127]))
                    p.step("vector", lambda e, db=db, b0=b0: e.tensor_copy(
                        self.sm[0][b0:b0 + 64, 384:385], self.och[b0:b0 + 64, 1024 + db:1025 + db]))
                    wk_ap = self.sm[0][b0:b0 + 64, 384:512]
                else:
                    wk_ap = self.xn_fm[b0:b0 + 64,
                                       db * 2048 + tt * 128 - 1: db * 2048 + (tt + 1) * 128 - 1]
                fl = self.vflush()
                p.wait("tensor", fl)
                bi_i, bank_i = self.pool.acquire()
                p.step("tensor", lambda e, wk_ap=wk_ap, db=db, b0=b0, tt=tt, bank_i=bank_i: e.matmul(
                    bank_i[:, 0:128], wk_ap,
                    self.xn_fm[b0:b0 + 64, db * 2048 + tt * 128: db * 2048 + (tt + 1) * 128],
                    start=True, stop=True))
                p.wait("vector", p.tok("T"))
                p.step("vector", lambda e, bank_i=bank_i: e.tensor_tensor(
                    self.sm[0][:, 0:128], bank_i[:, 0:128], self.sm[1][:, 128:256], ALU.mult))
                ifl = self.vflush()
                self.pool.release(bi_i, ifl)
                p.wait("tensor", ifl)
                bi_n, bank_n = self.pool.acquire()
                p.step("tensor", lambda e, tt=tt, h=h, bank_n=bank_n: e.matmul(
                    bank_n[:, 0:64], self.sm[0][:, 0:128],
                    v_tm[:, tt * 1024 + h * 64: tt * 1024 + (h + 1) * 64], start=True, stop=True))
                bi_n2, bank_n2 = self.pool.acquire()
                p.step("tensor", lambda e, db=db, b0=b0, hb=hb, tt=tt, bank_n2=bank_n2: e.matmul(
                    bank_n2[:, 0:64],
                    self.xn_fm[b0:b0 + 64, db * 2048 + tt * 128: db * 2048 + (tt + 1) * 128],
                    self.sst_b[b0:b0 + 64, hb * 64:(hb + 1) * 64], start=True, stop=True))
                ntok = p.tok("T")
                p.wait("vector", ntok)
                p.step("vector", lambda e, h=h, bank_n2=bank_n2: e.tensor_scalar(
                    self.sm[0][:, 256:320], bank_n2[:, 0:64], self.sst_f[:, 640 + h:641 + h],
                    None, ALU.mult))
                p.step("vector", lambda e, bank_n=bank_n: e.tensor_tensor(
                    self.sm[0][:, 256:320], self.sm[0][:, 256:320], bank_n[:, 0:64], ALU.add))
                otok = p.step("vector", lambda e, tt=tt, h=h: e.tensor_scalar(
                    self.och[:, h * 64:(h + 1) * 64], self.sm[0][:, 256:320],
                    self.sst_f[:, 768 + tt * 16 + h:769 + tt * 16 + h], None, ALU.mult))
                self.pool.release(bi_n, otok)
                self.pool.release(bi_n2, otok)
                p.step("vector", lambda e, tt=tt, h=h: e.tensor_scalar(
                    self.sm[0][:, 320:384], k_tm[:, tt * 1024 + h * 64: tt * 1024 + (h + 1) * 64],
                    st[:, 224 + h:225 + h], None, ALU.mult))
                ufl = self.vflush()
                p.wait("tensor", ufl)
                bi_u, bank_u = self.pool.acquire()
                p.step("tensor", lambda e, tt=tt, h=h, b0=b0, bank_u=bank_u: e.matmul(
                    bank_u[b0:b0 + 64, 0:64], self.sm[0][:, 320:384],
                    v_tm[:, tt * 1024 + h * 64: tt * 1024 + (h + 1) * 64], start=True, stop=True))
                p.wait("vector", p.tok("T"))
                p.step("vector", lambda e, b0=b0, hb=hb, h=h, bank_u=bank_u: e.scalar_tensor_tensor(
                    self.sst_f[b0:b0 + 64, hb * 64:(hb + 1) * 64],
                    self.sst_f[b0:b0 + 64, hb * 64:(hb + 1) * 64],
                    st[b0:b0 + 64, 208 + h:209 + h],
                    bank_u[b0:b0 + 64, 0:64], ALU.mult, ALU.add))
                cfl = p.step("vector", lambda e, b0=b0, hb=hb: e.tensor_copy(
                    self.sst_b[b0:b0 + 64, hb * 64:(hb + 1) * 64],
                    self.sst_f[b0:b0 + 64, hb * 64:(hb + 1) * 64]))
                self.pool.release(bi_u, cfl)
            fl = self.vflush()
            p.wait("sync", fl)
            p.dma("sync", o_dram[tt * 128:(tt + 1) * 128, :], self.och[:, 0:1024])
        self.dump("l3_o", self.och[:, 0:1024])

        for ob in range(8):
            build_wT_l3(w_out, ob * 128, 1, k_tm[:, ob * 1024: ob * 1024 + 1024],
                        self.sm[0][:, 0:1024])
        woutT = k_tm.rearrange("pp (ob eb j) -> pp ob eb j", eb=8, j=128)
        for tt in range(NT):
            p.barrier("sync", ("T", "V"))
            ltok = p.dma("sync", self.och[:, 0:1024], o_dram[tt * 128:(tt + 1) * 128, :])
            xtok = p.dma("sync", self.sm[0][:, 0:1024], xnd[tt * 128:(tt + 1) * 128, :])
            p.wait("vector", xtok)
            dstx = self.xres[:, tt * 1024:(tt + 1) * 1024]
            p.step("vector", lambda e, dstx=dstx: e.tensor_tensor(
                dstx, dstx, self.sm[0][:, 0:1024], ALU.add))
            p.wait("tensor", ltok)
            for eb in range(8):
                self.transpose(self.och[:, eb * 128:(eb + 1) * 128],
                               self.och[:, 1024 + eb * 128: 1024 + (eb + 1) * 128])
            p.wait("tensor", p.tok("A"))
            for oc in range(2):
                bi, bank = self.pool.acquire()
                for eb in range(8):
                    p.emit("tensor", lambda e, eb=eb, oc=oc, bank=bank: e.matmul(
                        bank[:, 0:512], self.och[:, 1024 + eb * 128: 1024 + (eb + 1) * 128],
                        woutT[:, oc * 4:(oc + 1) * 4, eb, :],
                        start=(eb == 0), stop=(eb == 7)), inc="T" if eb == 7 else None)
                mtok = p.tok("T")
                p.wait("vector", mtok)
                dst = self.xres[:, tt * 1024 + oc * 512: tt * 1024 + (oc + 1) * 512]
                etok = p.step("vector", lambda e, dst=dst, bank=bank: e.tensor_tensor(
                    dst, dst, bank[:, 0:512], ALU.add))
                self.pool.release(bi, etok)
        self.dump("l3_xres", self.xres[:, :])

    def emit_l3_ffn_gate(self):
        p = self.p
        st = self.stats
        w1 = self.din("L3_ffn_W1", [1024, 512])
        b1 = self.din("L3_ffn_b1", [512])
        w2 = self.din("L3_ffn_W2", [512, 8])
        b2 = self.din("L3_ffn_b2", [8])
        p.barrier("sync", ("T", "V", "A"))
        b1tok = p.dma("sync", self.rowf[0:1, 768:1280], b1.rearrange("f -> () f"))
        b2tok = p.dma("sync", self.rowf[0:1, 1280:1288], b2.rearrange("f -> () f"))
        w2tok = p.dma("sync", st[:, 640:672].rearrange("pp (fb o) -> pp fb o", o=8),
                      w2.rearrange("(fb pp) o -> pp fb o", pp=128))
        p.wait("vector", p.tok("Ds"))
        p.step("vector", lambda e: e.tensor_scalar(st[:, 672:680], self.c_iota8[:, :],
                                                   1e-6, None, ALU.mult))
        bi, bank = self.pool.acquire()
        for db in range(ND):
            wtok = p.dma("sync", self.sst_f[:, 128:640], w1[db * 128:(db + 1) * 128, :])
            p.wait("tensor", wtok)
            mt = p.emit("tensor", lambda e, db=db, bank=bank: e.matmul(
                bank[0:1, 0:512], self.c_ones_colf[:, :], self.sst_f[:, 128:640],
                start=(db == 0), stop=(db == 7)), inc="T")
            p.wait("sync", mt)
        p.wait("vector", p.tok("T"))
        cse = p.step("vector", lambda e, bank=bank: e.tensor_copy(
            self.rowf[0:1, 256:768], bank[0:1, 0:512]))
        self.pool.release(bi, cse)
        self.vflush()
        for tt in range(NT):
            bi, bank = self.pool.acquire()
            for db in range(ND):
                wtok = p.dma("sync", self.sst_f[:, 128:640], w1[db * 128:(db + 1) * 128, :])
                fl = self.transpose_f32(
                    self.xres[:, tt * 1024 + db * 128: tt * 1024 + (db + 1) * 128],
                    self.sst_f[:, 0:128])
                p.wait("tensor", fl)
                p.wait("tensor", wtok)
                mt = p.emit("tensor", lambda e, db=db, bank=bank: e.matmul(
                    bank[:, 0:512], self.sst_f[:, 0:128], self.sst_f[:, 128:640],
                    start=(db == 0), stop=False), inc="T")
                p.wait("sync", mt)
            p.step("vector", lambda e, tt=tt: e.tensor_scalar(
                st[:, 728:729], st[:, 32 + tt:33 + tt], -1.0, None, ALU.mult))
            p.wait("tensor", p.tok("V"))
            t1 = self.transpose_f32(st[:, 728:729], self.rowf[0:1, 0:128])
            t2 = self.transpose_f32(st[:, 96 + tt:97 + tt], self.rowf[0:1, 128:256])
            p.wait("tensor", t1)
            p.wait("tensor", t2)
            p.wait("tensor", b1tok)
            p.emit("tensor", lambda e, bank=bank: e.matmul(
                bank[:, 0:512], self.rowf[0:1, 0:128], self.rowf[0:1, 256:768],
                start=False, stop=False))
            p.emit("tensor", lambda e, bank=bank: e.matmul(
                bank[:, 0:512], self.rowf[0:1, 128:256], self.rowf[0:1, 768:1280],
                start=False, stop=True), inc="T")
            mtok = p.tok("T")
            p.wait("vector", mtok)
            p.step("vector", lambda e, tt=tt, bank=bank: e.tensor_scalar(
                self.sst_f[:, 640:1152], bank[:, 0:512], st[:, 112 + tt:113 + tt],
                0.0, ALU.mult, ALU.max))
            h1fl = self.vflush()
            self.pool.release(bi, h1fl)
            p.wait("tensor", h1fl)
            p.wait("tensor", w2tok)
            bi, bank = self.pool.acquire()
            for fb in range(4):
                fl = self.transpose_f32(self.sst_f[:, 640 + fb * 128:640 + (fb + 1) * 128],
                                        self.sst_f[:, 0:128])
                p.wait("tensor", fl)
                p.emit("tensor", lambda e, fb=fb, bank=bank: e.matmul(
                    bank[:, 0:8], self.sst_f[:, 0:128], st[:, 640 + fb * 8:648 + fb * 8],
                    start=(fb == 0), stop=False))
            p.wait("tensor", b2tok)
            p.emit("tensor", lambda e, bank=bank: e.matmul(
                bank[:, 0:8], self.c_ones_rowf[:, :], self.rowf[0:1, 1280:1288],
                start=False, stop=True), inc="T")
            mtok = p.tok("T")
            p.wait("vector", mtok)
            p.step("vector", lambda e, bank=bank: e.tensor_tensor(
                st[:, 680:688], bank[:, 0:8], st[:, 672:680], ALU.subtract))
            p.step("vector", lambda e: e.reduce_max(st[:, 688:689], st[:, 680:688], AX.X))
            cfe = p.step("vector", lambda e, tt=tt: e.tensor_scalar(
                st[:, 512 + tt * 8:520 + tt * 8], st[:, 680:688], st[:, 688:689],
                None, ALU.is_equal))
            self.pool.release(bi, cfe)
        self.vflush()
        self.dump("l3_coef", st[:, 512:640])

    def emit_l3_ffn_exp(self):
        p = self.p
        st = self.stats
        ew = self.din("L3_ffn_expert_W", [8, 1024, 1024])
        eb = self.din("L3_ffn_expert_b", [8, 1024])
        p.barrier("gpsimd", ("T", "V", "A"))
        ebtok = p.dma("gpsimd", self.sst_b[0:8, 0:1024], eb[:, :])
        for e_ in range(8):
            p.barrier("gpsimd", ("T",))
            wtok = p.dma("gpsimd", self.scrb[:, 0:8192].rearrange("pp (db f) -> pp db f", f=1024),
                         ew[e_, :, :].rearrange("(db pp) f -> pp db f", pp=128))
            p.wait("tensor", wtok)
            if e_ == 0:
                p.wait("tensor", ebtok)
            for tt in range(NT):
                if e_ == 0:
                    p.step("vector", lambda e, tt=tt: e.tensor_copy(
                        self.sm[1][:, 0:8], st[:, 512 + tt * 8:520 + tt * 8]))
                    fl = self.vflush()
                    p.wait("tensor", fl)
                    self.transpose(self.sm[1][:, 0:8], None, evac="vector",
                                   evac_fn=lambda pt_ap: p.step("vector",
                                       lambda e, pt_ap=pt_ap: e.tensor_copy(
                                           self.sm[1][0:8, 128:256], pt_ap)))
                for oc in range(2):
                    bi, bank = self.pool.acquire()
                    for db in range(ND):
                        p.emit("tensor", lambda e, db=db, tt=tt, oc=oc, bank=bank: e.matmul(
                            bank[:, 0:512],
                            self.xn_fm[:, db * 2048 + tt * 128:db * 2048 + (tt + 1) * 128],
                            self.scrb[:, db * 1024 + oc * 512:db * 1024 + (oc + 1) * 512],
                            start=(db == 0), stop=(db == 7)), inc="T" if db == 7 else None)
                    mtok = p.tok("T")
                    p.wait("vector", mtok)
                    sc_ap = st[:, 512 + tt * 8 + e_:513 + tt * 8 + e_]
                    dst = self.xres[:, tt * 1024 + oc * 512: tt * 1024 + (oc + 1) * 512]
                    etok = p.step("vector", lambda e, dst=dst, bank=bank, sc_ap=sc_ap:
                                  e.scalar_tensor_tensor(dst, bank[:, 0:512], sc_ap, dst,
                                                         ALU.mult, ALU.add))
                    self.pool.release(bi, etok)
                if e_ == 0:
                    for oc in range(2):
                        bi, bank = self.pool.acquire()
                        p.step("tensor", lambda e, oc=oc, bank=bank: e.matmul(
                            bank[:, 0:512], self.sm[1][0:8, 128:256],
                            self.sst_b[0:8, oc * 512:(oc + 1) * 512], start=True, stop=True))
                        mtok = p.tok("T")
                        p.wait("vector", mtok)
                        dst = self.xres[:, tt * 1024 + oc * 512: tt * 1024 + (oc + 1) * 512]
                        etok = p.step("vector", lambda e, dst=dst, bank=bank: e.tensor_tensor(
                            dst, dst, bank[:, 0:512], ALU.add))
                        self.pool.release(bi, etok)
        self.dump("l3f_xres", self.xres[:, :])



def build_layers(layers, dbg=False):
    nc = bass.Bass("TRN2", target_bir_lowering=False)
    p = Prog()
    mk = MK(nc, p, dbg=dbg)
    stk = ExitStack()
    with stk:
        mk.alloc(stk)
        x_in = mk.din("x_in", [T, 1024])
        y_out = mk.dout("y_out", [T, 1024])
        mk.load_consts()
        ld = mk.emit_load_x(x_in)
        for eng in ("tensor", "vector", "scalar"):
            p.wait_all(eng, "Ds")
            p.wait_all(eng, "Dg")
        for li in layers:
            if li == 0:
                ag16i, ag16o = mk.allgather("ag_x16", 16, 1024, BF16)
                agt = mk.emit_ln(xres_tok=ld, ag16=ag16i)
                ag16_tok = mk.ag_go(ag16i, ag16o, agt)
                ctx = mk.emit_l0_attn_pre()
                mk.emit_l0_attn_post(ctx)
                mk.emit_l0_ffn(ag16o, ag16_tok)
            elif li == 1:
                mk.emit_ln(xres_tok=ld)
                mk.emit_l1()
                mk.emit_l1_ffn()
            elif li == 2:
                mk.emit_ln(xres_tok=ld)
                mk.emit_l2()
                mk.emit_l2_ffn()
            elif li == 3:
                xnd = mk.dint("xn3_tm", [T, 1024], BF16)
                bdi, bdo = mk.allgather("ag_l3b", 1, 1024, BF16)
                mk.emit_ln(xres_tok=ld, write_dram=xnd)
                mk.emit_l3_ffn_gate()
                mk.emit_l3(xnd, bdi, bdo)
                mk.emit_l3_ffn_exp()
        mk.emit_store_x(y_out)
        p.realize(nc, stk)
    return nc, mk


def shard_inputs(full, core):
    return {}


_BUILD_CACHE = {}


def _get_prog(key, layers):
    if key not in _BUILD_CACHE:
        _BUILD_CACHE[key] = build_layers(layers)
    return _BUILD_CACHE[key]


def kernel(**inputs):
    consts = host_consts()
    x = np.ascontiguousarray(inputs["L0_attn_x"].reshape(16384, 1024))
    stages = [[0, 1, 2, 3]]
    cur = x
    for si, layers in enumerate(stages):
        nc, mk = _get_prog(tuple(layers), layers)
        in_maps = []
        fv = np.zeros((128, 64), np.float32)
        fv[:, :len(mk.fvals)] = np.array(mk.fvals, np.float32)[None, :]
        for c in range(NCORE):
            m = {"x_in": np.ascontiguousarray(cur[c * T:(c + 1) * T])}
            rm = np.zeros((128, 4), np.float32)
            rm[:, :c % 4] = 1.0
            rm8 = np.zeros((128, 8), np.float32)
            rm8[:, :c] = 1.0
            for name in mk.input_names:
                if name == "c_fvals":
                    m[name] = fv
                elif name == "c_rankmask":
                    m[name] = rm
                elif name == "c_rankmask8":
                    m[name] = rm8
                elif name == "c_prevsel":
                    pm = np.zeros((128, 4), np.float32)
                    if c % 4 > 0:
                        pm[:, c % 4 - 1] = 1.0
                    m[name] = pm
                elif name == "c_groupsel":
                    gs = np.zeros((128, 4), np.float32)
                    gs[:, c % 4] = 1.0
                    m[name] = gs
                elif name in consts:
                    m[name] = consts[name]
                elif name in inputs:
                    a = inputs[name]
                    if a.ndim == 0:
                        a = a.reshape(1)
                    m[name] = np.ascontiguousarray(a)
                elif name == "x_in":
                    pass
                else:
                    raise KeyError(name)
            in_maps.append(m)
        res = run_bass_kernel_spmd(nc, in_maps, core_ids=list(range(NCORE)))
        cur = np.concatenate([res.results[c]["y_out"] for c in range(NCORE)], axis=0)
    return cur.reshape(2, 8192, 1024)


# version 03
# speedup vs baseline: 1.3654x; 1.3535x over previous
import math
from contextlib import ExitStack

import numpy as np
import ml_dtypes
import concourse.bass as bass
import concourse.mybir as mybir
from concourse.bass_utils import run_bass_kernel_spmd

F32 = mybir.dt.float32
BF16 = mybir.dt.bfloat16
AF = mybir.ActivationFunctionType
ALU = mybir.AluOpType
AX = mybir.AxisListType

NCORE = 8
T = 2048
NT = 16
D = 1024
ND = 8
BF = ml_dtypes.bfloat16
RG = [[0, 1, 2, 3], [4, 5, 6, 7]]

ENGS = ("tensor", "vector", "scalar", "sync", "gpsimd")
SEMS = {"T": 1, "V": 1, "A": 1, "G": 1, "Dg": 16, "Ds": 16, "Da": 16, "C": 1}
ENG_SEM = {"tensor": "T", "vector": "V", "scalar": "A", "gpsimd": "G"}
ENG_DMA_SEM = {"gpsimd": "Dg", "sync": "Ds", "scalar": "Da"}


STRICT = True
STRICT_ENGS = ("vector", "scalar")
STRICT_SEMS = tuple(SEMS)
STRICT_SKIP_OWN = False


class Prog:
    def __init__(self):
        self.q = {e: [] for e in ENGS}
        self.cnt = {s: 0 for s in SEMS}
        self._lastw = {e: {s: -1 for s in SEMS} for e in ENGS}

    def emit(self, eng, fn, inc=None):
        if STRICT and eng in STRICT_ENGS:
            lw = self._lastw[eng]
            own = ENG_SEM.get(eng) if STRICT_SKIP_OWN else None
            for s2 in STRICT_SEMS:
                if s2 != own and self.cnt[s2] > lw[s2]:
                    self.q[eng].append(("w", s2, self.cnt[s2]))
                    lw[s2] = self.cnt[s2]
            if inc is None:
                inc = ENG_SEM.get(eng)
                if eng in ("sync",):
                    inc = None
        tok = None
        if inc is not None:
            self.cnt[inc] += SEMS[inc]
            tok = (inc, self.cnt[inc])
        self.q[eng].append(("i", fn, inc))
        return tok

    def step(self, eng, fn):
        return self.emit(eng, fn, inc=ENG_SEM[eng])

    def dma(self, eng, out, in_, **kw):
        return self.emit(eng, lambda e: e.dma_start(out=out, in_=in_, **kw),
                         inc=ENG_DMA_SEM[eng])

    def wait(self, eng, tok):
        if tok is None:
            return
        sem, val = tok
        assert val <= self.cnt[sem], f"wait on future {sem}={val} > {self.cnt[sem]}"
        if val > self._lastw[eng][sem]:
            self.q[eng].append(("w", sem, val))
            self._lastw[eng][sem] = val

    def wait_all(self, eng, sem):
        if self.cnt[sem] > self._lastw[eng][sem]:
            self.q[eng].append(("w", sem, self.cnt[sem]))
            self._lastw[eng][sem] = self.cnt[sem]

    def barrier(self, eng, sems=("T", "V", "A")):
        for s in sems:
            self.wait_all(eng, s)

    def tok(self, sem):
        return (sem, self.cnt[sem])

    def realize(self, nc, stk):
        semh = {s: stk.enter_context(nc.semaphore(f"sem_{s}")) for s in SEMS}
        blk = stk.enter_context(nc.Block())
        for eng in ENGS:
            items = self.q[eng]

            def make(eng_items):
                def body(e):
                    for it in eng_items:
                        if it[0] == "w":
                            e.wait_ge(semh[it[1]], it[2])
                        else:
                            inst = it[1](e)
                            if it[2] is not None:
                                inst.then_inc(semh[it[2]], SEMS[it[2]])
                return body

            getattr(blk, eng)(make(items))


class PsumPool:
    def __init__(self, p, banks):
        self.p = p
        self.banks = banks
        self.freetok = [None] * len(banks)
        self.i = 0

    def acquire(self, eng="tensor"):
        i = self.i
        self.i = (self.i + 1) % len(self.banks)
        self.p.wait(eng, self.freetok[i])
        return i, self.banks[i]

    def release(self, i, tok):
        self.freetok[i] = tok


def host_consts():
    c = {}
    c["c_ident"] = np.eye(128, dtype=BF)
    m = np.zeros((128, 2, 256), dtype=BF)
    for r in range(2):
        for pp in range(128):
            m[pp, r, :] = (np.arange(256) >= (r * 128 + pp))
    c["c_l1mask"] = np.ascontiguousarray(m.reshape(128, 512))
    s = np.zeros((128, 128), dtype=BF)
    for k in range(128):
        s[k, k + 1:] = 1
    c["c_l3strict"] = s
    tr = np.zeros((128, 128), dtype=BF)
    for k in range(128):
        tr[k, k:] = 1
    c["c_tril"] = tr
    c["c_ones_row"] = np.ones((1, 128), dtype=BF)
    c["c_ones_rowf"] = np.ones((1, 128), dtype=np.float32)
    c["c_ones_col"] = np.ones((128, 1), dtype=BF)
    c["c_iota8"] = np.tile(np.arange(8, dtype=np.float32)[None, :], (128, 1))
    c["c_identf"] = np.eye(128, dtype=np.float32)
    c["c_ones_colf"] = np.ones((128, 1), dtype=np.float32)
    c["c_trilf"] = np.tril(np.ones((128, 128), np.float32)).T.copy()
    return c


CONST_SPECS = {"c_ident": ([128, 128], BF16), "c_l1mask": ([128, 512], BF16),
               "c_l3strict": ([128, 128], BF16), "c_tril": ([128, 128], BF16),
               "c_ones_row": ([1, 128], BF16), "c_ones_rowf": ([1, 128], F32),
               "c_ones_col": ([128, 1], BF16), "c_iota8": ([128, 8], F32),
               "c_identf": ([128, 128], F32), "c_ones_colf": ([128, 1], F32),
               "c_trilf": ([128, 128], F32)}


def bcast_free(ap, n):
    return bass.AP(ap.tensor, ap.offset, list(ap.ap) + [[0, n]])


class MK:
    def __init__(self, nc, p, dbg=False):
        self.nc = nc
        self.p = p
        self.dbg = dbg
        self.dbg_outs = []
        self.input_names = []
        self._tp_i = 0
        self.fvals = []
        self._fmap = {}
        self._din_c = {}
        self._tpf_i = 0

    def fconst(self, v, dt=None):
        for d in ([dt] if dt is not None else [F32, BF16]):
            key = (d, float(v))
            if key not in self._fmap:
                if float(v) not in self.fvals:
                    self.fvals.append(float(v))
                i = self.fvals.index(float(v))
                ap = self.cvals[:, i:i + 1] if d == F32 else self.cvalsb[:, i:i + 1]
                self.nc.const_aps.register(d, float(v), ap)
                self._fmap[key] = True
        return float(v)

    def din(self, name, shape, dtype=F32):
        if name in self._din_c:
            return self._din_c[name]
        t = self.nc.dram_tensor(name, list(shape), dtype, kind="ExternalInput")
        self.input_names.append(name)
        self._din_c[name] = t.ap()
        return self._din_c[name]

    def dout(self, name, shape, dtype=F32):
        return self.nc.dram_tensor(name, list(shape), dtype, kind="ExternalOutput").ap()

    def dint(self, name, shape, dtype=F32):
        return self.nc.dram_tensor(name, list(shape), dtype, kind="Internal").ap()

    def dump(self, name, sb_ap, dtype=None):
        if not self.dbg:
            return
        p = self.p
        shape = [d[1] for d in sb_ap.ap]
        dt = dtype or sb_ap.dtype
        o = self.dout("dbg_" + name, shape, dt)
        self.dbg_outs.append("dbg_" + name)
        eng = "sync" if dt == sb_ap.dtype else "gpsimd"
        p.barrier(eng, ("T", "V", "A", "G", "Dg", "Ds", "Da", "C"))
        tok = p.dma(eng, o, sb_ap)
        for e2 in ("tensor", "vector", "scalar", "sync", "gpsimd"):
            p.wait(e2, tok)

    def alloc(self, stk):
        nc = self.nc
        S = lambda name, shape, dt: stk.enter_context(nc.sbuf_tensor(name, shape, dt))
        self.xres = S("xres", [128, NT * 1024], F32)
        self.xn_fm = S("xn_fm", [128, ND * 2048], BF16)
        self.scra = S("scra", [128, 16384], BF16)
        self.scrb = S("scrb", [128, 16384], BF16)
        self.stats = S("stats", [128, 768], F32)
        self.rowb = S("rowb", [1, 5120], BF16)
        self.rowf = S("rowf", [1, 2048], F32)
        self.wbuf = S("wbuf", [128, 4096], BF16)
        self.sm = [S("sm1", [128, 1024], BF16), S("sm2", [128, 1024], BF16)]
        self.sst_f = S("sst_f", [128, 1152], F32)
        self.sst_b = S("sst_b", [128, 1024], BF16)
        self.och = S("och", [128, 2048], BF16)
        for name, (shape, dt) in CONST_SPECS.items():
            setattr(self, name, S(name + "_sb", shape, dt))
        self.cvals = S("cvals", [128, 64], F32)
        self.cvalsb = S("cvalsb", [128, 64], BF16)
        P = lambda name, shape, dt: stk.enter_context(nc.psum_tensor(name, shape, dt))
        self.pb = [P(f"pb{i}", [128, 512], F32) for i in range(6)]
        self.pt = [P(f"pt{i}", [128, 1024], BF16) for i in range(2)]
        self.pt_tok = [None, None]
        self.pool = PsumPool(self.p, self.pb[:5])
        self.ptf = self.pb[5]
        self.ptf_tok = [None, None]

    def load_consts(self):
        for name in CONST_SPECS:
            shape, dt = CONST_SPECS[name]
            ap = self.din(name, shape, dt)
            self.p.dma("sync", getattr(self, name)[:, :], ap[:, :])
        fv = self.din("c_fvals", [128, 64], F32)
        self.p.dma("sync", self.cvals[:, :], fv[:, :])
        self.p.dma("gpsimd", self.cvalsb[:, :], fv[:, :])
        return self.p.tok("Ds")

    def transpose(self, src, dst=None, evac="scalar", evac_fn=None, out_base=0):
        p = self.p
        pp = src.ap[0][1]
        ff = src.ap[-1][1]
        b = self._tp_i
        self._tp_i = 1 - b
        p.wait("tensor", self.pt_tok[b])
        p.step("tensor", lambda e: e.transpose(self.pt[b][out_base:out_base + ff, 0:pp], src,
                                               self.c_ident[0:pp, 0:pp]))
        ttok = p.tok("T")
        pt_ap = self.pt[b][out_base:out_base + ff, 0:pp]
        p.wait(evac, ttok)
        if evac_fn is not None:
            etok = evac_fn(pt_ap)
        elif evac == "scalar":
            etok = p.step("scalar", lambda e: e.copy(dst, pt_ap))
        else:
            etok = p.step("vector", lambda e: e.tensor_copy(dst, pt_ap))
        self.pt_tok[b] = etok
        return etok

    def vflush(self):
        return self.p.step("vector", lambda e: e.tensor_copy(
            self.stats[0:1, 767:768], self.cvals[0:1, 0:1]))

    def transpose_f32(self, src, dst, evac="vector"):
        p = self.p
        pp = src.ap[0][1]
        ff = src.ap[-1][1]
        b = self._tpf_i
        self._tpf_i = 1 - b
        p.wait("tensor", self.ptf_tok[b])
        bank_ap = self.ptf[0:ff, b * 128:b * 128 + pp]
        p.step("tensor", lambda e: e.transpose(bank_ap, src, self.c_identf[0:pp, 0:pp]))
        ttok = p.tok("T")
        p.wait(evac, ttok)
        if evac == "scalar":
            etok = p.step("scalar", lambda e: e.copy(dst, bank_ap))
        else:
            etok = p.step("vector", lambda e: e.tensor_copy(dst, bank_ap))
        self.ptf_tok[b] = etok
        return etok

    def gate_stage1_fp32(self, tt, rhs_fn, nout, cs_row, svb_row):
        p = self.p
        st = self.stats
        p.step("vector", lambda e, tt=tt: e.tensor_scalar(
            st[:, 720 + 0:721 + 0], st[:, 32 + tt:33 + tt], -1.0, None, ALU.mult))
        p.wait("tensor", p.tok("V"))
        t1 = self.transpose_f32(st[:, 720:721], self.rowf[0:1, 0:128])
        t2 = self.transpose_f32(st[:, 96 + tt:97 + tt], self.rowf[0:1, 128:256])
        p.wait("tensor", t1)
        p.wait("tensor", t2)
        bi, bank = self.pool.acquire()
        for db in range(ND):
            fl = self.transpose_f32(self.xres[:, tt * 1024 + db * 128: tt * 1024 + (db + 1) * 128],
                                    self.sst_f[:, 0:128])
            p.wait("tensor", fl)
            p.emit("tensor", lambda e, db=db, bank=bank, rhs=rhs_fn(db), nout=nout: e.matmul(
                bank[:, 0:nout], self.sst_f[:, 0:128], rhs, start=(db == 0), stop=False))
        p.emit("tensor", lambda e, bank=bank, nout=nout, cs_row=cs_row: e.matmul(
            bank[:, 0:nout], self.rowf[0:1, 0:128], cs_row,
            start=False, stop=(svb_row is None)), inc="T" if svb_row is None else None)
        if svb_row is not None:
            p.emit("tensor", lambda e, bank=bank, nout=nout, svb_row=svb_row: e.matmul(
                bank[:, 0:nout], self.rowf[0:1, 128:256], svb_row,
                start=False, stop=True), inc="T")
        return bi, bank, p.tok("T")

    def allgather(self, name, rows, cols, dtype=F32):
        i = self.dint(name + "_in", [rows, cols], dtype)
        o = self.dint(name + "_out", [rows * 4, cols], dtype)
        return i, o

    def ag_go(self, in_ap, out_ap, fill_tok, kind="AllGather", groups=None):
        p = self.p
        p.wait("gpsimd", fill_tok)
        op = ALU.bypass if kind == "AllGather" else ALU.add
        rg = groups if groups is not None else RG
        return p.emit("gpsimd", lambda g: g.collective_compute(
            kind, op, replica_groups=rg,
            ins=[in_ap[:, :]], outs=[out_ap[:, :]]), inc="C")

    def emit_load_x(self, x_in):
        return self.p.dma("sync", self.xres.rearrange("p (t d) -> p t d", d=1024),
                          x_in.rearrange("(t p) d -> p t d", p=128))

    def emit_store_x(self, y_out):
        p = self.p
        p.barrier("sync", ("V", "T", "A"))
        tok = p.dma("sync", y_out.rearrange("(t p) d -> p t d", p=128),
                    self.xres.rearrange("p (t d) -> p t d", d=1024))
        p.wait("sync", tok)

    def emit_ln(self, xres_tok=None, write_dram=None, ag16=None):
        p = self.p
        st = self.stats
        if xres_tok is not None:
            p.wait("vector", xres_tok)
            p.wait("scalar", xres_tok)
        self.dump("xres_in", self.xres[:, :])
        dumps = [self.sm[0][:, :], self.sm[1][:, :],
                 self.wbuf[:, 0:1024], self.wbuf[:, 1024:2048]]
        for t in range(NT):
            xs = self.xres[:, t * 1024:(t + 1) * 1024]
            p.step("vector", lambda e, xs=xs, t=t: e.tensor_reduce(
                st[:, t:t + 1], xs, AX.X, ALU.add))
            p.step("scalar", lambda e, xs=xs, t=t: e.activation(
                dumps[t % 4], xs, AF.Square, accum_out=st[:, 16 + t:17 + t]))
        p.wait("vector", p.tok("A"))
        p.step("vector", lambda e: e.tensor_scalar(st[:, 32:48], st[:, 0:16], 1.0 / 1024, None, ALU.mult))
        p.step("vector", lambda e: e.tensor_scalar(st[:, 48:64], st[:, 16:32], 1.0 / 1024, None, ALU.mult))
        p.step("vector", lambda e: e.tensor_tensor(st[:, 64:80], st[:, 32:48], st[:, 32:48], ALU.mult))
        p.step("vector", lambda e: e.tensor_tensor(st[:, 80:96], st[:, 48:64], st[:, 64:80], ALU.subtract))
        p.wait("scalar", p.tok("V"))
        p.step("scalar", lambda e: e.activation(st[:, 96:112], st[:, 80:96], AF.Sqrt, bias=self.fconst(1e-5)))
        p.wait("vector", p.tok("A"))
        p.step("vector", lambda e: e.reciprocal(st[:, 112:128], st[:, 96:112]))
        sm_war = [p.tok("A"), p.tok("A")]
        ag_tok = None
        for t in range(NT):
            sm = self.sm[t % 2]
            p.wait("vector", sm_war[t % 2])
            p.step("vector", lambda e, t=t, sm=sm: e.tensor_scalar(
                sm[:, :], self.xres[:, t * 1024:(t + 1) * 1024],
                st[:, 32 + t:33 + t], st[:, 112 + t:113 + t], ALU.subtract, ALU.mult))
            vt = self.vflush()
            if t == 0 and ag16 is not None:
                p.wait("gpsimd", vt)
                ag_tok = p.dma("gpsimd", ag16[:, :], sm[1:17, :])
            if write_dram is not None:
                p.wait("sync", vt)
                p.dma("sync", write_dram[t * 128:(t + 1) * 128, :], sm[:, :])
            p.wait("tensor", vt)
            if t == 0 and ag16 is not None:
                p.wait("tensor", ag_tok)
            last = None
            for db in range(ND):
                last = self.transpose(
                    sm[:, db * 128:(db + 1) * 128],
                    self.xn_fm[:, db * 2048 + t * 128: db * 2048 + (t + 1) * 128])
            sm_war[t % 2] = last
            if write_dram is not None:
                sm_war[t % 2] = None
        if write_dram is not None:
            p.wait_all("vector", "Ds")
        self.dump("stats_ln", self.stats[:, :])
        self.dump("xnfm", self.xn_fm[:, :])
        return ag_tok

    def emit_l0_attn_pre(self):
        p = self.p
        wq = self.din("L0_attn_qkv_w", [1024, 3072])
        bq = self.din("L0_attn_qkv_b", [3072])
        wp = self.din("L0_attn_proj_w", [1024, 1024])
        bp = self.din("L0_attn_proj_b", [1024])
        agi = self.dint("ag_l0kv_in", [64, 1024])
        ago = self.dint("ag_l0kv_out", [64, 1024])
        p.barrier("gpsimd", ("A", "V"))
        bq_tok = p.dma("gpsimd", self.rowb[0:1, 0:3072], bq.rearrange("f -> () f"))
        xn_tok = p.tok("A")

        k_tm = self.scra
        v_tm = self.scrb
        wslab = self.wbuf.rearrange("p (db f) -> p db f", f=512)

        def project_tm(fcs, dest):
            for fc in fcs:
                p.wait_all("gpsimd", "A")
                wtok = p.dma("gpsimd", wslab,
                             wq[:, fc * 512:(fc + 1) * 512].rearrange("(db p) f -> p db f", p=128))
                p.wait("tensor", wtok)
                p.wait("tensor", bq_tok)
                for tt in range(NT):
                    bi, bank = self.pool.acquire()
                    for db in range(ND):
                        p.emit("tensor", lambda e, db=db, tt=tt, bank=bank: e.matmul(
                            bank[:, 0:512],
                            self.xn_fm[:, db * 2048 + tt * 128:db * 2048 + (tt + 1) * 128],
                            wslab[:, db, :], start=(db == 0), stop=False))
                    p.step("tensor", lambda e, fc=fc, bank=bank: e.matmul(
                        bank[:, 0:512], self.c_ones_row[:, :],
                        self.rowb[0:1, fc * 512:(fc + 1) * 512], start=False, stop=True))
                    mtok = p.tok("T")
                    p.wait("scalar", mtok)
                    etok = p.step("scalar", lambda e, d=dest(fc, tt), bank=bank: e.copy(d, bank[:, 0:512]))
                    self.pool.release(bi, etok)

        def l2norm_tiles(buf):
            st = self.stats
            p.wait_all("scalar", "A")
            for t in range(NT):
                ks = buf[:, t * 1024:(t + 1) * 1024]
                sm = self.sm[t % 2]
                c0 = 128 + 64 * (t % 2)
                p.wait_all("scalar", "V")
                for h in range(16):
                    p.step("scalar", lambda e, ks=ks, sm=sm, h=h, c0=c0: e.activation(
                        sm[:, h * 64:(h + 1) * 64],
                        bass.AP(ks.tensor, ks.offset + h * 64, [list(ks.ap[0]), [1, 64]]),
                        AF.Square, accum_out=st[:, c0 + h:c0 + h + 1]))
                p.wait("scalar", p.tok("A"))
                p.step("scalar", lambda e, c0=c0: e.activation(
                    st[:, c0 + 16:c0 + 32], st[:, c0:c0 + 16], AF.Sqrt))
                p.wait("vector", p.tok("A"))
                p.step("vector", lambda e, c0=c0: e.tensor_scalar(
                    st[:, c0 + 16:c0 + 32], st[:, c0 + 16:c0 + 32], 1e-12, None, ALU.max))
                p.step("vector", lambda e, c0=c0: e.reciprocal(
                    st[:, c0 + 32:c0 + 48], st[:, c0 + 16:c0 + 32]))
                p.step("vector", lambda e, ks=ks, sm=sm, c0=c0: e.tensor_tensor(
                    sm.rearrange("p (h d) -> p h d", d=64),
                    ks.rearrange("p (h d) -> p h d", d=64),
                    bcast_free(st[:, c0 + 32:c0 + 48], 64), ALU.mult))
                p.step("vector", lambda e, ks=ks, sm=sm: e.tensor_copy(ks, sm[:, :]))
            self.vflush()

        p.wait("tensor", xn_tok)
        project_tm([2, 3], lambda fc, tt: k_tm[:, tt * 1024 + (fc - 2) * 512: tt * 1024 + (fc - 1) * 512])
        project_tm([4, 5], lambda fc, tt: v_tm[:, tt * 1024 + (fc - 4) * 512: tt * 1024 + (fc - 3) * 512])
        l2norm_tiles(k_tm)
        self.dump("stats_l2", self.stats[:, :])
        self.dump("ktm", k_tm[:, :])
        self.dump("vtm", v_tm[:, :])

        p.wait_all("tensor", "V")
        for h in range(16):
            bi, bank = self.pool.acquire()
            for tt in range(NT):
                p.emit("tensor", lambda e, h=h, tt=tt, bank=bank: e.matmul(
                    bank[0:64, 0:64],
                    k_tm[:, tt * 1024 + h * 64: tt * 1024 + (h + 1) * 64],
                    v_tm[:, tt * 1024 + h * 64: tt * 1024 + (h + 1) * 64],
                    start=(tt == 0), stop=(tt == 15)),
                    inc="T" if tt == 15 else None)
            mtok = p.tok("T")
            p.wait("vector", mtok)
            etok = p.step("vector", lambda e, h=h, bank=bank: e.tensor_copy(
                self.sst_f[0:64, h * 64:(h + 1) * 64], bank[0:64, 0:64]))
            self.pool.release(bi, etok)
        fill = None
        p.wait("gpsimd", self.vflush())
        fill = p.dma("gpsimd", agi[:, :], self.sst_f[0:64, 0:1024])
        ag_tok = self.ag_go(agi, ago, fill, kind="AllReduce")

        project_tm([0, 1], lambda fc, tt: k_tm[:, tt * 1024 + fc * 512: tt * 1024 + (fc + 1) * 512])
        l2norm_tiles(k_tm)
        p.wait("tensor", p.tok("V"))
        for t in range(NT):
            for pbk in range(8):
                self.transpose(k_tm[:, t * 1024 + pbk * 128: t * 1024 + (pbk + 1) * 128],
                               self.scrb[:, pbk * 2048 + t * 128: pbk * 2048 + (t + 1) * 128])
        self.dump("qfm", self.scrb[:, :])
        p.barrier("gpsimd", ("T",))
        bp_tok = p.dma("gpsimd", self.rowb[0:1, 4096:5120], bp.rearrange("f -> () f"))
        return dict(ag=ag_tok, ago=ago, bp=bp_tok, wp=wp)

    def emit_l0_attn_post(self, ctx):
        p = self.p
        ago = ctx["ago"]
        p.wait("gpsimd", ctx["ag"])
        p.wait_all("gpsimd", "V")
        tok = p.dma("gpsimd", self.sst_f[0:64, 0:1024], ago[0:64, :])
        p.wait("sync", tok)
        self.dump("kvsum", self.sst_f[0:64, 0:1024])
        p.wait("vector", tok)
        p.step("vector", lambda e: e.memset(self.sst_b[:, 0:1024], 0.0))
        vw = lambda buf, p0, p1, sub: buf[p0:p1, 0:1024].rearrange(
            "p (h two c) -> p h two c", two=2, c=64)[:, :, sub, :]
        p.step("vector", lambda e: e.tensor_copy(
            vw(self.sst_b, 0, 64, 0), vw(self.sst_f, 0, 64, 0)))
        vtok = self.vflush()
        p.wait("gpsimd", vtok)
        kv2_tok = p.dma("gpsimd", vw(self.sst_b, 64, 128, 1), vw(self.sst_f, 0, 64, 1))
        p.wait("tensor", vtok)
        p.wait("tensor", kv2_tok)
        ao = self.scra
        for pr in range(8):
            for tc in range(4):
                bi, bank = self.pool.acquire()
                p.step("tensor", lambda e, pr=pr, tc=tc, bank=bank: e.matmul(
                    bank[:, 0:512], self.sst_b[:, pr * 128:(pr + 1) * 128],
                    self.scrb[:, pr * 2048 + tc * 512: pr * 2048 + (tc + 1) * 512],
                    start=True, stop=True))
                mtok = p.tok("T")
                p.wait("scalar", mtok)
                etok = p.step("scalar", lambda e, pr=pr, tc=tc, bank=bank: e.copy(
                    ao[:, pr * 2048 + tc * 512: pr * 2048 + (tc + 1) * 512], bank[:, 0:512]))
                self.pool.release(bi, etok)
        ao_tok = p.tok("A")
        wslab = self.wbuf.rearrange("p (db f) -> p db f", f=512)
        wp = ctx["wp"]
        p.wait("tensor", ao_tok)
        for oc in range(2):
            p.wait_all("gpsimd", "T")
            wtok = p.dma("gpsimd", wslab,
                         wp[:, oc * 512:(oc + 1) * 512].rearrange("(db p) f -> p db f", p=128))
            p.wait("tensor", wtok)
            p.wait("tensor", ctx["bp"])
            for tt in range(NT):
                bi, bank = self.pool.acquire()
                for cb in range(8):
                    p.emit("tensor", lambda e, cb=cb, tt=tt, bank=bank: e.matmul(
                        bank[:, 0:512], ao[:, cb * 2048 + tt * 128: cb * 2048 + (tt + 1) * 128],
                        wslab[:, cb, :], start=(cb == 0), stop=False))
                p.step("tensor", lambda e, oc=oc, bank=bank: e.matmul(
                    bank[:, 0:512], self.c_ones_row[:, :],
                    self.rowb[0:1, 4096 + oc * 512: 4096 + (oc + 1) * 512], start=False, stop=True))
                mtok = p.tok("T")
                p.wait("vector", mtok)
                dst = self.xres[:, tt * 1024 + oc * 512: tt * 1024 + (oc + 1) * 512]
                etok = p.step("vector", lambda e, dst=dst, bank=bank: e.tensor_tensor(
                    dst, dst, bank[:, 0:512], ALU.add))
                self.pool.release(bi, etok)

    def emit_l0_ffn(self, ago16, ag16_tok):
        p = self.p
        w1 = self.din("L0_ffn_W1", [4096, 1024])
        b1 = self.din("L0_ffn_b1", [4096])
        w2 = self.din("L0_ffn_W2", [1024, 4096])
        b2 = self.din("L0_ffn_b2", [1024])
        scale = self.din("L0_ffn_scale", [1])
        cdram = self.dint("l0c", [4, 4, 2048], BF16)
        hbuf = self.scra
        scrb = self.scrb
        wb = self.wbuf

        def scol(ap2d, start, step, count):
            return bass.AP(ap2d.tensor, ap2d.offset + start, [list(ap2d.ap[0]), [step, count]])

        sct = p.dma("sync", self.stats[0:1, 513:514], scale.rearrange("a -> () a"))
        p.wait("tensor", sct)
        bi, bank = self.pool.acquire()
        p.step("tensor", lambda e, bank=bank: e.matmul(
            bank[:, 0:1], self.c_ones_rowf[:, :], self.stats[0:1, 513:514], start=True, stop=True))
        p.wait("vector", p.tok("T"))
        et = p.step("vector", lambda e, bank=bank: e.tensor_copy(self.stats[:, 512:513], bank[:, 0:1]))
        self.pool.release(bi, et)

        p.barrier("gpsimd", ("T", "V", "A"))
        bt = p.dma("gpsimd", self.sm[0][0:32, 0:128], b1.rearrange("(a b) -> a b", b=128))
        p.wait("tensor", bt)
        self.transpose(self.sm[0][0:32, 0:128], evac="vector",
                       evac_fn=lambda pt: p.step("vector", lambda e, pt=pt: e.tensor_copy(
                           self.stats[:, 544:576], pt)))

        p.wait("gpsimd", ag16_tok)
        p.wait("sync", ag16_tok)
        ev1_tok = p.dma("gpsimd", self.rowb[0:1, 0:4096].rearrange("o (r f) -> o r f", r=4),
                        ago16.rearrange("(r two) f -> r two f", two=2)[0:4, 0, :])
        b2_tok = p.dma("gpsimd", self.rowb[0:1, 4096:5120], b2.rearrange("f -> () f"))
        p.barrier("sync", ("V", "T", "A"))
        x16_tok = p.dma("sync", wb[0:16, 2048:3072], ago16[0:16, :])
        p.wait("tensor", x16_tok)
        for db in range(ND):
            self.transpose(wb[0:16, 2048 + db * 128: 2048 + (db + 1) * 128],
                           wb[:, 3072 + db * 16: 3072 + (db + 1) * 16])
        xf_tok = p.tok("A")
        p.wait("sync", xf_tok)
        for i in range(4):
            st_tok = p.dma("sync", self.sm[1][0:128, 0:8].rearrange("p (h j) -> p h j", h=2),
                           ago16[8 + 2 * i, :].rearrange("(h p j) -> p h j", h=2, p=128))
            p.wait("tensor", st_tok)
            for h in range(2):
                self.transpose(self.sm[1][0:128, h * 4:(h + 1) * 4],
                               wb[0:4, 2048 + i * 256 + h * 128: 2048 + i * 256 + (h + 1) * 128])
            p.wait("sync", p.tok("T"))
        e2_tok = p.tok("A")

        p.wait("tensor", xf_tok)
        for tc in range(4):
            bi, bank = self.pool.acquire()
            for db in range(ND):
                lhs = scol(wb[:, 3072 + db * 16: 3072 + (db + 1) * 16], 1, 2, 4)
                p.emit("tensor", lambda e, lhs=lhs, db=db, tc=tc, bank=bank: e.matmul(
                    bank[0:4, 0:512], lhs,
                    self.xn_fm[:, db * 2048 + tc * 512: db * 2048 + (tc + 1) * 512],
                    start=(db == 0), stop=(db == 7)), inc="T" if db == 7 else None)
            mtok = p.tok("T")
            p.wait("vector", mtok)
            etok = p.step("vector", lambda e, tc=tc, bank=bank: e.tensor_scalar(
                wb[0:4, tc * 512:(tc + 1) * 512], bank[0:4, 0:512],
                self.stats[0:4, 512:513], None, ALU.mult))
            self.pool.release(bi, etok)
        p.wait("tensor", ev1_tok)
        p.wait("tensor", self.vflush())

        for j in range(4):
            if j >= 1:
                p.wait("gpsimd", p.tok("T"))
                at = p.dma("gpsimd", wb[0:1, 0:2048], wb[j:j + 1, 0:2048])
                p.wait("tensor", at)
            for hb in range(8):
                pbg = j * 8 + hb
                sb = hb % 2
                stage = scrb[:, 2048 + sb * 1024: 2048 + (sb + 1) * 1024]
                w1t = scrb[:, sb * 1024: (sb + 1) * 1024]
                p.barrier("gpsimd", ("T",))
                stok = p.dma("gpsimd", stage, w1[pbg * 128:(pbg + 1) * 128, :])
                p.wait("tensor", stok)
                for db in range(ND):
                    self.transpose(stage[:, db * 128:(db + 1) * 128],
                                   w1t[:, db * 128:(db + 1) * 128])
                p.wait("tensor", p.tok("A"))
                for tc in range(4):
                    bi, bank = self.pool.acquire()
                    for db in range(ND):
                        p.emit("tensor", lambda e, db=db, tc=tc, bank=bank, w1t=w1t: e.matmul(
                            bank[:, 0:512], w1t[:, db * 128:(db + 1) * 128],
                            self.xn_fm[:, db * 2048 + tc * 512: db * 2048 + (tc + 1) * 512],
                            start=(db == 0), stop=False))
                    lora_lhs = self.rowb[0:1, j * 1024 + hb * 128: j * 1024 + (hb + 1) * 128]
                    p.step("tensor", lambda e, lora_lhs=lora_lhs, tc=tc, bank=bank: e.matmul(
                        bank[:, 0:512], lora_lhs, wb[0:1, tc * 512:(tc + 1) * 512],
                        start=False, stop=True))
                    mtok = p.tok("T")
                    p.wait("scalar", mtok)
                    etok = p.step("scalar", lambda e, hb=hb, tc=tc, bank=bank, pbg=pbg: e.activation(
                        hbuf[:, hb * 2048 + tc * 512: hb * 2048 + (tc + 1) * 512],
                        bank[:, 0:512], AF.Gelu, bias=self.stats[:, 544 + pbg:545 + pbg]))
                    self.pool.release(bi, etok)
            h_tok = p.tok("A")
            if j == 0:
                self.dump("h0", hbuf[:, :])
                self.dump("aT", wb[0:4, 0:2048])
                self.dump("x16fm", wb[:, 3072:3200])
                self.dump("evens", self.rowb[0:1, 0:5120])

            p.wait("tensor", h_tok)
            p.wait("tensor", e2_tok)
            for tc in range(4):
                bi, bank = self.pool.acquire()
                for hb in range(ND):
                    lhs = scol(wb[:, 3072 + hb * 16: 3072 + (hb + 1) * 16], 9, 2, 4)
                    p.emit("tensor", lambda e, lhs=lhs, hb=hb, tc=tc, bank=bank: e.matmul(
                        bank[0:4, 0:512], lhs,
                        hbuf[:, hb * 2048 + tc * 512: hb * 2048 + (tc + 1) * 512],
                        start=(hb == 0), stop=(hb == 7)), inc="T" if hb == 7 else None)
                mtok = p.tok("T")
                p.wait("vector", mtok)
                etok = p.step("vector", lambda e, tc=tc, bank=bank: e.tensor_scalar(
                    scrb[0:4, 6144 + tc * 512: 6144 + (tc + 1) * 512], bank[0:4, 0:512],
                    self.stats[0:4, 512:513], None, ALU.mult))
                self.pool.release(bi, etok)
            p.wait("sync", self.vflush())
            c_tok = p.dma("sync", cdram[j, :, :], scrb[0:4, 6144:8192])

            for ob in range(8):
                sb = ob % 2
                stage = scrb[:, 2048 + sb * 1024: 2048 + (sb + 1) * 1024]
                p.barrier("gpsimd", ("T",))
                stok = p.dma("gpsimd", stage, w2[ob * 128:(ob + 1) * 128, j * 1024:(j + 1) * 1024])
                p.wait("tensor", stok)
                for hb in range(8):
                    self.transpose(stage[:, hb * 128:(hb + 1) * 128],
                                   scrb[:, 8192 + hb * 1024 + ob * 128: 8192 + hb * 1024 + (ob + 1) * 128])
            w2_tok = p.tok("A")

            cslot = [scrb[0:4, 4096 + 0:4096 + 2048], scrb[0:4, 0 + 0:0 + 2048]]
            if j == 3:
                p.wait("sync", c_tok)
                p.barrier("sync", ("T",))
                cs_tok = {}
                for i in range(2):
                    cs_tok[i] = p.dma("sync", cslot[i][:, :], cdram[:, i, :])

            p.wait("tensor", w2_tok)
            p.wait("tensor", b2_tok)
            for oc in range(2):
                if j == 3 and oc == 1:
                    for i in range(2):
                        p.wait("sync", p.tok("T"))
                        cs_tok[i] = p.dma("sync", cslot[i][:, :], cdram[:, 2 + i, :])
                for tt in range(NT):
                    bi, bank = self.pool.acquire()
                    mms = []
                    for hb in range(8):
                        mms.append((bank[:, 0:512],
                                    hbuf[:, hb * 2048 + tt * 128: hb * 2048 + (tt + 1) * 128],
                                    scrb[:, 8192 + hb * 1024 + oc * 512: 8192 + hb * 1024 + (oc + 1) * 512]))
                    if j == 0:
                        mms.append((bank[:, 0:512], self.c_ones_row[:, :],
                                    self.rowb[0:1, 4096 + oc * 512: 4096 + (oc + 1) * 512]))
                    if j == 3:
                        for isub in range(2):
                            i = 2 * oc + isub
                            p.wait("tensor", cs_tok[isub])
                            mms.append((bank[:, isub * 256:(isub + 1) * 256],
                                        cslot[isub][:, tt * 128:(tt + 1) * 128],
                                        wb[0:4, 2048 + i * 256: 2048 + (i + 1) * 256]))
                    for mi, (o_, l_, r_) in enumerate(mms):
                        p.emit("tensor", lambda e, o_=o_, l_=l_, r_=r_, mi=mi, n=len(mms): e.matmul(
                            o_, l_, r_, start=(mi == 0), stop=(mi == n - 1)),
                            inc="T" if mi == len(mms) - 1 else None)
                    mtok = p.tok("T")
                    p.wait("vector", mtok)
                    dst = self.xres[:, tt * 1024 + oc * 512: tt * 1024 + (oc + 1) * 512]
                    etok = p.step("vector", lambda e, dst=dst, bank=bank: e.tensor_tensor(
                        dst, dst, bank[:, 0:512], ALU.add))
                    self.pool.release(bi, etok)

    def emit_l1(self):
        p = self.p
        wqkv = self.din("L1_attn_W_qkv", [3072, 1024])
        wout = self.din("L1_attn_W_out", [1024, 1024])
        q_dram = self.dint("l1q", [1024, 2048], BF16)
        agi = self.dint("ag_l1s_in", [64, 1040])
        ago = self.dint("ag_l1s_out", [256, 1040])
        rmask = self.din("c_rankmask", [128, 4])
        rm_tok = p.dma("sync", self.stats[:, 700:704], rmask[:, :])

        k_tm = self.scra
        v_tm = self.scrb
        wb = self.wbuf
        st = self.stats
        LN_S = self.fconst(math.log(0.125))

        def build_wT(wsrc, f0, nfb, dst, stage):
            for fb in range(nfb):
                p.barrier("gpsimd", ("T", "A", "V"))
                stok = p.dma("gpsimd", stage, wsrc[f0 + fb * 128:f0 + (fb + 1) * 128, :])
                p.wait("tensor", stok)
                for db in range(ND):
                    self.transpose(stage[:, db * 128:(db + 1) * 128],
                                   dst[:, db * (nfb * 128) + fb * 128: db * (nfb * 128) + (fb + 1) * 128])

        def elu_epi(bank, dest, scale_ln, do_elu=True):
            mtok = p.tok("T")
            p.wait("vector", mtok)
            if not do_elu:
                return p.step("vector", lambda e: e.tensor_copy(dest, bank[:, 0:512]))
            p.step("vector", lambda e: e.tensor_scalar(self.sm[1][:, 0:512], bank[:, 0:512], 0.0, None, ALU.min))
            p.step("scalar", lambda e: e.activation(self.och[:, 0:512], self.sm[1][:, 0:512], AF.Exp,
                                                    bias=scale_ln))
            p.step("vector", lambda e: e.tensor_scalar(self.sm[1][:, 512:1024], bank[:, 0:512],
                                                       math.exp(scale_ln), 0.0, ALU.mult, ALU.max))
            return p.step("vector", lambda e: e.tensor_tensor(dest, self.och[:, 0:512],
                                                              self.sm[1][:, 512:1024], ALU.add))

        xn_tok = p.tok("A")
        p.wait("tensor", xn_tok)
        for fb in range(8):
            build_wT(wqkv, fb * 128, 1, wb[:, 0:1024], wb[:, 1024:2048])
            p.wait("tensor", p.tok("A"))
            for tc in range(4):
                bi, bank = self.pool.acquire()
                for db in range(ND):
                    p.emit("tensor", lambda e, db=db, tc=tc, bank=bank: e.matmul(
                        bank[:, 0:512], wb[:, db * 128:(db + 1) * 128],
                        self.xn_fm[:, db * 2048 + tc * 512: db * 2048 + (tc + 1) * 512],
                        start=(db == 0), stop=(db == 7)), inc="T" if db == 7 else None)
                etok = elu_epi(bank, v_tm[:, fb * 2048 + tc * 512: fb * 2048 + (tc + 1) * 512], LN_S)
                self.pool.release(bi, etok)
        p.wait("sync", p.tok("V"))
        qd_tok = p.dma("sync", q_dram.rearrange("(pb p) t -> p pb t", p=128),
                       v_tm.rearrange("p (pb t) -> p pb t", t=2048))

        for part, dst, do_elu in ((1, k_tm, True), (2, v_tm, True)):
            for fc in range(2):
                f0 = part * 1024 + fc * 512
                if part == 2 and fc == 0:
                    p.wait("gpsimd", qd_tok)
                build_wT(wqkv, f0, 4, wb[:, 0:4096], self.och[:, 0:1024])
                p.wait("tensor", p.tok("A"))
                for tt in range(NT):
                    bi, bank = self.pool.acquire()
                    for db in range(ND):
                        p.emit("tensor", lambda e, db=db, tt=tt, bank=bank: e.matmul(
                            bank[:, 0:512],
                            self.xn_fm[:, db * 2048 + tt * 128:db * 2048 + (tt + 1) * 128],
                            wb[:, db * 512:(db + 1) * 512], start=(db == 0), stop=(db == 7)),
                            inc="T" if db == 7 else None)
                    etok = elu_epi(bank, dst[:, tt * 1024 + fc * 512: tt * 1024 + (fc + 1) * 512],
                                   0.0, do_elu=(part == 1))
                    self.pool.release(bi, etok)

        self.vflush()
        p.wait_all("tensor", "V")
        for h in range(16):
            bi, bank = self.pool.acquire()
            for tt in range(NT):
                p.emit("tensor", lambda e, h=h, tt=tt, bank=bank: e.matmul(
                    bank[0:64, 0:64],
                    k_tm[:, tt * 1024 + h * 64: tt * 1024 + (h + 1) * 64],
                    v_tm[:, tt * 1024 + h * 64: tt * 1024 + (h + 1) * 64],
                    start=(tt == 0), stop=False))
                p.emit("tensor", lambda e, h=h, tt=tt, bank=bank: e.matmul(
                    bank[0:64, 64:65],
                    k_tm[:, tt * 1024 + h * 64: tt * 1024 + (h + 1) * 64],
                    self.c_ones_col[:, :], start=False, stop=(tt == 15)),
                    inc="T" if tt == 15 else None)
            mtok = p.tok("T")
            p.wait("vector", mtok)
            etok = p.step("vector", lambda e, h=h, bank=bank: e.tensor_copy(
                self.sst_f[(h % 2) * 64:(h % 2) * 64 + 64, 520 + (h // 2) * 65: 520 + (h // 2) * 65 + 65],
                bank[0:64, 0:65]))
            self.pool.release(bi, etok)
        fl = self.vflush()
        p.wait("gpsimd", fl)
        agi3 = agi.rearrange("d (hb two c) -> d hb two c", two=2, c=65)
        f2a = p.dma("gpsimd", agi3[:, :, 0, :], self.sst_f[0:64, 520:1040].rearrange("d (hb c) -> d hb c", c=65))
        p.wait("gpsimd", f2a)
        f2 = p.dma("gpsimd", agi3[:, :, 1, :], self.sst_f[64:128, 520:1040].rearrange("d (hb c) -> d hb c", c=65))
        ag_tok = self.ag_go(agi, ago, f2)

        p.wait("vector", rm_tok)
        p.step("vector", lambda e: e.memset(self.sst_f[:, 0:520], 0.0))
        for r in range(4):
            p.wait("gpsimd", ag_tok)
            p.wait_all("gpsimd", "V")
            g1 = p.dma("gpsimd", self.sst_f[0:64, 520:1040].rearrange("d (hb c) -> d hb c", c=65),
                       ago.rearrange("(r d) (hb two c) -> r d hb two c", r=4, two=2, c=65)[r, :, :, 0, :])
            g2 = p.dma("gpsimd", self.sst_f[64:128, 520:1040].rearrange("d (hb c) -> d hb c", c=65),
                       ago.rearrange("(r d) (hb two c) -> r d hb two c", r=4, two=2, c=65)[r, :, :, 1, :])
            p.wait("vector", g2)
            p.step("vector", lambda e, r=r: e.scalar_tensor_tensor(
                self.sst_f[:, 0:520], self.sst_f[:, 520:1040], st[:, 700 + r:701 + r],
                self.sst_f[:, 0:520], ALU.mult, ALU.add))
        p.step("vector", lambda e: e.tensor_copy(self.sst_b[:, 0:520], self.sst_f[:, 0:520]))
        self.vflush()

        och_dram = self.dint("l1o", [2048, 1024], BF16)
        for c in range(8):
            p.barrier("sync", ("T",))
            qtok = p.dma("sync", wb[:, 0:2048].rearrange("p (pb t) -> p pb t", t=256),
                         q_dram.rearrange("(pb p) t -> p pb t", p=128)[:, :, c * 256:(c + 1) * 256])
            p.wait("tensor", qtok)
            for h in range(16):
                b0 = (h % 2) * 64
                hb = h // 2
                for mt in range(2):
                    tt = 2 * c + mt
                    self.transpose(k_tm[:, tt * 1024 + h * 64: tt * 1024 + (h + 1) * 64],
                                   self.sm[0][b0:b0 + 64, mt * 128:(mt + 1) * 128], out_base=b0)
                p.wait("tensor", p.tok("A"))
                bi_s, bank_s = self.pool.acquire()
                for mt in range(2):
                    p.emit("tensor", lambda e, mt=mt, b0=b0, hb=hb, bank_s=bank_s: e.matmul(
                        bank_s[0:128, mt * 256:(mt + 1) * 256],
                        self.sm[0][b0:b0 + 64, mt * 128:(mt + 1) * 128],
                        wb[b0:b0 + 64, hb * 256:(hb + 1) * 256],
                        start=(mt == 0), stop=True), inc="T" if mt == 1 else None)
                mtok = p.tok("T")
                p.wait("vector", mtok)
                p.step("vector", lambda e, bank_s=bank_s: e.tensor_tensor(
                    self.sm[1][:, 0:512], bank_s[:, 0:512], self.c_l1mask[:, :], ALU.mult))
                sv = self.vflush()
                self.pool.release(bi_s, sv)
                p.wait("tensor", sv)
                bi_n, bank_n = self.pool.acquire()
                nm = []
                for lt2 in range(2):
                    for mt in range(2):
                        tt = 2 * c + mt
                        nm.append((bank_n[:, lt2 * 65:lt2 * 65 + 64],
                                   self.sm[1][0:128, mt * 256 + lt2 * 128: mt * 256 + (lt2 + 1) * 128],
                                   v_tm[:, tt * 1024 + h * 64: tt * 1024 + (h + 1) * 64]))
                        nm.append((bank_n[:, lt2 * 65 + 64:lt2 * 65 + 65],
                                   self.sm[1][0:128, mt * 256 + lt2 * 128: mt * 256 + (lt2 + 1) * 128],
                                   self.c_ones_col[:, :]))
                for mi, (o_, l_, r_) in enumerate(nm):
                    p.emit("tensor", lambda e, o_=o_, l_=l_, r_=r_, mi=mi: e.matmul(
                        o_, l_, r_, start=(mi == 0), stop=False))
                for lt2 in range(2):
                    p.emit("tensor", lambda e, lt2=lt2, b0=b0, hb=hb, bank_n=bank_n: e.matmul(
                        bank_n[:, lt2 * 65:(lt2 + 1) * 65],
                        wb[b0:b0 + 64, hb * 256 + lt2 * 128: hb * 256 + (lt2 + 1) * 128],
                        self.sst_b[b0:b0 + 64, hb * 65:(hb + 1) * 65],
                        start=False, stop=(lt2 == 1)), inc="T" if lt2 == 1 else None)
                ntok = p.tok("T")
                p.wait("vector", ntok)
                for lt2 in range(2):
                    p.step("vector", lambda e, lt2=lt2, bank_n=bank_n: e.tensor_scalar(
                        st[:, 704 + lt2:705 + lt2], bank_n[:, lt2 * 65 + 64:lt2 * 65 + 65],
                        1e-6, None, ALU.max))
                    p.step("vector", lambda e, lt2=lt2: e.reciprocal(
                        st[:, 706 + lt2:707 + lt2], st[:, 704 + lt2:705 + lt2]))
                    dtok = p.step("vector", lambda e, lt2=lt2, h=h, bank_n=bank_n: e.tensor_scalar(
                        self.och[:, lt2 * 1024 + h * 64: lt2 * 1024 + (h + 1) * 64],
                        bank_n[:, lt2 * 65:lt2 * 65 + 64],
                        st[:, 706 + lt2:707 + lt2], None, ALU.mult))
                self.pool.release(bi_n, p.tok("V"))
                bi_u, bank_u = self.pool.acquire()
                for mt in range(2):
                    tt = 2 * c + mt
                    p.emit("tensor", lambda e, mt=mt, tt=tt, h=h, bank_u=bank_u: e.matmul(
                        bank_u[0:64, 0:64],
                        k_tm[:, tt * 1024 + h * 64: tt * 1024 + (h + 1) * 64],
                        v_tm[:, tt * 1024 + h * 64: tt * 1024 + (h + 1) * 64],
                        start=(mt == 0), stop=False))
                    p.emit("tensor", lambda e, mt=mt, tt=tt, h=h, bank_u=bank_u: e.matmul(
                        bank_u[0:64, 64:65],
                        k_tm[:, tt * 1024 + h * 64: tt * 1024 + (h + 1) * 64],
                        self.c_ones_col[:, :], start=False, stop=(mt == 1)),
                        inc="T" if mt == 1 else None)
                utok = p.tok("T")
                p.wait("vector", utok)
                p.step("vector", lambda e, b0=b0, hb=hb, bank_u=bank_u: e.tensor_tensor(
                    self.sst_f[b0:b0 + 64, hb * 65:(hb + 1) * 65],
                    self.sst_f[b0:b0 + 64, hb * 65:(hb + 1) * 65],
                    bank_u[0:64, 0:65], ALU.add))
                ctok = p.step("vector", lambda e, b0=b0, hb=hb: e.tensor_copy(
                    self.sst_b[b0:b0 + 64, hb * 65:(hb + 1) * 65],
                    self.sst_f[b0:b0 + 64, hb * 65:(hb + 1) * 65]))
                self.pool.release(bi_u, ctok)
            fl2 = self.vflush()
            p.wait("sync", fl2)
            p.dma("sync", och_dram[c * 256:(c + 1) * 256, :].rearrange("(lt p) o -> p lt o", p=128),
                  self.och[:, 0:2048].rearrange("p (lt o) -> p lt o", o=1024))
        for ob in range(8):
            build_wT(wout, ob * 128, 1, k_tm[:, ob * 1024: ob * 1024 + 1024], self.och[:, 0:1024])
        woutT = k_tm.rearrange("p (ob eb j) -> p ob eb j", eb=8, j=128)
        for tt in range(NT):
            p.barrier("sync", ("T",))
            ltok = p.dma("sync", self.och[:, 0:1024],
                         och_dram[tt * 128:(tt + 1) * 128, :])
            p.wait("tensor", ltok)
            for eb in range(8):
                self.transpose(self.och[:, eb * 128:(eb + 1) * 128],
                               self.och[:, 1024 + eb * 128: 1024 + (eb + 1) * 128])
            p.wait("tensor", p.tok("A"))
            for oc in range(2):
                bi, bank = self.pool.acquire()
                for eb in range(8):
                    p.emit("tensor", lambda e, eb=eb, oc=oc, tt=tt, bank=bank: e.matmul(
                        bank[:, 0:512], self.och[:, 1024 + eb * 128: 1024 + (eb + 1) * 128],
                        woutT[:, oc * 4:(oc + 1) * 4, eb, :],
                        start=(eb == 0), stop=(eb == 7)), inc="T" if eb == 7 else None)
                mtok = p.tok("T")
                p.wait("vector", mtok)
                dst = self.xres[:, tt * 1024 + oc * 512: tt * 1024 + (oc + 1) * 512]
                etok = p.step("vector", lambda e, dst=dst, bank=bank: e.tensor_tensor(
                    dst, dst, bank[:, 0:512], ALU.add))
                self.pool.release(bi, etok)


    def emit_l1_ffn(self):
        if hasattr(self, "mark"):
            self.mark("l1_ffn")
        """top-2 softmax MoE with capacity C=5120 (global token order), dense
        per-expert compute; gate/keep folded into the psum->xres evacuation."""
        p = self.p
        rtr_w = self.din("L1_ffn_router_w", [8, 1024])
        w1 = self.din("L1_ffn_w1", [8, 1024, 4096])
        w2 = self.din("L1_ffn_w2", [8, 4096, 1024])
        agci = self.dint("ag_l1c_in", [1, 16])
        agco = self.dint("ag_l1c_out", [8, 16])
        rmask8 = self.din("c_rankmask8", [128, 8])
        rm8_tok = p.dma("sync", self.stats[:, 708:716], rmask8[:, :])
        st = self.stats
        wb = self.wbuf
        C_CAP = 5120.0

        p.barrier("gpsimd", ("T", "A", "V"))
        stok = p.dma("gpsimd", self.sm[0][0:8, 0:1024], rtr_w[:, :])
        p.wait("tensor", stok)
        for db in range(ND):
            self.transpose(self.sm[0][0:8, db * 128:(db + 1) * 128], wb[:, db * 8:(db + 1) * 8])
        p.step("vector", lambda e: e.tensor_scalar(st[:, 640:648], self.c_iota8[:, :], 1e-6, None, ALU.mult))
        p.step("vector", lambda e: e.memset(st[0:1, 648:664], 0.0))
        p.wait("tensor", p.tok("A"))
        p.wait("tensor", p.tok("V"))

        for tt in range(NT):
            bi, bank = self.pool.acquire()
            for db in range(ND):
                p.emit("tensor", lambda e, db=db, tt=tt, bank=bank: e.matmul(
                    bank[:, 0:8], self.xn_fm[:, db * 2048 + tt * 128:db * 2048 + (tt + 1) * 128],
                    wb[:, db * 8:(db + 1) * 8], start=(db == 0), stop=(db == 7)),
                    inc="T" if db == 7 else None)
            mtok = p.tok("T")
            p.wait("vector", mtok)
            p.step("vector", lambda e, bank=bank: e.tensor_tensor(st[:, 664:672], bank[:, 0:8], st[:, 640:648], ALU.subtract))
            p.step("vector", lambda e: e.reduce_max(st[:, 672:673], st[:, 664:672], AX.X))
            p.step("vector", lambda e, tt=tt: e.tensor_scalar(
                self.och[:, tt * 8:(tt + 1) * 8], st[:, 664:672], st[:, 672:673], None, ALU.is_equal))
            p.step("vector", lambda e, tt=tt: e.scalar_tensor_tensor(
                st[:, 680:688], self.och[:, tt * 8:(tt + 1) * 8], -1e30, st[:, 664:672], ALU.mult, ALU.add))
            p.step("vector", lambda e: e.reduce_max(st[:, 673:674], st[:, 680:688], AX.X))
            p.step("vector", lambda e, tt=tt: e.tensor_scalar(
                self.och[:, 128 + tt * 8:128 + (tt + 1) * 8], st[:, 680:688], st[:, 673:674], None, ALU.is_equal))
            p.step("vector", lambda e: e.tensor_tensor(st[:, 674:675], st[:, 672:673], st[:, 673:674], ALU.subtract))
            p.wait("scalar", p.tok("V"))
            p.step("scalar", lambda e, tt=tt: e.activation(st[:, 448 + tt:449 + tt], st[:, 674:675], AF.Sigmoid))
            p.wait("vector", p.tok("A"))
            p.step("vector", lambda e, tt=tt: e.tensor_scalar(
                st[:, 464 + tt:465 + tt], st[:, 448 + tt:449 + tt], -1.0, 1.0, ALU.mult, ALU.add))
            for k in range(2):
                ohap = self.och[:, k * 128 + tt * 8: k * 128 + (tt + 1) * 8]
                p.wait("vector", mtok)
                bi2, bank2 = self.pool.acquire()
                p.wait("tensor", p.tok("V"))
                p.emit("tensor", lambda e, ohap=ohap, bank2=bank2: e.matmul(
                    bank2[:, 0:8], self.c_tril[:, :], ohap, start=True, stop=False))
                p.emit("tensor", lambda e, k=k, bank2=bank2: e.matmul(
                    bank2[:, 0:8], self.c_ones_rowf[:, :], st[0:1, 648 + k * 8:656 + k * 8],
                    start=False, stop=False))
                p.emit("tensor", lambda e, ohap=ohap, bank2=bank2: e.matmul(
                    bank2[0:1, 8:16], self.c_ones_col[:, :], ohap, start=False, stop=True),
                    inc="T")
                m2tok = p.tok("T")
                p.wait("vector", m2tok)
                p.step("vector", lambda e, ohap=ohap, bank2=bank2: e.tensor_tensor(
                    st[:, 680:688], bank2[:, 0:8], ohap, ALU.mult))
                p.step("vector", lambda e: e.reduce_sum(st[:, 688:689], st[:, 680:688], AX.X))
                p.step("vector", lambda e, k=k, tt=tt: e.tensor_scalar(
                    st[:, 480 + k * 16 + tt:481 + k * 16 + tt], st[:, 688:689], -1.0, None, ALU.add))
                etok = p.step("vector", lambda e, k=k, bank2=bank2: e.tensor_tensor(
                    st[0:1, 648 + k * 8:656 + k * 8], st[0:1, 648 + k * 8:656 + k * 8],
                    bank2[0:1, 8:16], ALU.add))
                self.pool.release(bi2, etok)
            self.pool.release(bi, p.tok("V"))
        fl = self.vflush()
        p.wait("gpsimd", fl)
        cf = p.dma("gpsimd", agci[:, :], st[0:1, 648:664])
        agc_tok = self.ag_go(agci, agco, cf, groups=[[0, 1, 2, 3, 4, 5, 6, 7]])
        p.wait("gpsimd", agc_tok)
        p.wait("vector", rm8_tok)
        p.step("vector", lambda e: e.memset(st[0:1, 624:640], 0.0))
        for r in range(8):
            p.wait_all("gpsimd", "V")
            g = p.dma("gpsimd", st[0:1, 608:624], agco[r:r + 1, :])
            p.wait("vector", g)
            p.step("vector", lambda e, r=r: e.scalar_tensor_tensor(
                st[0:1, 624:640], st[0:1, 608:624], st[0:1, 708 + r:709 + r],
                st[0:1, 624:640], ALU.mult, ALU.add))
        p.step("vector", lambda e: e.tensor_scalar(st[0:1, 624:640], st[0:1, 624:640], -1.0, C_CAP, ALU.mult, ALU.add))
        fl2 = self.vflush()
        p.wait("tensor", fl2)
        bi, bank = self.pool.acquire()
        for k in range(2):
            p.emit("tensor", lambda e, k=k, bank=bank: e.matmul(
                bank[:, k * 8:(k + 1) * 8], self.c_ones_rowf[:, :], st[0:1, 624 + k * 8:632 + k * 8],
                start=(k == 0), stop=(k == 1)), inc="T" if k == 1 else None)
        thr_tok = p.tok("T")
        p.wait("vector", thr_tok)
        p.step("vector", lambda e, bank=bank: e.tensor_copy(st[:, 576:592], bank[:, 0:16]))
        self.pool.release(bi, p.tok("V"))
        for tt in range(NT):
            for k in range(2):
                ohap = self.och[:, k * 128 + tt * 8: k * 128 + (tt + 1) * 8]
                p.step("vector", lambda e, ohap=ohap, k=k: e.tensor_tensor(
                    st[:, 680:688], ohap, st[:, 576 + k * 8:584 + k * 8], ALU.mult))
                p.step("vector", lambda e: e.reduce_sum(st[:, 688:689], st[:, 680:688], AX.X))
                p.step("vector", lambda e, k=k, tt=tt: e.tensor_tensor(
                    st[:, 689:690], st[:, 480 + k * 16 + tt:481 + k * 16 + tt], st[:, 688:689], ALU.is_lt))
                p.step("vector", lambda e, k=k, tt=tt: e.tensor_tensor(
                    st[:, 690:691], st[:, (448 if k == 0 else 464) + tt:(449 if k == 0 else 465) + tt],
                    st[:, 689:690], ALU.mult))
                if k == 0:
                    p.step("vector", lambda e, ohap=ohap, tt=tt: e.tensor_scalar(
                        st[:, 256 + tt * 8:264 + tt * 8], ohap, st[:, 690:691], None, ALU.mult))
                else:
                    p.step("vector", lambda e, ohap=ohap, tt=tt: e.scalar_tensor_tensor(
                        st[:, 256 + tt * 8:264 + tt * 8], ohap, st[:, 690:691],
                        st[:, 256 + tt * 8:264 + tt * 8], ALU.mult, ALU.add))
        self.vflush()

        hq = self.scrb
        for e_ in range(8):
            for q in range(4):
                p.barrier("gpsimd", ("T",))
                w1tok = p.dma("gpsimd", self.scra[:, 0:8192].rearrange("p (db f) -> p db f", f=1024),
                              w1[e_, :, q * 1024:(q + 1) * 1024].rearrange("(db p) f -> p db f", p=128))
                p.wait("tensor", w1tok)
                for hb in range(8):
                    for tc in range(4):
                        bi, bank = self.pool.acquire()
                        for db in range(ND):
                            p.emit("tensor", lambda e, db=db, hb=hb, tc=tc, bank=bank: e.matmul(
                                bank[:, 0:512],
                                self.scra[:, db * 1024 + hb * 128: db * 1024 + (hb + 1) * 128],
                                self.xn_fm[:, db * 2048 + tc * 512: db * 2048 + (tc + 1) * 512],
                                start=(db == 0), stop=(db == 7)), inc="T" if db == 7 else None)
                        mtok = p.tok("T")
                        p.wait("scalar", mtok)
                        etok = p.step("scalar", lambda e, hb=hb, tc=tc, bank=bank: e.activation(
                            hq[:, hb * 2048 + tc * 512: hb * 2048 + (tc + 1) * 512],
                            bank[:, 0:512], AF.Gelu))
                        self.pool.release(bi, etok)
                p.barrier("gpsimd", ("T", "A"))
                w2tok = p.dma("gpsimd", self.scra[:, 8192:16384].rearrange("p (hfb o) -> p hfb o", o=1024),
                              w2[e_, q * 1024:(q + 1) * 1024, :].rearrange("(hfb p) o -> p hfb o", p=128))
                p.wait("tensor", w2tok)
                p.wait("tensor", p.tok("A"))
                w2v = self.scra.rearrange("p (half hfb o) -> p half hfb o", half=2, o=1024)
                for tt in range(NT):
                    for oc in range(2):
                        bi, bank = self.pool.acquire()
                        for hfb in range(8):
                            p.emit("tensor", lambda e, hfb=hfb, tt=tt, oc=oc, bank=bank: e.matmul(
                                bank[:, 0:512],
                                hq[:, hfb * 2048 + tt * 128: hfb * 2048 + (tt + 1) * 128],
                                w2v[:, 1, hfb, oc * 512:(oc + 1) * 512],
                                start=(hfb == 0), stop=(hfb == 7)), inc="T" if hfb == 7 else None)
                        mtok = p.tok("T")
                        p.wait("vector", mtok)
                        dst = self.xres[:, tt * 1024 + oc * 512: tt * 1024 + (oc + 1) * 512]
                        etok = p.step("vector", lambda e, dst=dst, bank=bank, e_=e_, tt=tt: e.scalar_tensor_tensor(
                            dst, bank[:, 0:512], st[:, 256 + tt * 8 + e_:257 + tt * 8 + e_],
                            dst, ALU.mult, ALU.add))
                        self.pool.release(bi, etok)


    def emit_l2(self):
        p = self.p
        st = self.stats
        wb = self.wbuf
        phase = self.din("L2_attn_phase", [4, 1024])
        wq = self.din("L2_attn_Wq", [1024, 4096])
        bq = self.din("L2_attn_bq", [4096])
        wk = self.din("L2_attn_Wk", [1024, 4096])
        bk = self.din("L2_attn_bk", [4096])
        wv = self.din("L2_attn_Wv", [1024, 4096])
        bv = self.din("L2_attn_bv", [4096])
        wo = self.din("L2_attn_Wo", [4096, 1024])
        bo = self.din("L2_attn_bo", [1024])
        q_dram = self.dint("l2q", [2048, 4096], BF16)
        k_dram = self.dint("l2k", [2048, 4096], BF16)

        xn_tok = p.tok("A")
        p.wait("tensor", xn_tok)
        for which, wmat, bvec, dstd in (("q", wq, bq, q_dram), ("k", wk, bk, k_dram)):
            p.barrier("gpsimd", ("T", "A", "V"))
            btok = p.dma("gpsimd", self.rowb[0:1, 0:4096], bvec.rearrange("f -> () f"))
            for fc in range(8):
                p.wait_all("gpsimd", "T")
                wtok = p.dma("gpsimd", wb.rearrange("p (db f) -> p db f", f=512),
                             wmat[:, fc * 512:(fc + 1) * 512].rearrange("(db p) f -> p db f", p=128))
                p.wait("tensor", wtok)
                p.wait("tensor", btok)
                for tt in range(NT):
                    bi, bank = self.pool.acquire()
                    for db in range(ND):
                        p.emit("tensor", lambda e, db=db, tt=tt, bank=bank: e.matmul(
                            bank[:, 0:512],
                            self.xn_fm[:, db * 2048 + tt * 128:db * 2048 + (tt + 1) * 128],
                            wb[:, db * 512:(db + 1) * 512], start=(db == 0), stop=False))
                    p.emit("tensor", lambda e, fc=fc, bank=bank: e.matmul(
                        bank[:, 0:512], self.c_ones_row[:, :],
                        self.rowb[0:1, fc * 512:(fc + 1) * 512], start=False, stop=True), inc="T")
                    mtok = p.tok("T")
                    p.wait("scalar", mtok)
                    etok = p.step("scalar", lambda e, tt=tt, bank=bank: e.copy(
                        self.scra[:, tt * 512:(tt + 1) * 512], bank[:, 0:512]))
                    self.pool.release(bi, etok)
                fl = p.tok("A")
                p.wait("sync", fl)
                p.dma("sync", dstd.rearrange("(tt p) (fc f) -> p tt fc f", p=128, f=512)[:, :, fc, :],
                      self.scra[:, 0:8192].rearrange("p (tt f) -> p tt f", f=512))

        p.barrier("gpsimd", ("T", "A", "V"))
        for pr in range(16):
            p.dma("gpsimd", self.sst_f[pr:pr + 1, 0:1024], phase[pr // 4:pr // 4 + 1, :])
        ph_tok = p.tok("Dg")
        p.wait("vector", ph_tok)
        p.step("vector", lambda e: e.tensor_scalar(self.sst_f[0:16, 0:1024],
                                                   self.sst_f[0:16, 0:1024], -1.0, None, ALU.mult))
        neg = self.vflush()
        p.wait("gpsimd", neg)
        for pr in range(16):
            p.dma("gpsimd", self.sst_f[pr:pr + 1, 0:1024], phase[pr % 4:pr % 4 + 1, :],
                  accum_op=ALU.add)
        ph2 = p.tok("Dg")
        p.wait("scalar", ph2)
        p.step("scalar", lambda e: e.activation(self.sst_f[0:16, 0:1024], self.sst_f[0:16, 0:1024],
                                                AF.Sin, bias=self.fconst(math.pi / 2)))
        fl = p.tok("A")
        cosrow_d = self.dint("l2cd", [16, 1024], F32)
        p.wait("sync", fl)
        cd1 = p.dma("sync", cosrow_d[:, :], self.sst_f[0:16, 0:1024])
        p.wait("sync", cd1)
        for pr in range(16):
            rtok = p.dma("sync", self.rowf[0:1, 0:1024], cosrow_d[pr:pr + 1, :])
            p.wait("tensor", rtok)
            for dc in range(2):
                bi, bank = self.pool.acquire()
                p.step("tensor", lambda e, dc=dc, bank=bank: e.matmul(
                    bank[:, 0:512], self.c_ones_rowf[:, :], self.rowf[0:1, dc * 512:(dc + 1) * 512],
                    start=True, stop=True))
                mtok = p.tok("T")
                p.wait("scalar", mtok)
                etok = p.step("scalar", lambda e, pr=pr, dc=dc, bank=bank: e.copy(
                    self.scrb[:, pr * 1024 + dc * 512: pr * 1024 + (dc + 1) * 512], bank[:, 0:512]))
                self.pool.release(bi, etok)
            p.barrier("sync", ("T",))

        SC = 1.0 / 32.0
        for tt in range(NT):
            p.barrier("sync", ("V",))
            qtok = p.dma("sync", self.scra[:, 0:4096], q_dram[tt * 128:(tt + 1) * 128, :])
            ktok = p.dma("sync", self.scra[:, 4096:8192], k_dram[tt * 128:(tt + 1) * 128, :])
            p.wait("vector", ktok)
            base = 256 + tt * 16
            for pr in range(16):
                h, g = pr // 4, pr % 4
                for dh in range(2):
                    p.step("vector", lambda e, h=h, g=g, dh=dh: e.tensor_tensor(
                        self.sm[0][:, 0:512],
                        self.scra[:, h * 1024 + dh * 512: h * 1024 + dh * 512 + 512],
                        self.scra[:, 4096 + g * 1024 + dh * 512: 4096 + g * 1024 + dh * 512 + 512],
                        ALU.mult))
                    p.step("vector", lambda e, pr=pr, dh=dh: e.scalar_tensor_tensor(
                        self.sm[1][:, 0:512], self.sm[0][:, 0:512], SC,
                        self.scrb[:, pr * 1024 + dh * 512:pr * 1024 + (dh + 1) * 512],
                        ALU.mult, ALU.mult,
                        accum_out=st[:, 248 + dh:249 + dh]))
                p.step("vector", lambda e, pr=pr, base=base: e.tensor_tensor(
                    st[:, base + pr:base + pr + 1], st[:, 248:249], st[:, 249:250], ALU.add))
            p.step("vector", lambda e, base=base: e.tensor_reduce(
                st[:, 240:244], st[:, base:base + 16].rearrange("p (h g) -> p h g", g=4),
                AX.X, ALU.max))
            p.step("vector", lambda e, base=base: e.tensor_tensor(
                st[:, 224:240].rearrange("p (h g) -> p h g", g=4),
                st[:, base:base + 16].rearrange("p (h g) -> p h g", g=4),
                bcast_free(st[:, 240:244], 4), ALU.subtract))
            p.wait("scalar", p.tok("V"))
            p.step("scalar", lambda e: e.activation(st[:, 224:240], st[:, 224:240], AF.Exp))
            p.wait("vector", p.tok("A"))
            p.step("vector", lambda e: e.tensor_reduce(
                st[:, 240:244], st[:, 224:240].rearrange("p (h g) -> p h g", g=4), AX.X, ALU.add))
            p.step("vector", lambda e: e.reciprocal(st[:, 244:248], st[:, 240:244]))
            p.step("vector", lambda e, base=base: e.tensor_tensor(
                st[:, base:base + 16].rearrange("p (h g) -> p h g", g=4),
                st[:, 224:240].rearrange("p (h g) -> p h g", g=4),
                bcast_free(st[:, 244:248], 4), ALU.mult))
        self.vflush()
        self.dump("l2attn", self.stats[:, 256:512])

        if hasattr(self, "mark"):
            self.mark("l2_phaseB")
        bo_tok = p.dma("gpsimd", self.rowb[0:1, 4096:5120], bo.rearrange("f -> () f"))
        omix_dram = self.dint("l2om_in", [2048, 4096], BF16)
        omix_ago = self.dint("l2om_out", [8192, 4096], BF16)
        gsel = self.din("c_groupsel", [128, 4])
        gtok = p.dma("sync", st[:, 716:720], gsel[:, :])
        for dc in range(8):
            p.barrier("gpsimd", ("T", "A"))
            for hh in range(4):
                wvtok = p.dma("gpsimd",
                              wb.rearrange("p (db h j) -> p db h j", db=8, h=4)[:, :, hh, :],
                              wv.rearrange("(db p) (h s j) -> p db h s j", p=128, h=4, s=8)[:, :, hh, dc, :])
            bvtok = p.dma("gpsimd", self.rowb[0:1, 0:512].rearrange("o (h j) -> o h j", j=128),
                          bv.rearrange("(h s j) -> h s j", h=4, s=8)[:, dc, :].rearrange("h j -> () h j"))
            p.wait("tensor", wvtok)
            p.wait("tensor", bvtok)
            for tt in range(NT):
                bi, bank = self.pool.acquire()
                for db in range(ND):
                    p.emit("tensor", lambda e, db=db, tt=tt, bank=bank: e.matmul(
                        bank[:, 0:512],
                        self.xn_fm[:, db * 2048 + tt * 128:db * 2048 + (tt + 1) * 128],
                        wb[:, db * 512:(db + 1) * 512], start=(db == 0), stop=False))
                p.emit("tensor", lambda e, bank=bank: e.matmul(
                    bank[:, 0:512], self.c_ones_row[:, :], self.rowb[0:1, 0:512],
                    start=False, stop=True), inc="T")
                mtok = p.tok("T")
                p.wait("vector", mtok)
                etok = p.step("vector", lambda e, bank=bank: e.tensor_copy(
                    self.och[:, 0:512], bank[:, 0:512]))
                self.pool.release(bi, etok)
                for h in range(4):
                    for g in range(4):
                        sc_ap = st[:, 256 + tt * 16 + h * 4 + g: 256 + tt * 16 + h * 4 + g + 1]
                        if g == 0:
                            p.step("vector", lambda e, h=h, g=g, sc_ap=sc_ap: e.tensor_scalar(
                                self.och[:, 1024 + h * 128:1024 + (h + 1) * 128],
                                self.och[:, g * 128:(g + 1) * 128], sc_ap, None, ALU.mult))
                        else:
                            p.step("vector", lambda e, h=h, g=g, sc_ap=sc_ap: e.scalar_tensor_tensor(
                                self.och[:, 1024 + h * 128:1024 + (h + 1) * 128],
                                self.och[:, g * 128:(g + 1) * 128], sc_ap,
                                self.och[:, 1024 + h * 128:1024 + (h + 1) * 128], ALU.mult, ALU.add))
                omtok = self.vflush()
                p.wait("gpsimd", omtok)
                p.dma("gpsimd",
                      omix_dram[tt * 128:(tt + 1) * 128, :].rearrange("p (h s j) -> p h s j", h=4, s=8)[:, :, dc, :],
                      self.och[:, 1024:1536].rearrange("p (h j) -> p h j", j=128))
        fl = p.tok("Dg")
        ag_tok = self.ag_go(omix_dram, omix_ago, fl)
        if hasattr(self, "mark"):
            self.mark("l2_gather")
        p.wait("sync", ag_tok)
        p.wait("vector", gtok)
        for oc in range(2):
            p.barrier("gpsimd", ("T", "A", "V"))
            wotok = p.dma("gpsimd", self.scrb[:, 0:16384].rearrange("p (fb o) -> p fb o", o=512),
                          wo.rearrange("(fb p) o -> p fb o", p=128)[:, :, oc * 512:(oc + 1) * 512])
            p.wait("tensor", wotok)
            for tt in range(NT):
                p.barrier("sync", ("T", "V"))
                for r in range(4):
                    p.dma("sync", self.scra[:, r * 4096:(r + 1) * 4096].rearrange("p (j d) -> p j d", d=1024),
                          omix_ago.rearrange("(r q) (h d) -> r q h d", r=4, h=4)[tt // 4,
                          (tt % 4) * 512:(tt % 4 + 1) * 512, r, :]
                          .rearrange("(p j) d -> p j d", p=128))
                dtok = p.tok("Ds")
                p.wait("vector", dtok)
                for r in range(4):
                    sc_ap = st[:, 716 + r:717 + r]
                    op = "init" if r == 0 else "acc"
                    for dsti, (dst, lo, ln) in enumerate(
                            ((self.och[:, 0:2048], 0, 2048),
                             (self.sm[0][:, 0:1024], 2048, 1024),
                             (self.sm[1][:, 0:1024], 3072, 1024))):
                        if r == 0:
                            p.step("vector", lambda e, r=r, sc_ap=sc_ap, dst=dst, lo=lo, ln=ln: e.tensor_scalar(
                                dst, self.scra[:, r * 4096 + lo:r * 4096 + lo + ln], sc_ap, None, ALU.mult))
                        else:
                            p.step("vector", lambda e, r=r, sc_ap=sc_ap, dst=dst, lo=lo, ln=ln: e.scalar_tensor_tensor(
                                dst, self.scra[:, r * 4096 + lo:r * 4096 + lo + ln], sc_ap,
                                dst, ALU.mult, ALU.add))
                seltok = self.vflush()
                p.wait("tensor", seltok)
                srcs = [self.och[:, 0:2048], self.sm[0][:, 0:1024], self.sm[1][:, 0:1024]]
                def piece(fb):
                    if fb < 16:
                        return srcs[0].rearrange("p (a b) -> p a b", b=128)[:, fb, :]
                    if fb < 24:
                        return srcs[1].rearrange("p (a b) -> p a b", b=128)[:, fb - 16, :]
                    return srcs[2].rearrange("p (a b) -> p a b", b=128)[:, fb - 24, :]
                for fb in range(32):
                    self.transpose(piece(fb), self.scra[:, fb * 128:(fb + 1) * 128])
                p.wait("tensor", p.tok("A"))
                p.wait("tensor", bo_tok)
                bi, bank = self.pool.acquire()
                for fb in range(32):
                    p.emit("tensor", lambda e, fb=fb, bank=bank: e.matmul(
                        bank[:, 0:512], self.scra[:, fb * 128:(fb + 1) * 128],
                        self.scrb[:, fb * 512:(fb + 1) * 512],
                        start=(fb == 0), stop=False))
                p.emit("tensor", lambda e, oc=oc, bank=bank: e.matmul(
                    bank[:, 0:512], self.c_ones_row[:, :],
                    self.rowb[0:1, 4096 + oc * 512:4096 + (oc + 1) * 512], start=False, stop=True), inc="T")
                mtok = p.tok("T")
                p.wait("vector", mtok)
                dst = self.xres[:, tt * 1024 + oc * 512: tt * 1024 + (oc + 1) * 512]
                etok = p.step("vector", lambda e, dst=dst, bank=bank: e.tensor_tensor(
                    dst, dst, bank[:, 0:512], ALU.add))
                self.pool.release(bi, etok)

    def emit_l2_ffn(self):
        p = self.p
        if hasattr(self, "mark"):
            self.mark("l2_ffn")
        st = self.stats
        wb = self.wbuf
        gate_w = self.din("L2_ffn_gate_w", [1024, 8])
        gate_b = self.din("L2_ffn_gate_b", [8])
        ew = self.din("L2_ffn_expert_w", [8, 1024, 2048])
        eb = self.din("L2_ffn_expert_b", [8, 2048])
        ow = self.din("L2_ffn_out_w", [2048, 1024])
        ob = self.din("L2_ffn_out_b", [1024])
        sel_dram = self.dint("l2sel", [2048, 2048], BF16)

        self.dump("l2xres_a", self.xres[:, :])
        p.barrier("gpsimd", ("T", "A", "V"))
        gwtok = p.dma("gpsimd", wb[:, 0:64].rearrange("p (db f) -> p db f", f=8),
                      gate_w.rearrange("(db p) f -> p db f", p=128))
        gbtok = p.dma("gpsimd", self.rowb[0:1, 0:8], gate_b.rearrange("f -> () f"))
        ebtok = p.dma("gpsimd", self.scrb[0:8, 0:2048], eb[:, :])
        p.step("vector", lambda e: e.tensor_scalar(st[:, 640:648], self.c_iota8[:, :], 1e-6, None, ALU.mult))
        p.wait("tensor", gwtok)
        p.wait("tensor", gbtok)
        for tt in range(NT):
            bi, bank = self.pool.acquire()
            for db in range(ND):
                p.emit("tensor", lambda e, db=db, tt=tt, bank=bank: e.matmul(
                    bank[:, 0:8], self.xn_fm[:, db * 2048 + tt * 128:db * 2048 + (tt + 1) * 128],
                    wb[:, db * 8:(db + 1) * 8], start=(db == 0), stop=False))
            p.emit("tensor", lambda e, bank=bank: e.matmul(
                bank[:, 0:8], self.c_ones_row[:, :], self.rowb[0:1, 0:8],
                start=False, stop=True), inc="T")
            mtok = p.tok("T")
            p.wait("vector", mtok)
            p.step("vector", lambda e, bank=bank: e.tensor_tensor(st[:, 664:672], bank[:, 0:8], st[:, 640:648], ALU.subtract))
            p.step("vector", lambda e: e.reduce_max(st[:, 672:673], st[:, 664:672], AX.X))
            etok = p.step("vector", lambda e, tt=tt: e.tensor_scalar(
                st[:, 256 + tt * 8:264 + tt * 8], st[:, 664:672], st[:, 672:673], None, ALU.is_equal))
            self.pool.release(bi, etok)
        self.vflush()

        self.dump("l2coef", self.stats[:, 256:384])
        for e_ in range(8):
            p.barrier("gpsimd", ("T", "Ds"))
            wtok = p.dma("gpsimd", self.scra[:, 0:16384].rearrange("p (db f) -> p db f", f=2048),
                         ew[e_, :, :].rearrange("(db p) f -> p db f", p=128))
            p.wait("tensor", wtok)
            for tt in range(NT):
                if e_ == 0:
                    p.step("vector", lambda e, tt=tt: e.tensor_copy(
                        self.sm[1][:, 0:8], st[:, 256 + tt * 8:264 + tt * 8]))
                    p.wait("tensor", p.tok("V"))
                    self.transpose(self.sm[1][:, 0:8], None, evac="vector",
                                   evac_fn=lambda pt: p.step("vector", lambda e, pt=pt: e.tensor_copy(
                                       self.scrb[0:8, 2048:2176], pt)))
                for hc in range(4):
                    bi, bank = self.pool.acquire()
                    for db in range(ND):
                        p.emit("tensor", lambda e, db=db, tt=tt, hc=hc, bank=bank: e.matmul(
                            bank[:, 0:512],
                            self.xn_fm[:, db * 2048 + tt * 128:db * 2048 + (tt + 1) * 128],
                            self.scra[:, db * 2048 + hc * 512:db * 2048 + (hc + 1) * 512],
                            start=(db == 0), stop=(db == 7)), inc="T" if db == 7 else None)
                    mtok = p.tok("T")
                    p.wait("vector", mtok)
                    sc_ap = st[:, 256 + tt * 8 + e_:257 + tt * 8 + e_]
                    p.step("vector", lambda e, hc=hc, bank=bank, sc_ap=sc_ap: e.tensor_scalar(
                        self.och[:, hc * 512:(hc + 1) * 512], bank[:, 0:512], sc_ap, None, ALU.mult))
                    etok = p.tok("V")
                    self.pool.release(bi, etok)
                if e_ == 0:
                    for hc in range(4):
                        bi, bank = self.pool.acquire()
                        src = self.scrb[0:8, hc * 512:(hc + 1) * 512]
                        p.step("tensor", lambda e, bank=bank, src=src: e.matmul(
                            bank[:, 0:512], self.scrb[0:8, 2048:2176], src, start=True, stop=True))
                        mtok = p.tok("T")
                        p.wait("vector", mtok)
                        etok = p.step("vector", lambda e, hc=hc, bank=bank: e.tensor_tensor(
                            self.och[:, hc * 512:(hc + 1) * 512], self.och[:, hc * 512:(hc + 1) * 512],
                            bank[:, 0:512], ALU.add))
                        self.pool.release(bi, etok)
                fl = self.vflush()
                p.wait("gpsimd", fl)
                if e_ == 0:
                    p.dma("gpsimd", sel_dram[tt * 128:(tt + 1) * 128, :], self.och[:, 0:2048])
                else:
                    p.dma("gpsimd", sel_dram[tt * 128:(tt + 1) * 128, :], self.och[:, 0:2048],
                          accum_op=ALU.add)
        p.barrier("gpsimd", ("T", "A", "V", "Dg"))
        owtok = p.dma("gpsimd", self.scrb[:, 0:16384].rearrange("p (hb o) -> p hb o", o=1024),
                      ow.rearrange("(hb p) o -> p hb o", p=128))
        obtok = p.dma("gpsimd", self.rowb[0:1, 0:1024], ob.rearrange("f -> () f"))
        p.wait("tensor", owtok)
        p.wait("tensor", obtok)
        for tt in range(NT):
            p.barrier("sync", ("T", "Dg"))
            ltok = p.dma("sync", self.och[:, 0:2048], sel_dram[tt * 128:(tt + 1) * 128, :])
            p.wait("tensor", ltok)
            for fb in range(16):
                dst = self.sm[fb // 8][:, (fb % 8) * 128:(fb % 8 + 1) * 128]
                self.transpose(self.och[:, fb * 128:(fb + 1) * 128], dst)
            p.wait("tensor", p.tok("A"))
            for oc in range(2):
                bi, bank = self.pool.acquire()
                for fb in range(16):
                    lhs = self.sm[fb // 8][:, (fb % 8) * 128:(fb % 8 + 1) * 128]
                    p.emit("tensor", lambda e, fb=fb, oc=oc, bank=bank, lhs=lhs: e.matmul(
                        bank[:, 0:512], lhs,
                        self.scrb[:, fb * 1024 + oc * 512: fb * 1024 + (oc + 1) * 512],
                        start=(fb == 0), stop=False))
                p.emit("tensor", lambda e, oc=oc, bank=bank: e.matmul(
                    bank[:, 0:512], self.c_ones_row[:, :],
                    self.rowb[0:1, oc * 512:(oc + 1) * 512], start=False, stop=True), inc="T")
                mtok = p.tok("T")
                p.wait("vector", mtok)
                dst = self.xres[:, tt * 1024 + oc * 512: tt * 1024 + (oc + 1) * 512]
                etok = p.step("vector", lambda e, dst=dst, bank=bank: e.tensor_tensor(
                    dst, dst, bank[:, 0:512], ALU.add))
                self.pool.release(bi, etok)


    def emit_l3(self, xnd, bdi, bdo):
        p = self.p
        st = self.stats
        wb = self.wbuf
        w_write = self.din("L3_attn_W_write", [1024, 1024])
        w_gate = self.din("L3_attn_W_gate", [16, 1024])
        w_beta = self.din("L3_attn_W_beta", [16, 1024])
        w_alpha = self.din("L3_attn_W_alpha", [16, 1024])
        w_out = self.din("L3_attn_W_out", [1024, 1024])
        a_log = self.din("L3_attn_A_log", [16])
        dt_bias = self.din("L3_attn_dt_bias", [16])
        psel = self.din("c_prevsel", [128, 4])
        rmask = self.din("c_rankmask", [128, 4])
        agi = self.dint("ag_l3s_in", [64, 1040])
        ago = self.dint("ag_l3s_out", [256, 1040])
        o_dram = self.dint("l3o", [2048, 1024], BF16)
        decd = self.dint("l3decd", [256, 128])
        bdc = self.dint("l3bdc", [1, 1024], BF16)
        k_tm = self.scra
        v_tm = self.scrb

        bf_tok = p.dma("sync", bdi[0:1, :], xnd[2047:2048, :])
        ag_tok = self.ag_go(bdi, bdo, bf_tok)
        ps_tok = p.dma("sync", st[:, 720:724], psel[:, :])

        p.barrier("sync", ("T", "V", "A"))
        for tt in range(NT):
            p.dma("sync", k_tm[1:128, tt * 1024:(tt + 1) * 1024],
                  xnd[tt * 128:tt * 128 + 127, :])
            if tt > 0:
                p.dma("sync", k_tm[0:1, tt * 1024:(tt + 1) * 1024],
                      xnd[tt * 128 - 1:tt * 128, :])
        p.barrier("gpsimd", ("T", "V", "A"))
        p.wait("gpsimd", ag_tok)
        p.wait("vector", ps_tok)
        for r in range(4):
            bd_tok = p.dma("gpsimd", self.och[0:1, 0:1024], bdo[r:r + 1, :])
            p.wait("vector", bd_tok)
            if r == 0:
                p.step("vector", lambda e, r=r: e.tensor_scalar(
                    self.sm[0][0:1, 0:1024], self.och[0:1, 0:1024],
                    st[0:1, 720 + r:721 + r], None, ALU.mult))
            else:
                p.step("vector", lambda e, r=r: e.scalar_tensor_tensor(
                    self.sm[0][0:1, 0:1024], self.och[0:1, 0:1024],
                    st[0:1, 720 + r:721 + r], self.sm[0][0:1, 0:1024], ALU.mult, ALU.add))
            fl = self.vflush()
            p.wait("gpsimd", fl)
        bfl = self.vflush()
        p.wait("sync", bfl)
        bc_tok = p.dma("sync", bdc[0:1, :], self.sm[0][0:1, 0:1024])
        p.wait("sync", bc_tok)
        p.dma("sync", k_tm[0:1, 0:1024], bdc[0:1, :])
        s8_tok = p.dma("sync", self.sm[1][0:8, 0:128],
                       bdc.rearrange("o (db pp) -> (o db) pp", pp=128))
        p.wait("tensor", s8_tok)
        self.transpose(self.sm[1][0:8, 0:128], self.och[:, 1024:1032])

        p.barrier("gpsimd", ("T", "A", "V"))
        for wi, wmat in enumerate((w_gate, w_beta, w_alpha)):
            stok = p.dma("gpsimd", self.sm[0][0:16, 0:1024], wmat[:, :])
            p.wait("tensor", stok)
            for db in range(ND):
                self.transpose(self.sm[0][0:16, db * 128:(db + 1) * 128],
                               wb[:, db * 48 + wi * 16: db * 48 + (wi + 1) * 16])
            p.barrier("gpsimd", ("T",))
        p.step("vector", lambda e: e.memset(self.rowb[0:1, 0:48], 0.0))
        fl = self.vflush()
        p.wait("gpsimd", fl)
        p.dma("gpsimd", self.rowb[0:1, 32:48], dt_bias.rearrange("f -> () f"))
        al_tok = p.dma("sync", st[0:1, 144:160], a_log.rearrange("f -> () f"))
        p.wait("scalar", al_tok)
        p.step("scalar", lambda e: e.activation(st[0:1, 144:160], st[0:1, 144:160], AF.Exp))
        p.wait("vector", p.tok("A"))
        p.step("vector", lambda e: e.tensor_scalar(st[0:1, 144:160], st[0:1, 144:160], -1.0, None, ALU.mult))
        fl = self.vflush()
        p.wait("tensor", fl)
        bi, bank = self.pool.acquire()
        p.step("tensor", lambda e, bank=bank: e.matmul(
            bank[:, 0:16], self.c_ones_rowf[:, :], st[0:1, 144:160], start=True, stop=True))
        p.wait("vector", p.tok("T"))
        et = p.step("vector", lambda e, bank=bank: e.tensor_copy(st[:, 160:176], bank[:, 0:16]))
        self.pool.release(bi, et)

        p.wait("tensor", p.tok("Dg"))
        p.wait("tensor", p.tok("A"))
        for tt in range(NT):
            bi, bank = self.pool.acquire()
            for db in range(ND):
                p.emit("tensor", lambda e, db=db, tt=tt, bank=bank: e.matmul(
                    bank[:, 0:48], self.xn_fm[:, db * 2048 + tt * 128:db * 2048 + (tt + 1) * 128],
                    wb[:, db * 48:(db + 1) * 48], start=(db == 0), stop=False))
            p.emit("tensor", lambda e, bank=bank: e.matmul(
                bank[:, 0:48], self.c_ones_row[:, :], self.rowb[0:1, 0:48],
                start=False, stop=True), inc="T")
            mtok = p.tok("T")
            p.wait("scalar", mtok)
            p.step("scalar", lambda e, tt=tt, bank=bank: e.activation(
                self.sst_f[:, 768 + tt * 16:784 + tt * 16], bank[:, 0:16], AF.Sigmoid))
            p.step("scalar", lambda e, tt=tt, bank=bank: e.activation(
                self.och[:, tt * 16:(tt + 1) * 16], bank[:, 16:32], AF.Sigmoid))
            p.wait("vector", p.tok("A"))
            p.step("vector", lambda e, bank=bank: e.tensor_scalar(
                st[:, 128:144], bank[:, 32:48], -1.0, None, ALU.mult))
            p.step("vector", lambda e, bank=bank: e.tensor_tensor(
                st[:, 176:192], st[:, 128:144], bank[:, 32:48], ALU.min))
            p.wait("scalar", p.tok("V"))
            p.step("scalar", lambda e: e.activation(st[:, 176:192], st[:, 176:192], AF.Exp))
            p.step("scalar", lambda e: e.activation(st[:, 176:192], st[:, 176:192],
                                                    AF.Ln, bias=self.fconst(1.0)))
            p.wait("vector", p.tok("A"))
            p.step("vector", lambda e, bank=bank: e.tensor_scalar(
                st[:, 128:144], bank[:, 32:48], 0.0, None, ALU.max))
            p.step("vector", lambda e: e.tensor_tensor(st[:, 128:144], st[:, 128:144],
                                                       st[:, 176:192], ALU.add))
            p.step("vector", lambda e: e.tensor_tensor(st[:, 176:192], st[:, 128:144],
                                                       st[:, 160:176], ALU.mult))
            fl2 = self.vflush()
            self.pool.release(bi, fl2)
            p.wait("tensor", fl2)
            bi2, bank2 = self.pool.acquire()
            p.step("tensor", lambda e, bank2=bank2: e.matmul(
                bank2[:, 0:16], self.c_trilf[:, :], st[:, 176:192], start=True, stop=True))
            p.wait("vector", p.tok("T"))
            et2 = p.step("vector", lambda e, tt=tt, bank2=bank2: e.tensor_copy(
                st[:, 256 + tt * 16:272 + tt * 16], bank2[:, 0:16]))
            self.pool.release(bi2, et2)
            p.wait("tensor", et2)
            tfl = self.transpose_f32(st[:, 256 + tt * 16:272 + tt * 16],
                                     self.sst_f[0:16, 1024:1152])
            p.wait("sync", tfl)
            p.dma("sync", decd[tt * 16:(tt + 1) * 16, :], self.sst_f[0:16, 1024:1152])

        def build_wT_l3(wsrc, f0, nfb, dst, stage):
            for fb in range(nfb):
                p.barrier("gpsimd", ("T", "A", "V"))
                stok = p.dma("gpsimd", stage, wsrc[f0 + fb * 128:f0 + (fb + 1) * 128, :])
                p.wait("tensor", stok)
                for db in range(ND):
                    self.transpose(stage[:, db * 128:(db + 1) * 128],
                                   dst[:, db * (nfb * 128) + fb * 128: db * (nfb * 128) + (fb + 1) * 128])
        for fc in range(2):
            build_wT_l3(w_write, fc * 512, 4, wb[:, 0:4096], self.sm[0][:, 0:1024])
            p.wait("tensor", p.tok("A"))
            for tt in range(NT):
                bi, bank = self.pool.acquire()
                for db in range(ND):
                    p.emit("tensor", lambda e, db=db, tt=tt, bank=bank: e.matmul(
                        bank[:, 0:512],
                        self.xn_fm[:, db * 2048 + tt * 128:db * 2048 + (tt + 1) * 128],
                        wb[:, db * 512:(db + 1) * 512], start=(db == 0), stop=(db == 7)),
                        inc="T" if db == 7 else None)
                mtok = p.tok("T")
                p.wait("vector", mtok)
                etok = p.step("vector", lambda e, tt=tt, fc=fc, bank=bank: e.tensor_tensor(
                    v_tm[:, tt * 1024 + fc * 512: tt * 1024 + (fc + 1) * 512].rearrange(
                        "pp (h d) -> pp h d", d=64),
                    bank[:, 0:512].rearrange("pp (h d) -> pp h d", d=64),
                    bcast_free(self.och[:, tt * 16 + fc * 8:tt * 16 + (fc + 1) * 8], 64), ALU.mult))
                self.pool.release(bi, etok)
        self.vflush()

        def dec_batch(tt):
            p.barrier("sync", ("T",))
            rtok = p.dma("sync", self.rowf[0:1, 0:2048],
                         decd[tt * 16:(tt + 1) * 16, :].rearrange("(o r) c -> o (r c)", o=1))
            bi, bank = self.pool.acquire()
            p.wait("tensor", rtok)
            dlast = self.rowf[0:1, 0:2048].rearrange("o (h c) -> o h c", c=128)[:, :, 127:128]
            p.step("tensor", lambda e, bank=bank, dlast=dlast: e.matmul(
                bank[:, 0:16].rearrange("pp (h o) -> pp h o", o=1), self.c_ones_rowf[:, :],
                dlast, start=True, stop=True))
            p.wait("vector", p.tok("T"))
            p.step("vector", lambda e, bank=bank: e.tensor_copy(st[:, 192:208], bank[:, 0:16]))
            fl = self.vflush()
            self.pool.release(bi, fl)
            p.wait("scalar", fl)
            p.step("scalar", lambda e: e.activation(st[:, 208:224], st[:, 192:208], AF.Exp))
            p.wait("vector", p.tok("A"))
            p.step("vector", lambda e, tt=tt: e.tensor_tensor(
                st[:, 224:240], st[:, 192:208], st[:, 256 + tt * 16:272 + tt * 16], ALU.subtract))
            fl = self.vflush()
            p.wait("scalar", fl)
            p.step("scalar", lambda e: e.activation(st[:, 224:240], st[:, 224:240], AF.Exp))
            p.step("scalar", lambda e, tt=tt: e.activation(
                self.sst_f[:, 640:656], st[:, 256 + tt * 16:272 + tt * 16], AF.Exp))
            p.wait("vector", p.tok("A"))
            return rtok

        if hasattr(self, "mark"):
            self.mark("l3_pass1")
        p.step("vector", lambda e: e.memset(self.sst_f[:, 0:512], 0.0))
        p.step("vector", lambda e: e.memset(st[0:1, 240:256], 0.0))
        self.vflush()
        for tt in range(NT):
            dec_batch(tt)
            p.step("vector", lambda e: e.tensor_tensor(
                st[0:1, 240:256], st[0:1, 240:256], st[0:1, 192:208], ALU.add))
            for h in range(16):
                b0 = (h % 2) * 64
                hb = h // 2
                p.step("vector", lambda e, tt=tt, h=h: e.tensor_scalar(
                    self.sm[0][:, 0:64], k_tm[:, tt * 1024 + h * 64: tt * 1024 + (h + 1) * 64],
                    st[:, 224 + h:225 + h], None, ALU.mult))
                fl = self.vflush()
                p.wait("tensor", fl)
                bi, bank = self.pool.acquire()
                p.step("tensor", lambda e, tt=tt, h=h, b0=b0, bank=bank: e.matmul(
                    bank[b0:b0 + 64, 0:64], self.sm[0][:, 0:64],
                    v_tm[:, tt * 1024 + h * 64: tt * 1024 + (h + 1) * 64], start=True, stop=True))
                mtok = p.tok("T")
                p.wait("vector", mtok)
                etok = p.step("vector", lambda e, b0=b0, hb=hb, h=h, bank=bank: e.scalar_tensor_tensor(
                    self.sst_f[b0:b0 + 64, hb * 64:(hb + 1) * 64],
                    self.sst_f[b0:b0 + 64, hb * 64:(hb + 1) * 64],
                    st[b0:b0 + 64, 208 + h:209 + h],
                    bank[b0:b0 + 64, 0:64], ALU.mult, ALU.add))
                self.pool.release(bi, etok)
        fl = self.vflush()
        p.wait("gpsimd", fl)
        f1 = p.dma("gpsimd", agi[:, 0:512], self.sst_f[0:64, 0:512])
        f2 = p.dma("gpsimd", agi[:, 512:1024], self.sst_f[64:128, 0:512])
        f3 = p.dma("gpsimd", agi[0:1, 1024:1040], st[0:1, 240:256])
        ag2 = self.ag_go(agi, ago, f3)

        rm_tok = p.dma("sync", st[:, 160:164], rmask[:, :])
        p.wait("vector", ag2)
        p.wait("vector", rm_tok)
        p.step("vector", lambda e: e.memset(self.sst_f[:, 0:512], 0.0))
        for r in range(4):
            p.wait_all("gpsimd", "V")
            g3 = p.dma("gpsimd", st[0:1, 144:160], ago[r * 64:r * 64 + 1, 1024:1040])
            p.wait("scalar", g3)
            p.step("scalar", lambda e: e.activation(st[0:1, 176:192], st[0:1, 144:160], AF.Exp))
            p.wait("vector", p.tok("A"))
            p.step("vector", lambda e: e.tensor_scalar(
                st[0:1, 176:192], st[0:1, 176:192], -1.0, None, ALU.add))
            p.step("vector", lambda e, r=r: e.tensor_scalar(
                st[0:1, 176:192], st[0:1, 176:192], st[0:1, 160 + r:161 + r], 1.0,
                ALU.mult, ALU.add))
            fl = self.vflush()
            p.wait("tensor", fl)
            bi, bank = self.pool.acquire()
            p.step("tensor", lambda e, bank=bank: e.matmul(
                bank[:, 0:16], self.c_ones_rowf[:, :], st[0:1, 176:192], start=True, stop=True))
            p.wait("vector", p.tok("T"))
            p.step("vector", lambda e, bank=bank: e.tensor_copy(st[:, 176:192], bank[:, 0:16]))
            vv = self.vflush()
            self.pool.release(bi, vv)
            for half in range(2):
                p.wait_all("gpsimd", "V")
                g1 = p.dma("gpsimd", self.sst_f[0:64, 512:768],
                           ago[r * 64:(r + 1) * 64, half * 256:(half + 1) * 256])
                g2 = p.dma("gpsimd", self.sst_f[64:128, 512:768],
                           ago[r * 64:(r + 1) * 64, 512 + half * 256:512 + (half + 1) * 256])
                p.wait("vector", g1)
                p.wait("vector", g2)
                p.step("vector", lambda e, r=r: e.tensor_scalar(
                    self.sst_f[:, 512:768], self.sst_f[:, 512:768], st[:, 160 + r:161 + r],
                    None, ALU.mult))
                for hbh in range(4):
                    for par in range(2):
                        hb = half * 4 + hbh
                        h = hb * 2 + par
                        b0 = par * 64
                        p.step("vector", lambda e, b0=b0, hb=hb, hbh=hbh, h=h:
                               e.scalar_tensor_tensor(
                                   self.sst_f[b0:b0 + 64, hb * 64:(hb + 1) * 64],
                                   self.sst_f[b0:b0 + 64, hb * 64:(hb + 1) * 64],
                                   st[b0:b0 + 64, 176 + h:177 + h],
                                   self.sst_f[b0:b0 + 64, 512 + hbh * 64:512 + (hbh + 1) * 64],
                                   ALU.mult, ALU.add))
        p.step("vector", lambda e: e.tensor_copy(self.sst_b[:, 0:512], self.sst_f[:, 0:512]))
        self.vflush()

        if hasattr(self, "mark"):
            self.mark("l3_pass2")
        for tt in range(NT):
            dec_batch(tt)
            for h in range(16):
                b0 = (h % 2) * 64
                hb = h // 2
                db = h // 2
                bi_b, bank_b = self.pool.acquire()
                p.step("tensor", lambda e, h=h, bank_b=bank_b: e.matmul(
                    bank_b[:, 0:128], self.c_ones_rowf[:, :], self.rowf[0:1, h * 128:(h + 1) * 128],
                    start=True, stop=True))
                p.wait("vector", p.tok("T"))
                p.step("vector", lambda e, tt=tt, h=h, bank_b=bank_b: e.tensor_scalar(
                    self.sst_f[:, 512:640], bank_b[:, 0:128],
                    st[:, 256 + tt * 16 + h:257 + tt * 16 + h], 0.0, ALU.subtract, ALU.min))
                fl = self.vflush()
                self.pool.release(bi_b, fl)
                p.wait("scalar", fl)
                p.step("scalar", lambda e: e.activation(self.sm[1][:, 0:128],
                                                        self.sst_f[:, 512:640], AF.Exp))
                p.wait("vector", p.tok("A"))
                p.step("vector", lambda e: e.tensor_tensor(
                    self.sm[1][:, 128:256], self.sm[1][:, 0:128], self.c_l3strict[:, :], ALU.mult))
                if tt == 0:
                    p.step("vector", lambda e, db=db, b0=b0: e.tensor_copy(
                        self.sm[0][b0:b0 + 64, 385:512],
                        self.xn_fm[b0:b0 + 64, db * 2048: db * 2048 + 127]))
                    p.step("vector", lambda e, db=db, b0=b0: e.tensor_copy(
                        self.sm[0][b0:b0 + 64, 384:385], self.och[b0:b0 + 64, 1024 + db:1025 + db]))
                    wk_ap = self.sm[0][b0:b0 + 64, 384:512]
                else:
                    wk_ap = self.xn_fm[b0:b0 + 64,
                                       db * 2048 + tt * 128 - 1: db * 2048 + (tt + 1) * 128 - 1]
                fl = self.vflush()
                p.wait("tensor", fl)
                bi_i, bank_i = self.pool.acquire()
                p.step("tensor", lambda e, wk_ap=wk_ap, db=db, b0=b0, tt=tt, bank_i=bank_i: e.matmul(
                    bank_i[:, 0:128], wk_ap,
                    self.xn_fm[b0:b0 + 64, db * 2048 + tt * 128: db * 2048 + (tt + 1) * 128],
                    start=True, stop=True))
                p.wait("vector", p.tok("T"))
                p.step("vector", lambda e, bank_i=bank_i: e.tensor_tensor(
                    self.sm[0][:, 0:128], bank_i[:, 0:128], self.sm[1][:, 128:256], ALU.mult))
                ifl = self.vflush()
                self.pool.release(bi_i, ifl)
                p.wait("tensor", ifl)
                bi_n, bank_n = self.pool.acquire()
                p.step("tensor", lambda e, tt=tt, h=h, bank_n=bank_n: e.matmul(
                    bank_n[:, 0:64], self.sm[0][:, 0:128],
                    v_tm[:, tt * 1024 + h * 64: tt * 1024 + (h + 1) * 64], start=True, stop=True))
                bi_n2, bank_n2 = self.pool.acquire()
                p.step("tensor", lambda e, db=db, b0=b0, hb=hb, tt=tt, bank_n2=bank_n2: e.matmul(
                    bank_n2[:, 0:64],
                    self.xn_fm[b0:b0 + 64, db * 2048 + tt * 128: db * 2048 + (tt + 1) * 128],
                    self.sst_b[b0:b0 + 64, hb * 64:(hb + 1) * 64], start=True, stop=True))
                ntok = p.tok("T")
                p.wait("vector", ntok)
                p.step("vector", lambda e, h=h, bank_n2=bank_n2: e.tensor_scalar(
                    self.sm[0][:, 256:320], bank_n2[:, 0:64], self.sst_f[:, 640 + h:641 + h],
                    None, ALU.mult))
                p.step("vector", lambda e, bank_n=bank_n: e.tensor_tensor(
                    self.sm[0][:, 256:320], self.sm[0][:, 256:320], bank_n[:, 0:64], ALU.add))
                otok = p.step("vector", lambda e, tt=tt, h=h: e.tensor_scalar(
                    self.och[:, h * 64:(h + 1) * 64], self.sm[0][:, 256:320],
                    self.sst_f[:, 768 + tt * 16 + h:769 + tt * 16 + h], None, ALU.mult))
                self.pool.release(bi_n, otok)
                self.pool.release(bi_n2, otok)
                p.step("vector", lambda e, tt=tt, h=h: e.tensor_scalar(
                    self.sm[0][:, 320:384], k_tm[:, tt * 1024 + h * 64: tt * 1024 + (h + 1) * 64],
                    st[:, 224 + h:225 + h], None, ALU.mult))
                ufl = self.vflush()
                p.wait("tensor", ufl)
                bi_u, bank_u = self.pool.acquire()
                p.step("tensor", lambda e, tt=tt, h=h, b0=b0, bank_u=bank_u: e.matmul(
                    bank_u[b0:b0 + 64, 0:64], self.sm[0][:, 320:384],
                    v_tm[:, tt * 1024 + h * 64: tt * 1024 + (h + 1) * 64], start=True, stop=True))
                p.wait("vector", p.tok("T"))
                p.step("vector", lambda e, b0=b0, hb=hb, h=h, bank_u=bank_u: e.scalar_tensor_tensor(
                    self.sst_f[b0:b0 + 64, hb * 64:(hb + 1) * 64],
                    self.sst_f[b0:b0 + 64, hb * 64:(hb + 1) * 64],
                    st[b0:b0 + 64, 208 + h:209 + h],
                    bank_u[b0:b0 + 64, 0:64], ALU.mult, ALU.add))
                cfl = p.step("vector", lambda e, b0=b0, hb=hb: e.tensor_copy(
                    self.sst_b[b0:b0 + 64, hb * 64:(hb + 1) * 64],
                    self.sst_f[b0:b0 + 64, hb * 64:(hb + 1) * 64]))
                self.pool.release(bi_u, cfl)
            fl = self.vflush()
            p.wait("sync", fl)
            p.dma("sync", o_dram[tt * 128:(tt + 1) * 128, :], self.och[:, 0:1024])
        self.dump("l3_o", self.och[:, 0:1024])

        if hasattr(self, "mark"):
            self.mark("l3_out")
        for ob in range(8):
            build_wT_l3(w_out, ob * 128, 1, k_tm[:, ob * 1024: ob * 1024 + 1024],
                        self.sm[0][:, 0:1024])
        woutT = k_tm.rearrange("pp (ob eb j) -> pp ob eb j", eb=8, j=128)
        for tt in range(NT):
            p.barrier("sync", ("T", "V"))
            ltok = p.dma("sync", self.och[:, 0:1024], o_dram[tt * 128:(tt + 1) * 128, :])
            xtok = p.dma("sync", self.sm[0][:, 0:1024], xnd[tt * 128:(tt + 1) * 128, :])
            p.wait("vector", xtok)
            dstx = self.xres[:, tt * 1024:(tt + 1) * 1024]
            p.step("vector", lambda e, dstx=dstx: e.tensor_tensor(
                dstx, dstx, self.sm[0][:, 0:1024], ALU.add))
            p.wait("tensor", ltok)
            for eb in range(8):
                self.transpose(self.och[:, eb * 128:(eb + 1) * 128],
                               self.och[:, 1024 + eb * 128: 1024 + (eb + 1) * 128])
            p.wait("tensor", p.tok("A"))
            for oc in range(2):
                bi, bank = self.pool.acquire()
                for eb in range(8):
                    p.emit("tensor", lambda e, eb=eb, oc=oc, bank=bank: e.matmul(
                        bank[:, 0:512], self.och[:, 1024 + eb * 128: 1024 + (eb + 1) * 128],
                        woutT[:, oc * 4:(oc + 1) * 4, eb, :],
                        start=(eb == 0), stop=(eb == 7)), inc="T" if eb == 7 else None)
                mtok = p.tok("T")
                p.wait("vector", mtok)
                dst = self.xres[:, tt * 1024 + oc * 512: tt * 1024 + (oc + 1) * 512]
                etok = p.step("vector", lambda e, dst=dst, bank=bank: e.tensor_tensor(
                    dst, dst, bank[:, 0:512], ALU.add))
                self.pool.release(bi, etok)
        self.dump("l3_xres", self.xres[:, :])

    def emit_l3_ffn_gate(self):
        p = self.p
        st = self.stats
        w1 = self.din("L3_ffn_W1", [1024, 512])
        b1 = self.din("L3_ffn_b1", [512])
        w2 = self.din("L3_ffn_W2", [512, 8])
        b2 = self.din("L3_ffn_b2", [8])
        p.barrier("sync", ("T", "V", "A"))
        b1tok = p.dma("sync", self.rowf[0:1, 768:1280], b1.rearrange("f -> () f"))
        b2tok = p.dma("sync", self.rowf[0:1, 1280:1288], b2.rearrange("f -> () f"))
        w2tok = p.dma("sync", st[:, 640:672].rearrange("pp (fb o) -> pp fb o", o=8),
                      w2.rearrange("(fb pp) o -> pp fb o", pp=128))
        p.wait("vector", p.tok("Ds"))
        p.step("vector", lambda e: e.tensor_scalar(st[:, 672:680], self.c_iota8[:, :],
                                                   1e-6, None, ALU.mult))
        bi, bank = self.pool.acquire()
        for db in range(ND):
            wtok = p.dma("sync", self.sst_f[:, 128:640], w1[db * 128:(db + 1) * 128, :])
            p.wait("tensor", wtok)
            mt = p.emit("tensor", lambda e, db=db, bank=bank: e.matmul(
                bank[0:1, 0:512], self.c_ones_colf[:, :], self.sst_f[:, 128:640],
                start=(db == 0), stop=(db == 7)), inc="T")
            p.wait("sync", mt)
        p.wait("vector", p.tok("T"))
        cse = p.step("vector", lambda e, bank=bank: e.tensor_copy(
            self.rowf[0:1, 256:768], bank[0:1, 0:512]))
        self.pool.release(bi, cse)
        self.vflush()
        for tt in range(NT):
            bi, bank = self.pool.acquire()
            for db in range(ND):
                wtok = p.dma("sync", self.sst_f[:, 128:640], w1[db * 128:(db + 1) * 128, :])
                fl = self.transpose_f32(
                    self.xres[:, tt * 1024 + db * 128: tt * 1024 + (db + 1) * 128],
                    self.sst_f[:, 0:128])
                p.wait("tensor", fl)
                p.wait("tensor", wtok)
                mt = p.emit("tensor", lambda e, db=db, bank=bank: e.matmul(
                    bank[:, 0:512], self.sst_f[:, 0:128], self.sst_f[:, 128:640],
                    start=(db == 0), stop=False), inc="T")
                p.wait("sync", mt)
            p.step("vector", lambda e, tt=tt: e.tensor_scalar(
                st[:, 728:729], st[:, 32 + tt:33 + tt], -1.0, None, ALU.mult))
            p.wait("tensor", p.tok("V"))
            t1 = self.transpose_f32(st[:, 728:729], self.rowf[0:1, 0:128])
            t2 = self.transpose_f32(st[:, 96 + tt:97 + tt], self.rowf[0:1, 128:256])
            p.wait("tensor", t1)
            p.wait("tensor", t2)
            p.wait("tensor", b1tok)
            p.emit("tensor", lambda e, bank=bank: e.matmul(
                bank[:, 0:512], self.rowf[0:1, 0:128], self.rowf[0:1, 256:768],
                start=False, stop=False))
            p.emit("tensor", lambda e, bank=bank: e.matmul(
                bank[:, 0:512], self.rowf[0:1, 128:256], self.rowf[0:1, 768:1280],
                start=False, stop=True), inc="T")
            mtok = p.tok("T")
            p.wait("vector", mtok)
            p.step("vector", lambda e, tt=tt, bank=bank: e.tensor_scalar(
                self.sst_f[:, 640:1152], bank[:, 0:512], st[:, 112 + tt:113 + tt],
                0.0, ALU.mult, ALU.max))
            h1fl = self.vflush()
            self.pool.release(bi, h1fl)
            p.wait("tensor", h1fl)
            p.wait("tensor", w2tok)
            bi, bank = self.pool.acquire()
            for fb in range(4):
                fl = self.transpose_f32(self.sst_f[:, 640 + fb * 128:640 + (fb + 1) * 128],
                                        self.sst_f[:, 0:128])
                p.wait("tensor", fl)
                p.emit("tensor", lambda e, fb=fb, bank=bank: e.matmul(
                    bank[:, 0:8], self.sst_f[:, 0:128], st[:, 640 + fb * 8:648 + fb * 8],
                    start=(fb == 0), stop=False))
            p.wait("tensor", b2tok)
            p.emit("tensor", lambda e, bank=bank: e.matmul(
                bank[:, 0:8], self.c_ones_rowf[:, :], self.rowf[0:1, 1280:1288],
                start=False, stop=True), inc="T")
            mtok = p.tok("T")
            p.wait("vector", mtok)
            p.step("vector", lambda e, bank=bank: e.tensor_tensor(
                st[:, 680:688], bank[:, 0:8], st[:, 672:680], ALU.subtract))
            p.step("vector", lambda e: e.reduce_max(st[:, 688:689], st[:, 680:688], AX.X))
            cfe = p.step("vector", lambda e, tt=tt: e.tensor_scalar(
                st[:, 512 + tt * 8:520 + tt * 8], st[:, 680:688], st[:, 688:689],
                None, ALU.is_equal))
            self.pool.release(bi, cfe)
        self.vflush()
        self.dump("l3_coef", st[:, 512:640])

    def emit_l3_ffn_exp(self):
        p = self.p
        if hasattr(self, "mark"):
            self.mark("l3_ffn_exp")
        st = self.stats
        ew = self.din("L3_ffn_expert_W", [8, 1024, 1024])
        eb = self.din("L3_ffn_expert_b", [8, 1024])
        p.barrier("gpsimd", ("T", "V", "A"))
        ebtok = p.dma("gpsimd", self.sst_b[0:8, 0:1024], eb[:, :])
        for e_ in range(8):
            p.barrier("gpsimd", ("T",))
            wtok = p.dma("gpsimd", self.scrb[:, 0:8192].rearrange("pp (db f) -> pp db f", f=1024),
                         ew[e_, :, :].rearrange("(db pp) f -> pp db f", pp=128))
            p.wait("tensor", wtok)
            if e_ == 0:
                p.wait("tensor", ebtok)
            for tt in range(NT):
                if e_ == 0:
                    p.step("vector", lambda e, tt=tt: e.tensor_copy(
                        self.sm[1][:, 0:8], st[:, 512 + tt * 8:520 + tt * 8]))
                    fl = self.vflush()
                    p.wait("tensor", fl)
                    self.transpose(self.sm[1][:, 0:8], None, evac="vector",
                                   evac_fn=lambda pt_ap: p.step("vector",
                                       lambda e, pt_ap=pt_ap: e.tensor_copy(
                                           self.sm[1][0:8, 128:256], pt_ap)))
                for oc in range(2):
                    bi, bank = self.pool.acquire()
                    for db in range(ND):
                        p.emit("tensor", lambda e, db=db, tt=tt, oc=oc, bank=bank: e.matmul(
                            bank[:, 0:512],
                            self.xn_fm[:, db * 2048 + tt * 128:db * 2048 + (tt + 1) * 128],
                            self.scrb[:, db * 1024 + oc * 512:db * 1024 + (oc + 1) * 512],
                            start=(db == 0), stop=(db == 7)), inc="T" if db == 7 else None)
                    mtok = p.tok("T")
                    p.wait("vector", mtok)
                    sc_ap = st[:, 512 + tt * 8 + e_:513 + tt * 8 + e_]
                    dst = self.xres[:, tt * 1024 + oc * 512: tt * 1024 + (oc + 1) * 512]
                    etok = p.step("vector", lambda e, dst=dst, bank=bank, sc_ap=sc_ap:
                                  e.scalar_tensor_tensor(dst, bank[:, 0:512], sc_ap, dst,
                                                         ALU.mult, ALU.add))
                    self.pool.release(bi, etok)
                if e_ == 0:
                    for oc in range(2):
                        bi, bank = self.pool.acquire()
                        p.step("tensor", lambda e, oc=oc, bank=bank: e.matmul(
                            bank[:, 0:512], self.sm[1][0:8, 128:256],
                            self.sst_b[0:8, oc * 512:(oc + 1) * 512], start=True, stop=True))
                        mtok = p.tok("T")
                        p.wait("vector", mtok)
                        dst = self.xres[:, tt * 1024 + oc * 512: tt * 1024 + (oc + 1) * 512]
                        etok = p.step("vector", lambda e, dst=dst, bank=bank: e.tensor_tensor(
                            dst, dst, bank[:, 0:512], ALU.add))
                        self.pool.release(bi, etok)
        self.dump("l3f_xres", self.xres[:, :])



def build_layers(layers, dbg=False):
    nc = bass.Bass("TRN2", target_bir_lowering=False)
    p = Prog()
    mk = MK(nc, p, dbg=dbg)
    stk = ExitStack()
    with stk:
        mk.alloc(stk)
        x_in = mk.din("x_in", [T, 1024])
        y_out = mk.dout("y_out", [T, 1024])
        mk.load_consts()
        ld = mk.emit_load_x(x_in)
        for eng in ("tensor", "vector", "scalar"):
            p.wait_all(eng, "Ds")
            p.wait_all(eng, "Dg")
        marks = []
        mk.marks = marks

        def mark(name):
            marks.append((name, {e: len(p.q[e]) for e in ENGS}))
        mk.mark = mark
        for li in layers:
            mark(f"L{li}_start")
            if li == 0:
                ag16i, ag16o = mk.allgather("ag_x16", 16, 1024, BF16)
                agt = mk.emit_ln(xres_tok=ld, ag16=ag16i)
                ag16_tok = mk.ag_go(ag16i, ag16o, agt)
                ctx = mk.emit_l0_attn_pre()
                mk.emit_l0_attn_post(ctx)
                mk.emit_l0_ffn(ag16o, ag16_tok)
            elif li == 1:
                mk.emit_ln(xres_tok=ld)
                mk.emit_l1()
                mk.emit_l1_ffn()
            elif li == 2:
                mk.emit_ln(xres_tok=ld)
                mk.emit_l2()
                mk.emit_l2_ffn()
            elif li == 3:
                xnd = mk.dint("xn3_tm", [T, 1024], BF16)
                bdi, bdo = mk.allgather("ag_l3b", 1, 1024, BF16)
                mk.emit_ln(xres_tok=ld, write_dram=xnd)
                mk.emit_l3_ffn_gate()
                mk.emit_l3(xnd, bdi, bdo)
                mk.emit_l3_ffn_exp()
        mark("store")
        mk.emit_store_x(y_out)
        p.realize(nc, stk)
    return nc, mk


def shard_inputs(full, core):
    return {}


_BUILD_CACHE = {}


def _get_prog(key, layers):
    if key not in _BUILD_CACHE:
        _BUILD_CACHE[key] = build_layers(layers)
    return _BUILD_CACHE[key]


def kernel(**inputs):
    consts = host_consts()
    x = np.ascontiguousarray(inputs["L0_attn_x"].reshape(16384, 1024))
    stages = [[0, 1, 2, 3]]
    cur = x
    for si, layers in enumerate(stages):
        nc, mk = _get_prog(tuple(layers), layers)
        in_maps = []
        fv = np.zeros((128, 64), np.float32)
        fv[:, :len(mk.fvals)] = np.array(mk.fvals, np.float32)[None, :]
        for c in range(NCORE):
            m = {"x_in": np.ascontiguousarray(cur[c * T:(c + 1) * T])}
            rm = np.zeros((128, 4), np.float32)
            rm[:, :c % 4] = 1.0
            rm8 = np.zeros((128, 8), np.float32)
            rm8[:, :c] = 1.0
            for name in mk.input_names:
                if name == "c_fvals":
                    m[name] = fv
                elif name == "c_rankmask":
                    m[name] = rm
                elif name == "c_rankmask8":
                    m[name] = rm8
                elif name == "c_prevsel":
                    pm = np.zeros((128, 4), np.float32)
                    if c % 4 > 0:
                        pm[:, c % 4 - 1] = 1.0
                    m[name] = pm
                elif name == "c_groupsel":
                    gs = np.zeros((128, 4), np.float32)
                    gs[:, c % 4] = 1.0
                    m[name] = gs
                elif name in consts:
                    m[name] = consts[name]
                elif name in inputs:
                    a = inputs[name]
                    if a.ndim == 0:
                        a = a.reshape(1)
                    m[name] = np.ascontiguousarray(a)
                elif name == "x_in":
                    pass
                else:
                    raise KeyError(name)
            in_maps.append(m)
        res = run_bass_kernel_spmd(nc, in_maps, core_ids=list(range(NCORE)))
        cur = np.concatenate([res.results[c]["y_out"] for c in range(NCORE)], axis=0)
    return cur.reshape(2, 8192, 1024)
